# version 04
# baseline (speedup 1.0000x reference)
import numpy as np
import ml_dtypes
import concourse.bass as bass
import concourse.mybir as mybir
from concourse.bass_utils import run_bass_kernel_spmd

F32 = mybir.dt.float32
BF16 = mybir.dt.bfloat16
AF = mybir.ActivationFunctionType
ALU = mybir.AluOpType
AX = mybir.AxisListType
BF = ml_dtypes.bfloat16

NCORES = 8
T = 4096
D = 1024
H, DK, DV = 4, 256, 512
C = 128
NSEQ = 2
TOK = NSEQ * T
NBLK = TOK // C
BLK_PER_SEQ = T // C
A_TILE = 4
D_FF = 4096
NEXP = 9
TT = 1024
NBT = TOK // TT
BPT = TT // C
EPS = 1e-5
NSLOT = 8

_cache = {}
_LIMIT = {"a1": 64, "a2": 64, "bt": 8, "ne": 9, "a2p": 15}


def _consts():
    if "c" in _cache:
        return _cache["c"]
    ld = np.log1p(-np.exp2(-5.0 - np.arange(H, dtype=np.float64)))
    i = np.arange(C, dtype=np.float64)
    decay_q = np.exp(ld[None, :] * (i[:, None] + 1)) / 16.0
    decay_state = np.exp(ld[None, :] * (C - 1 - i[:, None]))
    lamC = np.exp(ld * C).astype(np.float32)
    maskT = np.zeros((C, H * C), dtype=np.float64)
    for h in range(H):
        maskT[:, h * C:(h + 1) * C] = (np.exp(ld[h] * -(i[:, None] + 1.0))
                                       * (i[None, :] >= i[:, None]))
    inv = 1.0 / 10000.0 ** (np.arange(0, DK, 2, dtype=np.float64) / DK)
    t = np.arange(T, dtype=np.float64)
    f = t[:, None] * inv[None, :]
    c = dict(
        dq_ds=np.concatenate([decay_q, decay_state], 1).astype(np.float32),
        lamC=lamC,
        maskT=maskT.astype(np.float32),
        cos_tm=np.cos(f).astype(np.float32),
        sin_tm=np.sin(f).astype(np.float32),
        ident=np.eye(128, dtype=np.float32).astype(BF),
    )
    _cache["c"] = c
    return c


def _wt_relayout(w):
    dout, din = w.shape
    wt = np.asarray(w, np.float32).T
    return np.ascontiguousarray(
        wt.reshape(din // 128, 128, dout).transpose(1, 0, 2).reshape(128, -1))


class _Op:
    __slots__ = ("eng", "fn", "deps", "idx", "is_dma", "queue", "slot",
                 "event", "tag", "force_wait")

    def __init__(self, eng, fn, tag=""):
        self.eng = eng
        self.fn = fn
        self.deps = set()
        self.idx = -1
        self.is_dma = False
        self.queue = None
        self.slot = None
        self.event = None
        self.tag = tag
        self.force_wait = False


class P:
    ENGINES = ("tensor", "vector", "scalar", "gpsimd", "sync")

    def __init__(self, nc):
        self.nc = nc
        self.ops = []
        self.writer = {}
        self.readers = {}
        self.last_on = {}
        self.dmaq = {"sync": 0, "gpsimd": 0}

    @staticmethod
    def _flat(rs):
        out = []
        for r in rs:
            if isinstance(r, str):
                out.append(r)
            else:
                out.extend(r)
        return out

    def op(self, eng, fn, reads=(), writes=(), tag="", extra_deps=()):
        reads = self._flat(reads)
        writes = self._flat(writes)
        o = _Op(eng, fn, tag)
        deps = set(extra_deps)
        for r in reads:
            w = self.writer.get(r)
            if w is not None:
                deps.add(w)
        for r in writes:
            w = self.writer.get(r)
            if w is not None:
                deps.add(w)
            deps.update(self.readers.get(r, ()))
        deps.discard(o)
        o.deps = deps
        for r in reads:
            self.readers.setdefault(r, set()).add(o)
        for r in writes:
            self.writer[r] = o
            self.readers[r] = set()
        o.idx = len(self.ops)
        self.ops.append(o)
        self.last_on[eng] = o
        return o

    def dma(self, queue, out_ap, in_ap, reads=(), writes=(), tag=""):
        eng = "gpsimd" if queue == "gpsimd" else "sync"
        slot = self.dmaq[queue] % NSLOT
        self.dmaq[queue] += 1
        slotres = f"@dma.{queue}.{slot}"
        o = self.op(eng, lambda e, oa=out_ap, ia=in_ap: e.dma_start(oa, ia),
                    reads=tuple(reads), writes=tuple(writes) + (slotres,), tag=tag)
        o.is_dma = True
        o.queue = queue
        o.slot = slot
        return o

    def mm_group(self, psum_ap, pairs, reads=(), writes=(), tag=""):
        return self.mm_multi([(psum_ap, pairs)], reads=reads, writes=writes, tag=tag)

    def mm_multi(self, groups, reads=(), writes=(), tag=""):
        gg = tuple((ps, tuple(pairs)) for ps, pairs in groups)

        def fn(t, gg=gg):
            inst = None
            for ps, prs in gg:
                n = len(prs)
                for k, (l, r) in enumerate(prs):
                    inst = t.matmul(ps, l, r, start=(k == 0), stop=(k == n - 1))
            return inst
        return self.op("tensor", fn, reads=reads, writes=writes, tag=tag)

    def barrier(self):
        s1 = []
        for e in self.ENGINES:
            dep = self.last_on.get(e)
            o = self.op(e, lambda x: x.nop(),
                        extra_deps=(dep,) if dep is not None else ())
            o.force_wait = True
            s1.append(o)
        deps = set(s1)
        for w in self.writer.values():
            deps.add(w)
        for rl in self.readers.values():
            deps.update(rl)
        b = self.op("sync", lambda x: x.nop(), extra_deps=deps, tag="barrier")
        for r in list(self.writer):
            self.writer[r] = b
            self.readers[r] = set()
        return b

    def emit(self):
        nc = self.nc
        needed = set()
        for o in self.ops:
            for d in o.deps:
                if (d.eng == "tensor" and o.eng == "tensor" and not d.is_dma
                        and not o.force_wait):
                    continue
                needed.add(d)
        eng_sem = {e: nc.alloc_semaphore(name=f"es_{e}") for e in self.ENGINES}
        dma_sem = {q: [nc.alloc_semaphore(name=f"ds_{q}{s}") for s in range(NSLOT)]
                   for q in self.dmaq}
        eng_cnt = {e: 0 for e in self.ENGINES}
        slot_cnt = {q: [0] * NSLOT for q in self.dmaq}
        for o in self.ops:
            if o.is_dma:
                slot_cnt[o.queue][o.slot] += 1
                o.event = (dma_sem[o.queue][o.slot], 16 * slot_cnt[o.queue][o.slot])
            elif o in needed:
                eng_cnt[o.eng] += 1
                o.event = (eng_sem[o.eng], eng_cnt[o.eng])
        streams = {e: [o for o in self.ops if o.eng == e] for e in self.ENGINES}
        waited = {e: {} for e in self.ENGINES}

        def emit_stream(e, eng_handle):
            w = waited[e]
            for o in streams[e]:
                req = {}
                for d in o.deps:
                    if (d.eng == "tensor" and e == "tensor" and not d.is_dma
                            and not o.force_wait):
                        continue
                    sem, val = d.event
                    if req.get(sem, 0) < val:
                        req[sem] = val
                for sem, val in req.items():
                    if w.get(sem, 0) < val:
                        eng_handle.wait_ge(sem, val)
                        w[sem] = val
                inst = o.fn(eng_handle)
                try:
                    _cache.setdefault("instmap", {})[inst.ins.name] = o
                except Exception:
                    pass
                if o.event is not None:
                    sem, _ = o.event
                    inst.then_inc(sem, 16 if o.is_dma else 1)

        with nc.Block() as block:
            block.tensor(lambda t: emit_stream("tensor", t))
            block.vector(lambda v: emit_stream("vector", v))
            block.scalar(lambda s: emit_stream("scalar", s))
            block.gpsimd(lambda g: emit_stream("gpsimd", g))
            block.sync(lambda sy: emit_stream("sync", sy))
        n_waits = sum(len(s) for s in streams.values())
        return n_waits


def _ap(h, off, dims):
    return bass.AP(h, off, [list(d) for d in dims])


def build_program():
    key = ("nc",) + tuple(sorted(_LIMIT.items()))
    if key in _cache:
        return _cache[key]
    nc = bass.Bass("TRN2", target_bir_lowering=False)
    dt_in = {}

    def din(name, shape, dt=F32):
        h = nc.dram_tensor(name, list(shape), dt, kind="ExternalInput")
        dt_in[name] = h
        return h

    x_h = din("x", [TOK, D])
    wq_h = din("wq_t", [128, 8 * D])
    wk_h = din("wk_t", [128, 8 * D])
    wv_h = din("wv_t", [128, 8 * 2048])
    wg_h = din("wg_t", [128, 8 * 2048])
    wo_h = din("wo_t", [128, 16 * D])
    w1_h = din("w1_all", [NEXP, 128, 8 * D_FF])
    w2_h = din("w2_all", [NEXP, 128, 32 * D])
    b1_h = din("b1_r", [128, NEXP * 32])
    b2_h = din("b2_all", [NEXP, D])
    rw_h = din("rw_t", [128, 64])
    rb_h = din("rb_bc", [128, 64])
    gn_h = din("gnorm_r", [128, 16])
    id_h = din("ident", [128, 128], BF16)
    cos_h = din("cos_tm", [T, 128])
    sin_h = din("sin_tm", [T, 128])
    dqds_h = din("dq_ds", [128, 8])
    mask_h = din("maskT", [128, H * C])

    og_s = nc.dram_tensor("og_s", [TOK, H * DV], BF16, kind="Internal")
    x1_s = nc.dram_tensor("x1_s", [TOK, D], F32, kind="Internal")
    out_h = nc.dram_tensor("out", [TOK, D], F32, kind="ExternalOutput")

    lamC = _consts()["lamC"]

    p = P(nc)

    ident = nc.alloc_sbuf_tensor("identsb", [128, 128], BF16)
    dqds = nc.alloc_sbuf_tensor("dqds", [128, 8], F32)
    maskT = nc.alloc_sbuf_tensor("maskTsb", [128, H * C], F32)
    rb_bc = nc.alloc_sbuf_tensor("rbbc", [128, 64], F32)
    b1r = nc.alloc_sbuf_tensor("b1r", [128, NEXP * 32], F32)
    b2bf = nc.alloc_sbuf_tensor("b2bf", [NEXP, D], BF16)
    rwbf = nc.alloc_sbuf_tensor("rwbf", [128, 64], BF16)
    gnr = nc.alloc_sbuf_tensor("gnr", [128, 16], F32)
    cosb = [nc.alloc_sbuf_tensor(f"cosb{i}", [128, A_TILE * 128], F32) for i in range(2)]
    sinb = [nc.alloc_sbuf_tensor(f"sinb{i}", [128, A_TILE * 128], F32) for i in range(2)]
    stat = nc.alloc_sbuf_tensor("stat", [128, 32], F32)
    rms4 = nc.alloc_sbuf_tensor("rms4", [128, 16], F32)

    PA = nc.alloc_psum_tensor("PA", [128, 1024], F32)
    PB = nc.alloc_psum_tensor("PB", [128, 1024], F32)
    P0 = nc.alloc_psum_tensor("P0", [128, 512], F32)
    P1 = nc.alloc_psum_tensor("P1", [128, 512], F32)
    P2 = nc.alloc_psum_tensor("P2", [128, 512], F32)
    PT = nc.alloc_psum_tensor("PTb", [128, 1024], BF16)

    region_base = nc.sbuf_base
    region_top = nc.sbuf_top

    class Region:
        def __init__(self):
            self.cur = region_base

        def t(self, name, shape, dt):
            per_part = int(np.prod(shape[1:])) * mybir.dt.size(dt)
            off = (self.cur + 31) // 32 * 32
            h = nc.alloc_sbuf_tensor_at(name, list(shape), dt, offset=off)
            self.cur = off + per_part
            assert self.cur <= region_top, (name, self.cur, region_top)
            return h

    p.dma("sync", ident[:, :], id_h[:, :], writes=("ident",))
    p.dma("sync", dqds[:, :], dqds_h[:, :], writes=("dqds",))
    p.dma("sync", maskT[:, :], mask_h[:, :], writes=("maskT",))
    p.dma("sync", rb_bc[:, :], rb_h[:, :], writes=("rbbc",))
    p.dma("sync", b1r[:, :], b1_h[:, :], writes=("b1r",))
    p.dma("gpsimd", b2bf[:, :], b2_h[:, :], writes=("b2bf",))
    p.dma("gpsimd", rwbf[:, :], rw_h[:, :], writes=("rwbf",))
    p.dma("sync", gnr[:, :], gn_h[:, :], writes=("gnr",))
    p.op("gpsimd", lambda e: e.memset(stat[:, 9:10], EPS), writes=("epsc",))

    ra = Region()
    x_t = ra.t("x_t", [128, A_TILE * D], F32)
    xn_tm = ra.t("xn_tm", [128, D], BF16)
    xn_fm = ra.t("xn_fm", [128, D], BF16)
    qkraw = ra.t("qkraw", [128, D], F32)
    rt1 = ra.t("rt1", [128, 512], F32)
    rt2 = ra.t("rt2", [128, 512], F32)
    q_tm = ra.t("q_tm", [128, D], BF16)
    q_fm = ra.t("q_fm", [128, D], BF16)
    k_rope = ra.t("k_rope", [128, D], BF16)
    k_state = ra.t("k_state", [128, D], BF16)
    k_fm = ra.t("k_fm", [128, D], BF16)
    v_tm = ra.t("v_tm", [128, H * DV], BF16)
    atm = ra.t("atm", [128, H * C], BF16)
    gate = ra.t("gate", [128, H * DV], F32)
    o_sb = ra.t("o_sb", [128, H * DV], F32)
    o_sq = ra.t("o_sq", [128, DV], F32)
    og = ra.t("og", [128, H * DV], BF16)
    S = ra.t("S", [128, H * 2 * DV], F32)
    S_bf = ra.t("S_bf", [128, H * 2 * DV], BF16)
    wq_bf = ra.t("wq_bf", [128, 8 * D], BF16)
    wk_bf = ra.t("wk_bf", [128, 8 * D], BF16)
    wv_bf = ra.t("wv_bf", [128, 8 * 2048], BF16)
    wg_bf = ra.t("wg_bf", [128, 8 * 2048], BF16)

    p.dma("gpsimd", wq_bf[:, :], wq_h[:, :], writes=("wq",))
    p.dma("gpsimd", wk_bf[:, :], wk_h[:, :], writes=("wk",))
    for hlf in range(2):
        p.dma("gpsimd", wv_bf[:, hlf * 8192:(hlf + 1) * 8192],
              wv_h[:, hlf * 8192:(hlf + 1) * 8192], writes=(f"wv{hlf}",))
        p.dma("gpsimd", wg_bf[:, hlf * 8192:(hlf + 1) * 8192],
              wg_h[:, hlf * 8192:(hlf + 1) * 8192], writes=(f"wg{hlf}",))

    def emit_ln(x_ap, xres, out_ap, out_res, sq_scratch_ap, sq_scr_res):
        p.op("scalar", lambda e, i=x_ap, o=sq_scratch_ap: e.activation(
            o, i, AF.Square, accum_out=stat[:, 0:1]),
            reads=(xres,), writes=(sq_scr_res, "st0"), tag="ln_sq")
        p.op("vector", lambda e, i=x_ap: e.reduce_sum(
            stat[:, 1:2], i, AX.X, negate=True),
            reads=(xres,), writes=("st1",), tag="ln_sum")
        p.op("vector", lambda e: e.tensor_scalar(
            stat[:, 2:3], stat[:, 1:2], 1.0 / D, None, ALU.mult),
            reads=("st1",), writes=("st2",))
        p.op("vector", lambda e: e.tensor_mul(stat[:, 3:4], stat[:, 2:3], stat[:, 2:3]),
             reads=("st2",), writes=("st3",))
        p.op("vector", lambda e: e.tensor_scalar(
            stat[:, 4:5], stat[:, 0:1], 1.0 / D, None, ALU.mult),
            reads=("st0",), writes=("st4",))
        p.op("vector", lambda e: e.tensor_sub(stat[:, 5:6], stat[:, 4:5], stat[:, 3:4]),
             reads=("st3", "st4"), writes=("st5",))
        p.op("scalar", lambda e: e.activation(stat[:, 6:7], stat[:, 5:6], AF.Ln,
                                              bias=stat[:, 9:10]),
             reads=("st5", "epsc"), writes=("st6",))
        p.op("scalar", lambda e: e.activation(stat[:, 7:8], stat[:, 6:7], AF.Exp, scale=-0.5),
             reads=("st6",), writes=("st7",))
        p.op("vector", lambda e: e.tensor_mul(stat[:, 8:9], stat[:, 2:3], stat[:, 7:8]),
             reads=("st2", "st7"), writes=("st8",))
        p.op("scalar", lambda e, i=x_ap, o=out_ap: e.activation(
            o, i, AF.Identity, bias=stat[:, 8:9], scale=stat[:, 7:8]),
            reads=(xres, "st7", "st8"), writes=(out_res,), tag="ln_out")

    def emit_transpose8(src, src_res, dst, dst_res, nchunk=8):
        rr = (src_res,) if isinstance(src_res, str) else tuple(src_res)
        def fn(t, s=src, n=nchunk):
            inst = None
            for cc in range(n):
                inst = t.transpose(PT[:, cc * 128:(cc + 1) * 128],
                                   s[:, cc * 128:(cc + 1) * 128], ident[:, :])
            return inst
        p.op("tensor", fn, reads=rr + ("ident",), writes=("PT",), tag="T8")
        p.op("vector", lambda e, d=dst, n=nchunk: e.tensor_copy(
            d[:, :n * 128], PT[:, :n * 128]),
            reads=("PT",), writes=(dst_res,), tag="T8evac")

    def emit_rope(src_f32, src_res, dst_bf, dst_res, cosap, sinap, cs_res):
        x1 = _ap(src_f32, 0, [[1024, 128], [256, 4], [1, 128]])
        x2 = _ap(src_f32, 128, [[1024, 128], [256, 4], [1, 128]])
        o1 = _ap(dst_bf, 0, [[1024, 128], [256, 4], [1, 128]])
        o2 = _ap(dst_bf, 128, [[1024, 128], [256, 4], [1, 128]])
        t1 = _ap(rt1, 0, [[512, 128], [128, 4], [1, 128]])
        t2 = _ap(rt2, 0, [[512, 128], [128, 4], [1, 128]])
        p.op("vector", lambda e, a=x1, b=cosap, o=t1: e.tensor_mul(o, a, b),
             reads=(src_res,) + tuple(cs_res), writes=("rt1",))
        p.op("vector", lambda e, a=x2, b=sinap, o=t2: e.tensor_mul(o, a, b),
             reads=(src_res,) + tuple(cs_res), writes=("rt2",))
        p.op("vector", lambda e, o=o1: e.tensor_sub(o, t1, t2),
             reads=("rt1", "rt2"), writes=(dst_res + ".1",))
        p.op("vector", lambda e, a=x1, b=sinap, o=t1: e.tensor_mul(o, a, b),
             reads=(src_res,) + tuple(cs_res), writes=("rt1",))
        p.op("vector", lambda e, a=x2, b=cosap, o=t2: e.tensor_mul(o, a, b),
             reads=(src_res,) + tuple(cs_res), writes=("rt2",))
        p.op("vector", lambda e, o=o2: e.tensor_add(o, t1, t2),
             reads=("rt1", "rt2"), writes=(dst_res + ".2",))

    for blk in range(_LIMIT["a1"]):
        bt = blk // A_TILE
        bi = blk % A_TILE
        seq_blk = blk % BLK_PER_SEQ
        pg = bt % 2
        if blk % BLK_PER_SEQ == 0:
            p.op("gpsimd", lambda e: e.memset(S[:, :], 0.0), writes=("S",))
            p.op("gpsimd", lambda e: e.memset(S_bf[:, :], 0.0),
                 writes=tuple(f"S_bf.{h}" for h in range(H)))
        if bi == 0:
            p.dma("sync", _ap(x_t, 0, [[A_TILE * D, 128], [D, A_TILE], [1, D]]),
                  _ap(x_h, blk * C * D, [[D, 128], [C * D, A_TILE], [1, D]]),
                  writes=("x_t",), tag=f"xload{bt}")
            pos0 = (seq_blk) * C
            p.dma("sync", _ap(cosb[pg], 0, [[A_TILE * 128, 128], [128, A_TILE], [1, 128]]),
                  _ap(cos_h, pos0 * 128, [[128, 128], [C * 128, A_TILE], [1, 128]]),
                  writes=(f"cos{pg}",))
            p.dma("sync", _ap(sinb[pg], 0, [[A_TILE * 128, 128], [128, A_TILE], [1, 128]]),
                  _ap(sin_h, pos0 * 128, [[128, 128], [C * 128, A_TILE], [1, 128]]),
                  writes=(f"sin{pg}",))
        x_b = x_t[:, bi * D:(bi + 1) * D]
        cosap = _ap(cosb[pg], bi * 128, [[A_TILE * 128, 128], [0, 4], [1, 128]])
        sinap = _ap(sinb[pg], bi * 128, [[A_TILE * 128, 128], [0, 4], [1, 128]])
        csres = (f"cos{pg}", f"sin{pg}")

        emit_ln(x_b, "x_t", xn_tm[:, :], "xn_tm", gate[:, 0:D], "gate")
        emit_transpose8(xn_tm, "xn_tm", xn_fm, "xn_fm")

        for n in range(2):
            ps = PA[:, n * 512:(n + 1) * 512]
            pairs = [(xn_fm[:, kc * 128:(kc + 1) * 128],
                      wq_bf[:, kc * D + n * 512: kc * D + (n + 1) * 512])
                     for kc in range(8)]
            p.mm_group(ps, pairs, reads=("xn_fm", "wq"), writes=(f"PA{n}",), tag="Qmm")
        for h in range(H):
            sl = slice(h * 256, (h + 1) * 256)
            p.op("scalar", lambda e, s=sl, h_=h: e.activation(
                qkraw[:, s], _ap(PA, s.start, [[1024, 128], [1, 256]]), AF.Identity,
                scale=dqds[:, h_:h_ + 1]),
                reads=("PA0", "PA1", "dqds"), writes=("qkraw",), tag="Qevac")
        emit_rope(qkraw, "qkraw", q_tm, "q_tm", cosap, sinap, csres)
        emit_transpose8(q_tm, ("q_tm.1", "q_tm.2"), q_fm, "q_fm")

        for n in range(2):
            ps = PA[:, n * 512:(n + 1) * 512]
            pairs = [(xn_fm[:, kc * 128:(kc + 1) * 128],
                      wk_bf[:, kc * D + n * 512: kc * D + (n + 1) * 512])
                     for kc in range(8)]
            p.mm_group(ps, pairs, reads=("xn_fm", "wk"), writes=(f"PA{n}",), tag="Kmm")
        p.op("scalar", lambda e: e.activation(qkraw[:, :], PA[:, :], AF.Identity),
             reads=("PA0", "PA1"), writes=("qkraw",), tag="Kevac")
        emit_rope(qkraw, "qkraw", k_rope, "k_rope", cosap, sinap, csres)
        for h in range(H):
            sl = slice(h * 256, (h + 1) * 256)
            p.op("scalar", lambda e, s=sl, h_=h: e.activation(
                k_state[:, s], k_rope[:, s], AF.Identity, scale=dqds[:, 4 + h_:5 + h_]),
                reads=("k_rope.1", "k_rope.2", "dqds"), writes=("k_state",), tag="Kst")
        emit_transpose8(k_rope, ("k_rope.1", "k_rope.2"), k_fm, "k_fm")

        for kind, wbf, wres, pref in (("v", wv_bf, ("wv0", "wv1"), "V"),
                                      ("g", wg_bf, ("wg0", "wg1"), "G")):
            for s4 in range(4):
                bank = (P0, P1)[s4 % 2]
                bres = ("P0b", "P1b")[s4 % 2]
                pairs = [(xn_fm[:, kc * 128:(kc + 1) * 128],
                          wbf[:, kc * 2048 + s4 * 512: kc * 2048 + (s4 + 1) * 512])
                         for kc in range(8)]
                p.mm_group(bank[:, :], pairs, reads=("xn_fm",) + wres,
                           writes=(bres,), tag=pref + "mm")
                if kind == "v":
                    p.op("scalar", lambda e, b=bank, s=s4: e.activation(
                        v_tm[:, s * 512:(s + 1) * 512], b[:, :], AF.Identity),
                        reads=(bres,), writes=(f"v.{s4}",), tag="Vevac")
                else:
                    p.op("scalar", lambda e, b=bank, s=s4: e.activation(
                        gate[:, s * 512:(s + 1) * 512], b[:, :], AF.Silu),
                        reads=(bres,), writes=("gate",), tag="silu")

        for h in range(H):
            pairs = [(k_fm[:, cc * 128:(cc + 1) * 128],
                      q_fm[:, cc * 128:(cc + 1) * 128]) for cc in (2 * h, 2 * h + 1)]
            p.mm_group(P2[:, h * 128:(h + 1) * 128], pairs,
                       reads=("k_fm", "q_fm"), writes=("P2w",), tag="ATmm")
        p.op("vector", lambda e: e.tensor_mul(atm[:, :], P2[:, :], maskT[:, :]),
             reads=("P2w", "maskT"), writes=("atm",), tag="mask")

        for h in range(H):
            ob = (P0, P1)[h % 2]
            obres = ("P0b", "P1b")[h % 2]
            pairs = [(atm[:, h * 128:(h + 1) * 128], v_tm[:, h * 512:(h + 1) * 512])]
            pairs += [(q_fm[:, cc * 128:(cc + 1) * 128],
                       S_bf[:, h * 1024 + j * 512: h * 1024 + (j + 1) * 512])
                      for j, cc in enumerate((2 * h, 2 * h + 1))]
            p.mm_group(ob[:, :], pairs,
                       reads=("atm", f"v.{h}", "q_fm", f"S_bf.{h}"),
                       writes=(obres,), tag="Omm")
            p.op("scalar", lambda e, b=ob, h_=h: e.activation(
                o_sb[:, h_ * 512:(h_ + 1) * 512], b[:, :], AF.Identity),
                reads=(obres,), writes=(f"o.{h}",), tag="Oevac")
            p.op("scalar", lambda e, h_=h: e.activation(
                o_sq[:, :], o_sb[:, h_ * 512:(h_ + 1) * 512], AF.Square,
                accum_out=rms4[:, h_:h_ + 1]),
                reads=(f"o.{h}",), writes=("o_sq", f"rms.{h}"), tag="Osq")
            for j in range(2):
                ps = PA[:, j * 512:(j + 1) * 512]
                pairs = [(k_state[:, h * 256 + j * 128: h * 256 + (j + 1) * 128],
                          v_tm[:, h * 512:(h + 1) * 512])]
                p.mm_group(ps, pairs, reads=("k_state", f"v.{h}"),
                           writes=(f"PA{j}",), tag="Smm")
                sl = slice(h * 1024 + j * 512, h * 1024 + (j + 1) * 512)
                p.op("vector", lambda e, s=sl, ps_=ps, l=float(lamC[h]):
                     e.scalar_tensor_tensor(S[:, s], S[:, s], l, ps_, ALU.mult, ALU.add),
                     reads=(f"PA{j}", "S"), writes=("S",), tag="Supd")

        p.op("scalar", lambda e: e.activation(rms4[:, 4:8], rms4[:, 0:4], AF.Ln,
                                              scale=1.0 / DV, bias=stat[:, 9:10]),
             reads=tuple(f"rms.{h}" for h in range(H)) + ("epsc",),
             writes=("rmsl",), tag="rmsln")
        p.op("scalar", lambda e: e.activation(rms4[:, 8:12], rms4[:, 4:8], AF.Exp,
                                              scale=-0.5),
             reads=("rmsl",), writes=("rstd4",), tag="rmsexp")
        for h in range(H):
            sl = slice(h * 512, (h + 1) * 512)
            p.op("vector", lambda e, s=sl, h_=h: e.scalar_tensor_tensor(
                og[:, s], o_sb[:, s], rms4[:, 8 + h_:9 + h_], gate[:, s],
                ALU.mult, ALU.mult),
                reads=(f"o.{h}", "rstd4", "gate"), writes=("og",), tag="og")
        for h in range(H):
            sl = slice(h * 1024, (h + 1) * 1024)
            p.op("scalar", lambda e, s=sl: e.activation(S_bf[:, s], S[:, s], AF.Identity),
                 reads=("S",), writes=(f"S_bf.{h}",), tag="Sbf")
        p.dma("sync", _ap(og_s, blk * C * 2048, [[2048, 128], [1, 2048]]),
              og[:, :], reads=("og",), writes=("og_hbm",), tag="ogst")

    p.barrier()

    ra2 = Region()
    wo_bf = ra2.t("wo_bf", [128, 16 * D], BF16)
    x_t2 = ra2.t("x_t2", [128, A_TILE * D], F32)
    og_in = [ra2.t(f"og_in{i}", [128, H * DV], BF16) for i in range(2)]
    og_fm = ra2.t("og_fm", [128, 2048], BF16)
    x1o = [ra2.t(f"x1o{i}", [128, D], F32) for i in range(2)]

    p.dma("gpsimd", wo_bf[:, :], wo_h[:, :], writes=("wo_raw",))
    def wo_fold(e):
        inst = None
        for cc in range(16):
            inst = e.tensor_scalar(wo_bf[:, cc * D:(cc + 1) * D],
                                   wo_bf[:, cc * D:(cc + 1) * D],
                                   gnr[:, cc:cc + 1], None, ALU.mult)
        return inst
    p.op("vector", wo_fold, reads=("wo_raw", "gnr"), writes=("wo",), tag="wofold")

    for blk in range(_LIMIT["a2"]):
        bt, bi = blk // A_TILE, blk % A_TILE
        pg = blk % 2
        if bi == 0:
            p.dma("sync", _ap(x_t2, 0, [[A_TILE * D, 128], [D, A_TILE], [1, D]]),
                  _ap(x_h, blk * C * D, [[D, 128], [C * D, A_TILE], [1, D]]),
                  writes=("x_t2",), tag="x2load")
        if _LIMIT["a2p"] & 1:
            p.dma("sync", og_in[pg][:, :],
                  _ap(og_s, blk * C * 2048, [[2048, 128], [1, 2048]]),
                  reads=("og_hbm",), writes=(f"ogin{pg}",), tag="ogld")
        for half in range(2 if _LIMIT["a2p"] & 2 else 0):
            def fnT(t, src=og_in[pg], hf=half):
                inst = None
                for cc in range(8):
                    inst = t.transpose(PT[:, cc * 128:(cc + 1) * 128],
                                       src[:, (hf * 8 + cc) * 128:(hf * 8 + cc + 1) * 128],
                                       ident[:, :])
                return inst
            p.op("tensor", fnT, reads=(f"ogin{pg}", "ident"), writes=("PT",), tag="ogT")
            p.op("vector", lambda e, hf=half: e.tensor_copy(
                og_fm[:, hf * 1024:(hf + 1) * 1024], PT[:, :]),
                reads=("PT",), writes=(f"ogfm{half}",), tag="ogTev")
        pp = (PA, PB)[pg]
        ppres = (("PA0", "PA1"), ("PB0", "PB1"))[pg]
        for n in range(2 if _LIMIT["a2p"] & 4 else 0):
            pairs = [(og_fm[:, cc * 128:(cc + 1) * 128],
                      wo_bf[:, cc * D + n * 512: cc * D + (n + 1) * 512])
                     for cc in range(16)]
            p.mm_group(pp[:, n * 512:(n + 1) * 512], pairs,
                       reads=("ogfm0", "ogfm1", "wo"), writes=(ppres[n],), tag="WOmm")
        if _LIMIT["a2p"] & 8:
            p.op("vector", lambda e, pp_=pp, xb=x_t2[:, bi * D:(bi + 1) * D],
                 o=x1o[pg]: e.tensor_add(o[:, :], pp_[:, :], xb),
                 reads=ppres + ("x_t2",), writes=(f"x1o{pg}",), tag="resid")
            p.dma("sync", _ap(x1_s, blk * C * D, [[D, 128], [1, D]]), x1o[pg][:, :],
                  reads=(f"x1o{pg}",), writes=("x1_hbm",), tag="x1st")

    p.barrier()

    rb = Region()
    x1n_fm = rb.t("x1n_fm", [128, 8 * TT], BF16)
    acc = rb.t("acc", [128, TT * 8], F32)
    h_bf = rb.t("h_bf", [128, 32 * TT], BF16)
    wpc = [rb.t(f"wpc{i}", [128, 8192], BF16) for i in range(3)]
    x1b = [rb.t(f"x1b{i}", [128, D], F32) for i in range(2)]
    outb = [rb.t(f"outb{i}", [128, D], F32) for i in range(2)]
    x1n_tm = rb.t("x1n_tm", [128, D], BF16)
    lnsq = rb.t("lnsq", [128, D], F32)
    lg = rb.t("lg", [128, 64], F32)
    r_sm = rb.t("r_sm", [128, 64], F32)
    sm8 = rb.t("sm8", [128, 64], F32)
    comb = rb.t("comb", [128, 64], F32)
    comb9 = rb.t("comb9", [128, 72], BF16)
    cw_sb = rb.t("cw_sb", [9, 1024], BF16)

    wslot = [0]

    def wpiece(e_i, which, pc):
        s = wslot[0] % 3
        wslot[0] += 1
        src = w1_h if which == 1 else w2_h
        off = e_i * 128 * 32768 + (pc * 1024 if which == 1 else pc * 8192)
        ap_in = _ap(src, off, [[32768, 128], [4096 if which == 1 else 1024, 8], [1, 1024]])
        res = f"wpc{s}"
        p.dma("gpsimd", _ap(wpc[s], 0, [[8192, 128], [1024, 8], [1, 1024]]), ap_in,
              writes=(res,), tag=f"w{which}ld")
        return wpc[s], res

    lgP = [lg, rb.t("lgB", [128, 64], F32)]
    rsmP = [r_sm, rb.t("r_smB", [128, 64], F32)]
    sm8P = [sm8, rb.t("sm8B", [128, 64], F32)]
    combP = [comb, rb.t("combB", [128, 64], F32)]
    comb9P = [comb9, rb.t("comb9B", [128, 72], BF16)]
    cwP = [cw_sb, rb.t("cw_sbB", [9, 1024], BF16)]

    def b_prologue(tt):
        for b in range(BPT):
            pg = b % 2
            blk = tt * BPT + b
            p.dma("sync", x1b[pg][:, :], _ap(x1_s, blk * C * D, [[D, 128], [1, D]]),
                  reads=("x1_hbm",), writes=(f"x1b{pg}",), tag="x1ld")
            emit_ln(x1b[pg][:, :], f"x1b{pg}", x1n_tm[:, :], "x1n_tm",
                    lnsq[:, :], "lnsq")
            def fnT(t, src=x1n_tm):
                inst = None
                for cc in range(8):
                    inst = t.transpose(PT[:, cc * 128:(cc + 1) * 128],
                                       src[:, cc * 128:(cc + 1) * 128], ident[:, :])
                return inst
            p.op("tensor", fnT, reads=("x1n_tm", "ident"), writes=("PT",), tag="xnT")
            def fnTe(e, b_=b):
                inst = None
                for cc in range(8):
                    inst = e.tensor_copy(
                        x1n_fm[:, cc * TT + b_ * 128: cc * TT + (b_ + 1) * 128],
                        PT[:, cc * 128:(cc + 1) * 128])
                return inst
            p.op("vector", fnTe, reads=("PT",), writes=(f"xnfm.{b}",), tag="xnTev")
            pairs = [(x1n_fm[:, kc * TT + b * 128: kc * TT + (b + 1) * 128],
                      rwbf[:, kc * 8:(kc + 1) * 8]) for kc in range(8)]
            p.mm_group(P2[:, b * 8:(b + 1) * 8], pairs,
                       reads=(f"xnfm.{b}", "rwbf"), writes=("P2w",), tag="routmm")

    def b_softmax(tt):
        par = tt % 2
        lg_, r_sm_, sm8_, comb_ = lgP[par], rsmP[par], sm8P[par], combP[par]
        comb9_, cw_ = comb9P[par], cwP[par]
        R = lambda n: f"{n}.{par}"
        p.op("vector", lambda e: e.tensor_add(lg_[:, :], P2[:, 0:64], rb_bc[:, :]),
             reads=("P2w", "rbbc"), writes=(R("lg"),), tag="lgevac")
        lg3 = _ap(lg_, 0, [[64, 128], [8, 8], [1, 8]])
        p.op("vector", lambda e: e.tensor_reduce(
            _ap(sm8_, 0, [[64, 128], [1, 8], [1, 1]]), lg3, AX.X, ALU.max, negate=True),
            reads=(R("lg"),), writes=(R("negmax"),), tag="rmax")
        nmb = _ap(sm8_, 0, [[64, 128], [1, 8], [0, 8]])
        r3 = _ap(r_sm_, 0, [[64, 128], [8, 8], [1, 8]])
        p.op("vector", lambda e: e.tensor_tensor(r3, lg3, nmb, ALU.add),
             reads=(R("lg"), R("negmax")), writes=(R("r_sm"),), tag="subm")
        p.op("scalar", lambda e: e.activation(r_sm_[:, :], r_sm_[:, :], AF.Exp),
             reads=(R("r_sm"),), writes=(R("r_sm"),), tag="exp")
        p.op("vector", lambda e: e.tensor_reduce(
            _ap(sm8_, 8, [[64, 128], [1, 8], [1, 1]]), r3, AX.X, ALU.add),
            reads=(R("r_sm"),), writes=(R("ssum"),), tag="rsum")
        p.op("vector", lambda e: e.reciprocal(sm8_[:, 16:24], sm8_[:, 8:16]),
             reads=(R("ssum"),), writes=(R("srec"),))
        srb = _ap(sm8_, 16, [[64, 128], [1, 8], [0, 8]])
        p.op("vector", lambda e: e.tensor_tensor(r3, r3, srb, ALU.mult),
             reads=(R("r_sm"), R("srec")), writes=(R("r_sm"),), tag="rnorm")
        p.op("vector", lambda e: e.tensor_reduce(
            _ap(sm8_, 24, [[64, 128], [1, 8], [1, 1]]), r3, AX.X, ALU.max),
            reads=(R("r_sm"),), writes=(R("m1"),), tag="m1")
        m1b = _ap(sm8_, 24, [[64, 128], [1, 8], [0, 8]])
        c3 = _ap(comb_, 0, [[64, 128], [8, 8], [1, 8]])
        p.op("vector", lambda e: e.tensor_tensor(c3, r3, m1b, ALU.is_ge),
             reads=(R("r_sm"), R("m1")), writes=(R("comb"),), tag="eq1")
        p.op("vector", lambda e: e.scalar_tensor_tensor(c3, c3, -1e9, r3,
                                                        ALU.mult, ALU.add),
             reads=(R("comb"), R("r_sm")), writes=(R("comb"),), tag="r2")
        p.op("vector", lambda e: e.tensor_reduce(
            _ap(sm8_, 32, [[64, 128], [1, 8], [1, 1]]), c3, AX.X, ALU.max),
            reads=(R("comb"),), writes=(R("m2"),), tag="m2")
        m2b = _ap(sm8_, 32, [[64, 128], [1, 8], [0, 8]])
        p.op("vector", lambda e: e.tensor_tensor(c3, r3, m2b, ALU.is_ge),
             reads=(R("r_sm"), R("m2")), writes=(R("comb"),), tag="mask2")
        p.op("vector", lambda e: e.tensor_tensor(c3, c3, r3, ALU.mult),
             reads=(R("comb"), R("r_sm")), writes=(R("comb"),), tag="combu")
        p.op("vector", lambda e: e.tensor_add(sm8_[:, 40:48], sm8_[:, 24:32],
                                              sm8_[:, 32:40]),
             reads=(R("m1"), R("m2")), writes=(R("denom"),))
        p.op("vector", lambda e: e.reciprocal(sm8_[:, 48:56], sm8_[:, 40:48]),
             reads=(R("denom"),), writes=(R("drec"),))
        drb = _ap(sm8_, 48, [[64, 128], [1, 8], [0, 8]])
        p.op("vector", lambda e: e.tensor_tensor(c3, c3, drb, ALU.mult),
             reads=(R("comb"), R("drec")), writes=(R("comb"),), tag="combn")
        p.op("vector", lambda e: e.memset(
            _ap(comb9_, 0, [[72, 128], [9, 8], [1, 1]]), 1.0),
            writes=(R("comb9"),), tag="c9one")
        p.op("vector", lambda e: e.tensor_copy(
            _ap(comb9_, 1, [[72, 128], [9, 8], [1, 8]]), c3),
            reads=(R("comb"),), writes=(R("comb9"),), tag="c9cp")
        def fnc9T(t):
            inst = None
            for b_ in range(8):
                inst = t.transpose(_ap(PT, b_ * 128, [[1024, 9], [1, 128]]),
                                   comb9_[:, b_ * 9:(b_ + 1) * 9], ident[:, :])
            return inst
        p.op("tensor", fnc9T, reads=(R("comb9"), "ident"), writes=("PT",), tag="c9T")
        p.op("vector", lambda e: e.tensor_copy(cw_[0:9, :],
                                               _ap(PT, 0, [[1024, 9], [1, 1024]])),
             reads=("PT",), writes=(R("cw"),), tag="cwev")

    def b_experts(tt):
        par = tt % 2
        comb_ = combP[par]
        n_e = _LIMIT["ne"]
        for e_i in range(n_e):
            for pc in range(4):
                wsb, wres = wpiece(e_i, 1, pc)
                for mb in range(8):
                    mglob = pc * 8 + mb
                    pp = (PA, PB)[mglob % 2]
                    ppres = (("PA0", "PA1"), ("PB0", "PB1"))[mglob % 2]
                    groups = [(pp[:, n * 512:(n + 1) * 512],
                               [(wsb[:, kc * 1024 + mb * 128: kc * 1024 + (mb + 1) * 128],
                                 x1n_fm[:, kc * TT + n * 512: kc * TT + (n + 1) * 512])
                                for kc in range(8)]) for n in range(2)]
                    p.mm_multi(groups,
                               reads=(wres,) + tuple(f"xnfm.{b}" for b in range(8)),
                               writes=ppres, tag="h_mm")
                    p.op("scalar", lambda e, pp_=pp, m=mglob, ei=e_i: e.activation(
                        h_bf[:, m * TT:(m + 1) * TT], pp_[:, :], AF.Gelu_apprx_tanh,
                        bias=b1r[:, ei * 32 + m: ei * 32 + m + 1]),
                        reads=ppres + ("b1r",), writes=(f"h.{mglob}",), tag="gelu")
            if e_i == n_e - 1 and tt + 1 < _LIMIT["bt"]:
                b_prologue(tt + 1)
                b_softmax(tt + 1)
            cw_ = cwP[par]
            for pc in range(4):
                wsb, wres = wpiece(e_i, 2, pc)
                for b in range(BPT):
                    g = b + pc * BPT
                    pp = (PA, PB)[g % 2]
                    ppres = (("PA0", "PA1"), ("PB0", "PB1"))[g % 2]
                    hres = tuple(f"h.{pc * 8 + k}" for k in range(8))
                    rds = (wres,) + hres
                    groups = []
                    for n in range(2):
                        pairs = []
                        if e_i == 0 and pc == 0:
                            pairs.append((cw_[0:9, b * 128:(b + 1) * 128],
                                          b2bf[0:9, n * 512:(n + 1) * 512]))
                        pairs += [(h_bf[:, (pc * 8 + k) * TT + b * 128:
                                        (pc * 8 + k) * TT + (b + 1) * 128],
                                   wsb[:, k * 1024 + n * 512: k * 1024 + (n + 1) * 512])
                                  for k in range(8)]
                        groups.append((pp[:, n * 512:(n + 1) * 512], pairs))
                    if e_i == 0 and pc == 0:
                        rds = rds + (f"cw.{par}", "b2bf")
                    p.mm_multi(groups, reads=rds, writes=ppres, tag="y_mm")
                    if e_i == 0 and pc == 0:
                        p.op("vector", lambda e, pp_=pp, b_=b: e.tensor_copy(
                            acc[:, b_ * D:(b_ + 1) * D], pp_[:, :]),
                            reads=ppres, writes=(f"acc.{b}",), tag="acc0")
                    elif e_i == 0:
                        p.op("vector", lambda e, pp_=pp, b_=b: e.tensor_add(
                            acc[:, b_ * D:(b_ + 1) * D], acc[:, b_ * D:(b_ + 1) * D],
                            pp_[:, :]),
                            reads=ppres + (f"acc.{b}",), writes=(f"acc.{b}",), tag="accs")
                    else:
                        p.op("vector", lambda e, pp_=pp, b_=b, ei=e_i:
                             e.scalar_tensor_tensor(
                                 acc[:, b_ * D:(b_ + 1) * D], pp_[:, :],
                                 comb_[:, b_ * 8 + ei - 1: b_ * 8 + ei],
                                 acc[:, b_ * D:(b_ + 1) * D], ALU.mult, ALU.add),
                             reads=ppres + (f"acc.{b}", f"comb.{par}"),
                             writes=(f"acc.{b}",), tag="acce")

    def b_final(tt):
        for b in range(BPT):
            pg = b % 2
            blk = tt * BPT + b
            p.dma("sync", x1b[pg][:, :], _ap(x1_s, blk * C * D, [[D, 128], [1, D]]),
                  reads=("x1_hbm",), writes=(f"x1b{pg}",), tag="x1ld2")
            p.op("vector", lambda e, b_=b, pg_=pg: e.tensor_add(
                outb[pg_][:, :], acc[:, b_ * D:(b_ + 1) * D], x1b[pg_][:, :]),
                reads=(f"acc.{b}", f"x1b{pg}"), writes=(f"outb{pg}",), tag="out")
            p.dma("sync", _ap(out_h, blk * C * D, [[D, 128], [1, D]]), outb[pg][:, :],
                  reads=(f"outb{pg}",), writes=("out_hbm",), tag="outst")

    if _LIMIT["bt"] > 0:
        b_prologue(0)
        b_softmax(0)
    for tt in range(_LIMIT["bt"]):
        b_experts(tt)
        b_final(tt)

    p.barrier()
    n = p.emit()
    _cache[key] = nc
    _cache["n_ops"] = len(p.ops)
    return nc


def kernel(**inputs):
    cst = _consts()
    gn = np.asarray(inputs["L0_attn_g_norm_w"], np.float32)
    gn_bc = np.tile(gn, H)
    gnorm_r = np.ascontiguousarray(gn_bc.reshape(16, 128).T)
    rb = np.asarray(inputs["L0_ffn_router_b"], np.float32)
    rb_bc = np.ascontiguousarray(np.broadcast_to(np.tile(rb, 8)[None, :], (128, 64)))
    b1 = np.concatenate([np.asarray(inputs["L0_ffn_shared_b1"], np.float32)[None],
                         np.asarray(inputs["L0_ffn_spec_b1"], np.float32)], 0)
    b1_r = np.ascontiguousarray(
        b1.reshape(NEXP, 32, 128).transpose(2, 0, 1).reshape(128, NEXP * 32))
    b2 = np.concatenate([np.asarray(inputs["L0_ffn_shared_b2"], np.float32)[None],
                         np.asarray(inputs["L0_ffn_spec_b2"], np.float32)], 0)
    w1 = np.concatenate([np.asarray(inputs["L0_ffn_shared_w1"], np.float32)[None],
                         np.asarray(inputs["L0_ffn_spec_w1"], np.float32)], 0)
    w2 = np.concatenate([np.asarray(inputs["L0_ffn_shared_w2"], np.float32)[None],
                         np.asarray(inputs["L0_ffn_spec_w2"], np.float32)], 0)
    w1_all = np.stack([_wt_relayout(w1[i]) for i in range(NEXP)], 0)
    w2_all = np.stack([_wt_relayout(w2[i]) for i in range(NEXP)], 0)

    shared = dict(
        wq_t=_wt_relayout(np.asarray(inputs["L0_attn_Wq"], np.float32)),
        wk_t=_wt_relayout(np.asarray(inputs["L0_attn_Wk"], np.float32)),
        wv_t=_wt_relayout(np.asarray(inputs["L0_attn_Wv"], np.float32)),
        wg_t=_wt_relayout(np.asarray(inputs["L0_attn_Wg"], np.float32)),
        wo_t=_wt_relayout(np.asarray(inputs["L0_attn_Wo"], np.float32)),
        w1_all=w1_all, w2_all=w2_all, b1_r=b1_r, b2_all=np.ascontiguousarray(b2),
        rw_t=_wt_relayout(np.asarray(inputs["L0_ffn_router_w"], np.float32)),
        rb_bc=rb_bc, gnorm_r=gnorm_r, ident=cst["ident"],
        cos_tm=cst["cos_tm"], sin_tm=cst["sin_tm"], dq_ds=cst["dq_ds"],
        maskT=cst["maskT"],
    )
    x = np.asarray(inputs["L0_attn_x"], np.float32)
    nc = build_program()
    in_maps = []
    for c in range(NCORES):
        m = dict(shared)
        m["x"] = np.ascontiguousarray(x[NSEQ * c:NSEQ * (c + 1)].reshape(TOK, D))
        in_maps.append(m)
    res = run_bass_kernel_spmd(nc, in_maps, core_ids=list(range(NCORES)))
    out = np.empty((16, T, D), np.float32)
    for c in range(NCORES):
        out[NSEQ * c:NSEQ * (c + 1)] = res.results[c]["out"].reshape(NSEQ, T, D)
    return out


# version 05
# speedup vs baseline: 1.0031x; 1.0031x over previous
import numpy as np
import ml_dtypes
import concourse.bass as bass
import concourse.mybir as mybir
from concourse.bass_utils import run_bass_kernel_spmd

F32 = mybir.dt.float32
BF16 = mybir.dt.bfloat16
AF = mybir.ActivationFunctionType
ALU = mybir.AluOpType
AX = mybir.AxisListType
BF = ml_dtypes.bfloat16

NCORES = 8
T = 4096
D = 1024
H, DK, DV = 4, 256, 512
C = 128
NSEQ = 2
TOK = NSEQ * T
NBLK = TOK // C
BLK_PER_SEQ = T // C
A_TILE = 4
D_FF = 4096
NEXP = 9
TT = 1024
NBT = TOK // TT
BPT = TT // C
EPS = 1e-5
NSLOT = 8

_cache = {}
_LIMIT = {"a1": 64, "a2": 64, "bt": 8, "ne": 9, "a2p": 15}


def _consts():
    if "c" in _cache:
        return _cache["c"]
    ld = np.log1p(-np.exp2(-5.0 - np.arange(H, dtype=np.float64)))
    i = np.arange(C, dtype=np.float64)
    decay_q = np.exp(ld[None, :] * (i[:, None] + 1)) / 16.0
    decay_state = np.exp(ld[None, :] * (C - 1 - i[:, None]))
    lamC = np.exp(ld * C).astype(np.float32)
    maskT = np.zeros((C, H * C), dtype=np.float64)
    for h in range(H):
        maskT[:, h * C:(h + 1) * C] = (np.exp(ld[h] * -(i[:, None] + 1.0))
                                       * (i[None, :] >= i[:, None]))
    inv = 1.0 / 10000.0 ** (np.arange(0, DK, 2, dtype=np.float64) / DK)
    t = np.arange(T, dtype=np.float64)
    f = t[:, None] * inv[None, :]
    c = dict(
        dq_ds=np.concatenate([decay_q, decay_state], 1).astype(np.float32),
        lamC=lamC,
        maskT=maskT.astype(np.float32),
        cos_tm=np.cos(f).astype(np.float32),
        sin_tm=np.sin(f).astype(np.float32),
        ident=np.eye(128, dtype=np.float32).astype(BF),
    )
    _cache["c"] = c
    return c


def _wt_relayout(w):
    dout, din = w.shape
    wt = np.asarray(w, np.float32).T
    return np.ascontiguousarray(
        wt.reshape(din // 128, 128, dout).transpose(1, 0, 2).reshape(128, -1))


class _Op:
    __slots__ = ("eng", "fn", "deps", "idx", "is_dma", "queue", "slot",
                 "event", "tag", "force_wait")

    def __init__(self, eng, fn, tag=""):
        self.eng = eng
        self.fn = fn
        self.deps = set()
        self.idx = -1
        self.is_dma = False
        self.queue = None
        self.slot = None
        self.event = None
        self.tag = tag
        self.force_wait = False


class P:
    ENGINES = ("tensor", "vector", "scalar", "gpsimd", "sync")

    def __init__(self, nc):
        self.nc = nc
        self.ops = []
        self.writer = {}
        self.readers = {}
        self.last_on = {}
        self.dmaq = {"sync": 0, "gpsimd": 0}

    @staticmethod
    def _flat(rs):
        out = []
        for r in rs:
            if isinstance(r, str):
                out.append(r)
            else:
                out.extend(r)
        return out

    def op(self, eng, fn, reads=(), writes=(), tag="", extra_deps=()):
        reads = self._flat(reads)
        writes = self._flat(writes)
        o = _Op(eng, fn, tag)
        deps = set(extra_deps)
        for r in reads:
            w = self.writer.get(r)
            if w is not None:
                deps.add(w)
        for r in writes:
            w = self.writer.get(r)
            if w is not None:
                deps.add(w)
            deps.update(self.readers.get(r, ()))
        deps.discard(o)
        o.deps = deps
        for r in reads:
            self.readers.setdefault(r, set()).add(o)
        for r in writes:
            self.writer[r] = o
            self.readers[r] = set()
        o.idx = len(self.ops)
        self.ops.append(o)
        self.last_on[eng] = o
        return o

    def dma(self, queue, out_ap, in_ap, reads=(), writes=(), tag=""):
        eng = "gpsimd" if queue == "gpsimd" else "sync"
        slot = self.dmaq[queue] % NSLOT
        self.dmaq[queue] += 1
        slotres = f"@dma.{queue}.{slot}"
        o = self.op(eng, lambda e, oa=out_ap, ia=in_ap: e.dma_start(oa, ia),
                    reads=tuple(reads), writes=tuple(writes) + (slotres,), tag=tag)
        o.is_dma = True
        o.queue = queue
        o.slot = slot
        return o

    def mm_group(self, psum_ap, pairs, reads=(), writes=(), tag=""):
        return self.mm_multi([(psum_ap, pairs)], reads=reads, writes=writes, tag=tag)

    def mm_multi(self, groups, reads=(), writes=(), tag=""):
        gg = tuple((ps, tuple(pairs)) for ps, pairs in groups)

        def fn(t, gg=gg):
            inst = None
            for ps, prs in gg:
                n = len(prs)
                for k, (l, r) in enumerate(prs):
                    inst = t.matmul(ps, l, r, start=(k == 0), stop=(k == n - 1))
            return inst
        return self.op("tensor", fn, reads=reads, writes=writes, tag=tag)

    def barrier(self):
        s1 = []
        for e in self.ENGINES:
            dep = self.last_on.get(e)
            o = self.op(e, lambda x: x.nop(),
                        extra_deps=(dep,) if dep is not None else ())
            o.force_wait = True
            s1.append(o)
        deps = set(s1)
        for w in self.writer.values():
            deps.add(w)
        for rl in self.readers.values():
            deps.update(rl)
        b = self.op("sync", lambda x: x.nop(), extra_deps=deps, tag="barrier")
        for r in list(self.writer):
            self.writer[r] = b
            self.readers[r] = set()
        return b

    def emit(self):
        nc = self.nc
        needed = set()
        for o in self.ops:
            for d in o.deps:
                if (d.eng == "tensor" and o.eng == "tensor" and not d.is_dma
                        and not o.force_wait):
                    continue
                needed.add(d)
        eng_sem = {e: nc.alloc_semaphore(name=f"es_{e}") for e in self.ENGINES}
        dma_sem = {q: [nc.alloc_semaphore(name=f"ds_{q}{s}") for s in range(NSLOT)]
                   for q in self.dmaq}
        eng_cnt = {e: 0 for e in self.ENGINES}
        slot_cnt = {q: [0] * NSLOT for q in self.dmaq}
        for o in self.ops:
            if o.is_dma:
                slot_cnt[o.queue][o.slot] += 1
                o.event = (dma_sem[o.queue][o.slot], 16 * slot_cnt[o.queue][o.slot])
            elif o in needed:
                eng_cnt[o.eng] += 1
                o.event = (eng_sem[o.eng], eng_cnt[o.eng])
        streams = {e: [o for o in self.ops if o.eng == e] for e in self.ENGINES}
        waited = {e: {} for e in self.ENGINES}

        def emit_stream(e, eng_handle):
            w = waited[e]
            for o in streams[e]:
                req = {}
                for d in o.deps:
                    if (d.eng == "tensor" and e == "tensor" and not d.is_dma
                            and not o.force_wait):
                        continue
                    sem, val = d.event
                    if req.get(sem, 0) < val:
                        req[sem] = val
                for sem, val in req.items():
                    if w.get(sem, 0) < val:
                        eng_handle.wait_ge(sem, val)
                        w[sem] = val
                inst = o.fn(eng_handle)
                try:
                    _cache.setdefault("instmap", {})[inst.ins.name] = o
                except Exception:
                    pass
                if o.event is not None:
                    sem, _ = o.event
                    inst.then_inc(sem, 16 if o.is_dma else 1)

        with nc.Block() as block:
            block.tensor(lambda t: emit_stream("tensor", t))
            block.vector(lambda v: emit_stream("vector", v))
            block.scalar(lambda s: emit_stream("scalar", s))
            block.gpsimd(lambda g: emit_stream("gpsimd", g))
            block.sync(lambda sy: emit_stream("sync", sy))
        n_waits = sum(len(s) for s in streams.values())
        return n_waits


def _ap(h, off, dims):
    return bass.AP(h, off, [list(d) for d in dims])


def build_program():
    key = ("nc",) + tuple(sorted(_LIMIT.items()))
    if key in _cache:
        return _cache[key]
    nc = bass.Bass("TRN2", target_bir_lowering=False)
    dt_in = {}

    def din(name, shape, dt=F32):
        h = nc.dram_tensor(name, list(shape), dt, kind="ExternalInput")
        dt_in[name] = h
        return h

    x_h = din("x", [TOK, D])
    wq_h = din("wq_t", [128, 8 * D])
    wk_h = din("wk_t", [128, 8 * D])
    wv_h = din("wv_t", [128, 8 * 2048])
    wg_h = din("wg_t", [128, 8 * 2048])
    wo_h = din("wo_t", [128, 16 * D])
    w1_h = din("w1_all", [NEXP, 128, 8 * D_FF])
    w2_h = din("w2_all", [NEXP, 128, 32 * D])
    b1_h = din("b1_r", [128, NEXP * 32])
    b2_h = din("b2_all", [NEXP, D])
    rw_h = din("rw_t", [128, 64])
    rb_h = din("rb_bc", [128, 64])
    gn_h = din("gnorm_r", [128, 16])
    id_h = din("ident", [128, 128], BF16)
    cos_h = din("cos_tm", [T, 128])
    sin_h = din("sin_tm", [T, 128])
    dqds_h = din("dq_ds", [128, 8])
    mask_h = din("maskT", [128, H * C])

    og_s = nc.dram_tensor("og_s", [TOK, H * DV], BF16, kind="Internal")
    x1_s = nc.dram_tensor("x1_s", [TOK, D], F32, kind="Internal")
    out_h = nc.dram_tensor("out", [TOK, D], F32, kind="ExternalOutput")

    lamC = _consts()["lamC"]

    p = P(nc)

    ident = nc.alloc_sbuf_tensor("identsb", [128, 128], BF16)
    dqds = nc.alloc_sbuf_tensor("dqds", [128, 8], F32)
    maskT = nc.alloc_sbuf_tensor("maskTsb", [128, H * C], F32)
    rb_bc = nc.alloc_sbuf_tensor("rbbc", [128, 64], F32)
    b1r = nc.alloc_sbuf_tensor("b1r", [128, NEXP * 32], F32)
    b2bf = nc.alloc_sbuf_tensor("b2bf", [NEXP, D], BF16)
    rwbf = nc.alloc_sbuf_tensor("rwbf", [128, 64], BF16)
    gnr = nc.alloc_sbuf_tensor("gnr", [128, 16], F32)
    cosb = [nc.alloc_sbuf_tensor(f"cosb{i}", [128, A_TILE * 128], F32) for i in range(2)]
    sinb = [nc.alloc_sbuf_tensor(f"sinb{i}", [128, A_TILE * 128], F32) for i in range(2)]
    stat = nc.alloc_sbuf_tensor("stat", [128, 32], F32)
    rms4 = nc.alloc_sbuf_tensor("rms4", [128, 16], F32)

    PA = nc.alloc_psum_tensor("PA", [128, 1024], F32)
    PB = nc.alloc_psum_tensor("PB", [128, 1024], F32)
    P0 = nc.alloc_psum_tensor("P0", [128, 512], F32)
    P1 = nc.alloc_psum_tensor("P1", [128, 512], F32)
    PT = nc.alloc_psum_tensor("PTb", [128, 1024], BF16)
    PT2 = nc.alloc_psum_tensor("PTb2", [128, 1024], BF16)
    PTS = [(PT, "PT"), (PT2, "PT2")]

    region_base = nc.sbuf_base
    region_top = nc.sbuf_top

    class Region:
        def __init__(self):
            self.cur = region_base

        def t(self, name, shape, dt):
            per_part = int(np.prod(shape[1:])) * mybir.dt.size(dt)
            off = (self.cur + 31) // 32 * 32
            h = nc.alloc_sbuf_tensor_at(name, list(shape), dt, offset=off)
            self.cur = off + per_part
            assert self.cur <= region_top, (name, self.cur, region_top)
            return h

    p.dma("sync", ident[:, :], id_h[:, :], writes=("ident",))
    p.dma("sync", dqds[:, :], dqds_h[:, :], writes=("dqds",))
    p.dma("sync", maskT[:, :], mask_h[:, :], writes=("maskT",))
    p.dma("sync", rb_bc[:, :], rb_h[:, :], writes=("rbbc",))
    p.dma("sync", b1r[:, :], b1_h[:, :], writes=("b1r",))
    p.dma("gpsimd", b2bf[:, :], b2_h[:, :], writes=("b2bf",))
    p.dma("gpsimd", rwbf[:, :], rw_h[:, :], writes=("rwbf",))
    p.dma("sync", gnr[:, :], gn_h[:, :], writes=("gnr",))
    p.op("gpsimd", lambda e: e.memset(stat[:, 9:10], EPS), writes=("epsc",))

    ra = Region()
    x_t = ra.t("x_t", [128, A_TILE * D], F32)
    xn_tm = ra.t("xn_tm", [128, D], BF16)
    xn_fm = ra.t("xn_fm", [128, D], BF16)
    qkraw = ra.t("qkraw", [128, D], F32)
    rt1 = ra.t("rt1", [128, 512], F32)
    rt2 = ra.t("rt2", [128, 512], F32)
    q_tm = ra.t("q_tm", [128, D], BF16)
    q_fm = ra.t("q_fm", [128, D], BF16)
    k_rope = ra.t("k_rope", [128, D], BF16)
    k_state = ra.t("k_state", [128, D], BF16)
    k_fm = ra.t("k_fm", [128, D], BF16)
    v_tm = ra.t("v_tm", [128, H * DV], BF16)
    atm = ra.t("atm", [128, H * C], BF16)
    gate = ra.t("gate", [128, H * DV], F32)
    o_sb = ra.t("o_sb", [128, H * DV], F32)
    o_sq = ra.t("o_sq", [128, DV], F32)
    og = ra.t("og", [128, H * DV], BF16)
    S = ra.t("S", [128, H * 2 * DV], F32)
    S_bf = ra.t("S_bf", [128, H * 2 * DV], BF16)
    lnsq_a = ra.t("lnsq_a", [128, D], F32)
    wq_bf = ra.t("wq_bf", [128, 8 * D], BF16)
    wk_bf = ra.t("wk_bf", [128, 8 * D], BF16)
    wv_bf = ra.t("wv_bf", [128, 8 * 2048], BF16)
    wg_bf = ra.t("wg_bf", [128, 8 * 2048], BF16)

    p.dma("gpsimd", wq_bf[:, :], wq_h[:, :], writes=("wq",))
    p.dma("gpsimd", wk_bf[:, :], wk_h[:, :], writes=("wk",))
    for hlf in range(2):
        p.dma("gpsimd", wv_bf[:, hlf * 8192:(hlf + 1) * 8192],
              wv_h[:, hlf * 8192:(hlf + 1) * 8192], writes=(f"wv{hlf}",))
        p.dma("gpsimd", wg_bf[:, hlf * 8192:(hlf + 1) * 8192],
              wg_h[:, hlf * 8192:(hlf + 1) * 8192], writes=(f"wg{hlf}",))

    def emit_ln(x_ap, xres, out_ap, out_res, sq_scratch_ap, sq_scr_res):
        p.op("scalar", lambda e, i=x_ap, o=sq_scratch_ap: e.activation(
            o, i, AF.Square, accum_out=stat[:, 0:1]),
            reads=(xres,), writes=(sq_scr_res, "st0"), tag="ln_sq")
        p.op("vector", lambda e, i=x_ap: e.reduce_sum(
            stat[:, 1:2], i, AX.X, negate=True),
            reads=(xres,), writes=("st1",), tag="ln_sum")
        p.op("vector", lambda e: e.tensor_scalar(
            stat[:, 2:3], stat[:, 1:2], 1.0 / D, None, ALU.mult),
            reads=("st1",), writes=("st2",))
        p.op("vector", lambda e: e.tensor_mul(stat[:, 3:4], stat[:, 2:3], stat[:, 2:3]),
             reads=("st2",), writes=("st3",))
        p.op("vector", lambda e: e.tensor_scalar(
            stat[:, 4:5], stat[:, 0:1], 1.0 / D, None, ALU.mult),
            reads=("st0",), writes=("st4",))
        p.op("vector", lambda e: e.tensor_sub(stat[:, 5:6], stat[:, 4:5], stat[:, 3:4]),
             reads=("st3", "st4"), writes=("st5",))
        p.op("scalar", lambda e: e.activation(stat[:, 6:7], stat[:, 5:6], AF.Ln,
                                              bias=stat[:, 9:10]),
             reads=("st5", "epsc"), writes=("st6",))
        p.op("scalar", lambda e: e.activation(stat[:, 7:8], stat[:, 6:7], AF.Exp, scale=-0.5),
             reads=("st6",), writes=("st7",))
        p.op("vector", lambda e: e.tensor_mul(stat[:, 8:9], stat[:, 2:3], stat[:, 7:8]),
             reads=("st2", "st7"), writes=("st8",))
        p.op("scalar", lambda e, i=x_ap, o=out_ap: e.activation(
            o, i, AF.Identity, bias=stat[:, 8:9], scale=stat[:, 7:8]),
            reads=(xres, "st7", "st8"), writes=(out_res,), tag="ln_out")

    def emit_transpose8(src, src_res, dst, dst_res, nchunk=8, slot=0):
        rr = (src_res,) if isinstance(src_res, str) else tuple(src_res)
        pt, ptres = PTS[slot % 2]
        def fn(t, s=src, n=nchunk, pt_=pt):
            inst = None
            for cc in range(n):
                inst = t.transpose(pt_[:, cc * 128:(cc + 1) * 128],
                                   s[:, cc * 128:(cc + 1) * 128], ident[:, :])
            return inst
        p.op("tensor", fn, reads=rr + ("ident",), writes=(ptres,), tag="T8")
        p.op("vector", lambda e, d=dst, n=nchunk, pt_=pt: e.tensor_copy(
            d[:, :n * 128], pt_[:, :n * 128]),
            reads=(ptres,), writes=(dst_res,), tag="T8evac")

    def emit_rope(src_f32, src_res, dst_bf, dst_res, cosap, sinap, cs_res):
        x1 = _ap(src_f32, 0, [[1024, 128], [256, 4], [1, 128]])
        x2 = _ap(src_f32, 128, [[1024, 128], [256, 4], [1, 128]])
        o1 = _ap(dst_bf, 0, [[1024, 128], [256, 4], [1, 128]])
        o2 = _ap(dst_bf, 128, [[1024, 128], [256, 4], [1, 128]])
        t1 = _ap(rt1, 0, [[512, 128], [128, 4], [1, 128]])
        t2 = _ap(rt2, 0, [[512, 128], [128, 4], [1, 128]])
        p.op("vector", lambda e, a=x1, b=cosap, o=t1: e.tensor_mul(o, a, b),
             reads=(src_res,) + tuple(cs_res), writes=("rt1",))
        p.op("vector", lambda e, a=x2, b=sinap, o=t2: e.tensor_mul(o, a, b),
             reads=(src_res,) + tuple(cs_res), writes=("rt2",))
        p.op("vector", lambda e, o=o1: e.tensor_sub(o, t1, t2),
             reads=("rt1", "rt2"), writes=(dst_res + ".1",))
        p.op("vector", lambda e, a=x1, b=sinap, o=t1: e.tensor_mul(o, a, b),
             reads=(src_res,) + tuple(cs_res), writes=("rt1",))
        p.op("vector", lambda e, a=x2, b=cosap, o=t2: e.tensor_mul(o, a, b),
             reads=(src_res,) + tuple(cs_res), writes=("rt2",))
        p.op("vector", lambda e, o=o2: e.tensor_add(o, t1, t2),
             reads=("rt1", "rt2"), writes=(dst_res + ".2",))

    for blk in range(_LIMIT["a1"]):
        bt = blk // A_TILE
        bi = blk % A_TILE
        seq_blk = blk % BLK_PER_SEQ
        pg = bt % 2
        if blk % BLK_PER_SEQ == 0:
            p.op("gpsimd", lambda e: e.memset(S[:, :], 0.0), writes=("S",))
            p.op("gpsimd", lambda e: e.memset(S_bf[:, :], 0.0),
                 writes=tuple(f"S_bf.{h}" for h in range(H)))
        if bi == 0:
            p.dma("sync", _ap(x_t, 0, [[A_TILE * D, 128], [D, A_TILE], [1, D]]),
                  _ap(x_h, blk * C * D, [[D, 128], [C * D, A_TILE], [1, D]]),
                  writes=("x_t",), tag=f"xload{bt}")
            pos0 = (seq_blk) * C
            p.dma("sync", _ap(cosb[pg], 0, [[A_TILE * 128, 128], [128, A_TILE], [1, 128]]),
                  _ap(cos_h, pos0 * 128, [[128, 128], [C * 128, A_TILE], [1, 128]]),
                  writes=(f"cos{pg}",))
            p.dma("sync", _ap(sinb[pg], 0, [[A_TILE * 128, 128], [128, A_TILE], [1, 128]]),
                  _ap(sin_h, pos0 * 128, [[128, 128], [C * 128, A_TILE], [1, 128]]),
                  writes=(f"sin{pg}",))
        x_b = x_t[:, bi * D:(bi + 1) * D]
        cosap = _ap(cosb[pg], bi * 128, [[A_TILE * 128, 128], [0, 4], [1, 128]])
        sinap = _ap(sinb[pg], bi * 128, [[A_TILE * 128, 128], [0, 4], [1, 128]])
        csres = (f"cos{pg}", f"sin{pg}")

        emit_ln(x_b, "x_t", xn_tm[:, :], "xn_tm", lnsq_a[:, :], "lnsq_a")
        emit_transpose8(xn_tm, "xn_tm", xn_fm, "xn_fm", slot=0)

        p.mm_multi([(PA[:, n * 512:(n + 1) * 512],
                     [(xn_fm[:, kc * 128:(kc + 1) * 128],
                       wq_bf[:, kc * D + n * 512: kc * D + (n + 1) * 512])
                      for kc in range(8)]) for n in range(2)],
                   reads=("xn_fm", "wq"), writes=("PA0", "PA1"), tag="Qmm")
        p.mm_multi([(PB[:, n * 512:(n + 1) * 512],
                     [(xn_fm[:, kc * 128:(kc + 1) * 128],
                       wk_bf[:, kc * D + n * 512: kc * D + (n + 1) * 512])
                      for kc in range(8)]) for n in range(2)],
                   reads=("xn_fm", "wk"), writes=("PB0", "PB1"), tag="Kmm")
        for h in range(H):
            sl = slice(h * 256, (h + 1) * 256)
            p.op("scalar", lambda e, s=sl, h_=h: e.activation(
                qkraw[:, s], _ap(PA, s.start, [[1024, 128], [1, 256]]), AF.Identity,
                scale=dqds[:, h_:h_ + 1]),
                reads=("PA0", "PA1", "dqds"), writes=("qkraw",), tag="Qevac")
        emit_rope(qkraw, "qkraw", q_tm, "q_tm", cosap, sinap, csres)

        for pair in range(2):
            p.mm_multi([((P0, P1)[i][:, :],
                         [(xn_fm[:, kc * 128:(kc + 1) * 128],
                           wv_bf[:, kc * 2048 + (pair * 2 + i) * 512:
                                 kc * 2048 + (pair * 2 + i + 1) * 512])
                          for kc in range(8)]) for i in range(2)],
                       reads=("xn_fm", "wv0", "wv1"),
                       writes=("P0b", "P1b"), tag="Vmm")
            for i in range(2):
                s4 = pair * 2 + i
                p.op("scalar", lambda e, b=(P0, P1)[i], s_=s4: e.activation(
                    v_tm[:, s_ * 512:(s_ + 1) * 512], b[:, :], AF.Identity),
                    reads=(("P0b", "P1b")[i],), writes=(f"v.{s4}",), tag="Vevac")
        emit_transpose8(q_tm, ("q_tm.1", "q_tm.2"), q_fm, "q_fm", slot=1)

        p.op("scalar", lambda e: e.activation(qkraw[:, :], PB[:, :], AF.Identity),
             reads=("PB0", "PB1"), writes=("qkraw",), tag="Kevac")
        for pair in range(2):
            p.mm_multi([((P0, P1)[i][:, :],
                         [(xn_fm[:, kc * 128:(kc + 1) * 128],
                           wg_bf[:, kc * 2048 + (pair * 2 + i) * 512:
                                 kc * 2048 + (pair * 2 + i + 1) * 512])
                          for kc in range(8)]) for i in range(2)],
                       reads=("xn_fm", "wg0", "wg1"),
                       writes=("P0b", "P1b"), tag="Gmm")
            for i in range(2):
                s4 = pair * 2 + i
                p.op("scalar", lambda e, b=(P0, P1)[i], s_=s4: e.activation(
                    gate[:, s_ * 512:(s_ + 1) * 512], b[:, :], AF.Silu),
                    reads=(("P0b", "P1b")[i],), writes=("gate",), tag="silu")
        emit_rope(qkraw, "qkraw", k_rope, "k_rope", cosap, sinap, csres)
        for h in range(H):
            sl = slice(h * 256, (h + 1) * 256)
            p.op("scalar", lambda e, s=sl, h_=h: e.activation(
                k_state[:, s], k_rope[:, s], AF.Identity, scale=dqds[:, 4 + h_:5 + h_]),
                reads=("k_rope.1", "k_rope.2", "dqds"), writes=("k_state",), tag="Kst")
        emit_transpose8(k_rope, ("k_rope.1", "k_rope.2"), k_fm, "k_fm", slot=0)

        p.mm_multi([(PA[:, h * 128:(h + 1) * 128],
                     [(k_fm[:, cc * 128:(cc + 1) * 128],
                       q_fm[:, cc * 128:(cc + 1) * 128]) for cc in (2 * h, 2 * h + 1)])
                    for h in range(H)],
                   reads=("k_fm", "q_fm"), writes=("PA0",), tag="ATmm")
        p.op("vector", lambda e: e.tensor_mul(atm[:, :], PA[:, 0:512], maskT[:, :]),
             reads=("PA0", "maskT"), writes=("atm",), tag="mask")

        for h in range(H):
            ob = (P0, P1)[h % 2]
            obres = ("P0b", "P1b")[h % 2]
            pairs = [(atm[:, h * 128:(h + 1) * 128], v_tm[:, h * 512:(h + 1) * 512])]
            pairs += [(q_fm[:, cc * 128:(cc + 1) * 128],
                       S_bf[:, h * 1024 + j * 512: h * 1024 + (j + 1) * 512])
                      for j, cc in enumerate((2 * h, 2 * h + 1))]
            p.mm_group(ob[:, :], pairs,
                       reads=("atm", f"v.{h}", "q_fm", f"S_bf.{h}"),
                       writes=(obres,), tag="Omm")
            p.op("scalar", lambda e, b=ob, h_=h: e.activation(
                o_sb[:, h_ * 512:(h_ + 1) * 512], b[:, :], AF.Identity),
                reads=(obres,), writes=(f"o.{h}",), tag="Oevac")
            p.op("scalar", lambda e, h_=h: e.activation(
                o_sq[:, :], o_sb[:, h_ * 512:(h_ + 1) * 512], AF.Square,
                accum_out=rms4[:, h_:h_ + 1]),
                reads=(f"o.{h}",), writes=("o_sq", f"rms.{h}"), tag="Osq")
            p.mm_multi([(PB[:, j * 512:(j + 1) * 512],
                         [(k_state[:, h * 256 + j * 128: h * 256 + (j + 1) * 128],
                           v_tm[:, h * 512:(h + 1) * 512])]) for j in range(2)],
                       reads=("k_state", f"v.{h}"), writes=("PB0", "PB1"), tag="Smm")
            for j in range(2):
                sl = slice(h * 1024 + j * 512, h * 1024 + (j + 1) * 512)
                ps = PB[:, j * 512:(j + 1) * 512]
                p.op("vector", lambda e, s=sl, ps_=ps, l=float(lamC[h]):
                     e.scalar_tensor_tensor(S[:, s], S[:, s], l, ps_, ALU.mult, ALU.add),
                     reads=(("PB0", "PB1")[j], "S"), writes=("S",), tag="Supd")

        p.op("scalar", lambda e: e.activation(rms4[:, 4:8], rms4[:, 0:4], AF.Ln,
                                              scale=1.0 / DV, bias=stat[:, 9:10]),
             reads=tuple(f"rms.{h}" for h in range(H)) + ("epsc",),
             writes=("rmsl",), tag="rmsln")
        p.op("scalar", lambda e: e.activation(rms4[:, 8:12], rms4[:, 4:8], AF.Exp,
                                              scale=-0.5),
             reads=("rmsl",), writes=("rstd4",), tag="rmsexp")
        for h in range(H):
            sl = slice(h * 512, (h + 1) * 512)
            p.op("vector", lambda e, s=sl, h_=h: e.scalar_tensor_tensor(
                og[:, s], o_sb[:, s], rms4[:, 8 + h_:9 + h_], gate[:, s],
                ALU.mult, ALU.mult),
                reads=(f"o.{h}", "rstd4", "gate"), writes=("og",), tag="og")
        for h in range(H):
            sl = slice(h * 1024, (h + 1) * 1024)
            p.op("scalar", lambda e, s=sl: e.activation(S_bf[:, s], S[:, s], AF.Identity),
                 reads=("S",), writes=(f"S_bf.{h}",), tag="Sbf")
        p.dma("sync", _ap(og_s, blk * C * 2048, [[2048, 128], [1, 2048]]),
              og[:, :], reads=("og",), writes=("og_hbm",), tag="ogst")

    p.barrier()

    ra2 = Region()
    wo_bf = ra2.t("wo_bf", [128, 16 * D], BF16)
    x_t2 = ra2.t("x_t2", [128, A_TILE * D], F32)
    og_in = [ra2.t(f"og_in{i}", [128, H * DV], BF16) for i in range(2)]
    og_fm = ra2.t("og_fm", [128, 2048], BF16)
    x1o = [ra2.t(f"x1o{i}", [128, D], F32) for i in range(2)]

    p.dma("gpsimd", wo_bf[:, :], wo_h[:, :], writes=("wo_raw",))
    def wo_fold(e):
        inst = None
        for cc in range(16):
            inst = e.tensor_scalar(wo_bf[:, cc * D:(cc + 1) * D],
                                   wo_bf[:, cc * D:(cc + 1) * D],
                                   gnr[:, cc:cc + 1], None, ALU.mult)
        return inst
    p.op("vector", wo_fold, reads=("wo_raw", "gnr"), writes=("wo",), tag="wofold")

    for blk in range(_LIMIT["a2"]):
        bt, bi = blk // A_TILE, blk % A_TILE
        pg = blk % 2
        if bi == 0:
            p.dma("sync", _ap(x_t2, 0, [[A_TILE * D, 128], [D, A_TILE], [1, D]]),
                  _ap(x_h, blk * C * D, [[D, 128], [C * D, A_TILE], [1, D]]),
                  writes=("x_t2",), tag="x2load")
        if _LIMIT["a2p"] & 1:
            p.dma("sync", og_in[pg][:, :],
                  _ap(og_s, blk * C * 2048, [[2048, 128], [1, 2048]]),
                  reads=("og_hbm",), writes=(f"ogin{pg}",), tag="ogld")
        for half in range(2 if _LIMIT["a2p"] & 2 else 0):
            def fnT(t, src=og_in[pg], hf=half):
                inst = None
                for cc in range(8):
                    inst = t.transpose(PT[:, cc * 128:(cc + 1) * 128],
                                       src[:, (hf * 8 + cc) * 128:(hf * 8 + cc + 1) * 128],
                                       ident[:, :])
                return inst
            p.op("tensor", fnT, reads=(f"ogin{pg}", "ident"), writes=("PT",), tag="ogT")
            p.op("vector", lambda e, hf=half: e.tensor_copy(
                og_fm[:, hf * 1024:(hf + 1) * 1024], PT[:, :]),
                reads=("PT",), writes=(f"ogfm{half}",), tag="ogTev")
        pp = (PA, PB)[pg]
        ppres = (("PA0", "PA1"), ("PB0", "PB1"))[pg]
        for n in range(2 if _LIMIT["a2p"] & 4 else 0):
            pairs = [(og_fm[:, cc * 128:(cc + 1) * 128],
                      wo_bf[:, cc * D + n * 512: cc * D + (n + 1) * 512])
                     for cc in range(16)]
            p.mm_group(pp[:, n * 512:(n + 1) * 512], pairs,
                       reads=("ogfm0", "ogfm1", "wo"), writes=(ppres[n],), tag="WOmm")
        if _LIMIT["a2p"] & 8:
            p.op("vector", lambda e, pp_=pp, xb=x_t2[:, bi * D:(bi + 1) * D],
                 o=x1o[pg]: e.tensor_add(o[:, :], pp_[:, :], xb),
                 reads=ppres + ("x_t2",), writes=(f"x1o{pg}",), tag="resid")
            p.dma("sync", _ap(x1_s, blk * C * D, [[D, 128], [1, D]]), x1o[pg][:, :],
                  reads=(f"x1o{pg}",), writes=("x1_hbm",), tag="x1st")

    p.barrier()

    rb = Region()
    x1n_fm = rb.t("x1n_fm", [128, 8 * TT], BF16)
    acc = rb.t("acc", [128, TT * 8], F32)
    h_bf = rb.t("h_bf", [128, 32 * TT], BF16)
    wpc = [rb.t(f"wpc{i}", [128, 8192], BF16) for i in range(3)]
    x1b = [rb.t(f"x1b{i}", [128, D], F32) for i in range(2)]
    outb = [rb.t(f"outb{i}", [128, D], F32) for i in range(2)]
    x1n_tm = rb.t("x1n_tm", [128, D], BF16)
    lnsq = rb.t("lnsq", [128, D], F32)
    lg = rb.t("lg", [128, 64], F32)
    r_sm = rb.t("r_sm", [128, 64], F32)
    sm8 = rb.t("sm8", [128, 64], F32)
    comb = rb.t("comb", [128, 64], F32)
    comb9 = rb.t("comb9", [128, 72], BF16)
    cw_sb = rb.t("cw_sb", [9, 1024], BF16)

    wslot = [0]

    def wpiece(e_i, which, pc):
        s = wslot[0] % 3
        wslot[0] += 1
        src = w1_h if which == 1 else w2_h
        off = e_i * 128 * 32768 + (pc * 1024 if which == 1 else pc * 8192)
        ap_in = _ap(src, off, [[32768, 128], [4096 if which == 1 else 1024, 8], [1, 1024]])
        res = f"wpc{s}"
        p.dma("gpsimd", _ap(wpc[s], 0, [[8192, 128], [1024, 8], [1, 1024]]), ap_in,
              writes=(res,), tag=f"w{which}ld")
        return wpc[s], res

    lgP = [lg, rb.t("lgB", [128, 64], F32)]
    rsmP = [r_sm, rb.t("r_smB", [128, 64], F32)]
    sm8P = [sm8, rb.t("sm8B", [128, 64], F32)]
    combP = [comb, rb.t("combB", [128, 64], F32)]
    comb9P = [comb9, rb.t("comb9B", [128, 72], BF16)]
    cwP = [cw_sb, rb.t("cw_sbB", [9, 1024], BF16)]

    def b_prologue(tt):
        for b in range(BPT):
            pg = b % 2
            blk = tt * BPT + b
            p.dma("sync", x1b[pg][:, :], _ap(x1_s, blk * C * D, [[D, 128], [1, D]]),
                  reads=("x1_hbm",), writes=(f"x1b{pg}",), tag="x1ld")
            emit_ln(x1b[pg][:, :], f"x1b{pg}", x1n_tm[:, :], "x1n_tm",
                    lnsq[:, :], "lnsq")
            pt_, ptres = PTS[b % 2]
            def fnT(t, src=x1n_tm, pt2=pt_):
                inst = None
                for cc in range(8):
                    inst = t.transpose(pt2[:, cc * 128:(cc + 1) * 128],
                                       src[:, cc * 128:(cc + 1) * 128], ident[:, :])
                return inst
            p.op("tensor", fnT, reads=("x1n_tm", "ident"), writes=(ptres,), tag="xnT")
            def fnTe(e, b_=b, pt2=pt_):
                inst = None
                for cc in range(8):
                    inst = e.tensor_copy(
                        x1n_fm[:, cc * TT + b_ * 128: cc * TT + (b_ + 1) * 128],
                        pt2[:, cc * 128:(cc + 1) * 128])
                return inst
            p.op("vector", fnTe, reads=(ptres,), writes=(f"xnfm.{b}",), tag="xnTev")
            pairs = [(x1n_fm[:, kc * TT + b * 128: kc * TT + (b + 1) * 128],
                      rwbf[:, kc * 8:(kc + 1) * 8]) for kc in range(8)]
            p.mm_group(P0[:, b * 8:(b + 1) * 8], pairs,
                       reads=(f"xnfm.{b}", "rwbf"), writes=("P0b",), tag="routmm")

    def b_softmax(tt):
        par = tt % 2
        lg_, r_sm_, sm8_, comb_ = lgP[par], rsmP[par], sm8P[par], combP[par]
        comb9_, cw_ = comb9P[par], cwP[par]
        R = lambda n: f"{n}.{par}"
        p.op("vector", lambda e: e.tensor_add(lg_[:, :], P0[:, 0:64], rb_bc[:, :]),
             reads=("P0b", "rbbc"), writes=(R("lg"),), tag="lgevac")
        lg3 = _ap(lg_, 0, [[64, 128], [8, 8], [1, 8]])
        p.op("vector", lambda e: e.tensor_reduce(
            _ap(sm8_, 0, [[64, 128], [1, 8], [1, 1]]), lg3, AX.X, ALU.max, negate=True),
            reads=(R("lg"),), writes=(R("negmax"),), tag="rmax")
        nmb = _ap(sm8_, 0, [[64, 128], [1, 8], [0, 8]])
        r3 = _ap(r_sm_, 0, [[64, 128], [8, 8], [1, 8]])
        p.op("vector", lambda e: e.tensor_tensor(r3, lg3, nmb, ALU.add),
             reads=(R("lg"), R("negmax")), writes=(R("r_sm"),), tag="subm")
        p.op("scalar", lambda e: e.activation(r_sm_[:, :], r_sm_[:, :], AF.Exp),
             reads=(R("r_sm"),), writes=(R("r_sm"),), tag="exp")
        p.op("vector", lambda e: e.tensor_reduce(
            _ap(sm8_, 8, [[64, 128], [1, 8], [1, 1]]), r3, AX.X, ALU.add),
            reads=(R("r_sm"),), writes=(R("ssum"),), tag="rsum")
        p.op("vector", lambda e: e.reciprocal(sm8_[:, 16:24], sm8_[:, 8:16]),
             reads=(R("ssum"),), writes=(R("srec"),))
        srb = _ap(sm8_, 16, [[64, 128], [1, 8], [0, 8]])
        p.op("vector", lambda e: e.tensor_tensor(r3, r3, srb, ALU.mult),
             reads=(R("r_sm"), R("srec")), writes=(R("r_sm"),), tag="rnorm")
        p.op("vector", lambda e: e.tensor_reduce(
            _ap(sm8_, 24, [[64, 128], [1, 8], [1, 1]]), r3, AX.X, ALU.max),
            reads=(R("r_sm"),), writes=(R("m1"),), tag="m1")
        m1b = _ap(sm8_, 24, [[64, 128], [1, 8], [0, 8]])
        c3 = _ap(comb_, 0, [[64, 128], [8, 8], [1, 8]])
        p.op("vector", lambda e: e.tensor_tensor(c3, r3, m1b, ALU.is_ge),
             reads=(R("r_sm"), R("m1")), writes=(R("comb"),), tag="eq1")
        p.op("vector", lambda e: e.scalar_tensor_tensor(c3, c3, -1e9, r3,
                                                        ALU.mult, ALU.add),
             reads=(R("comb"), R("r_sm")), writes=(R("comb"),), tag="r2")
        p.op("vector", lambda e: e.tensor_reduce(
            _ap(sm8_, 32, [[64, 128], [1, 8], [1, 1]]), c3, AX.X, ALU.max),
            reads=(R("comb"),), writes=(R("m2"),), tag="m2")
        m2b = _ap(sm8_, 32, [[64, 128], [1, 8], [0, 8]])
        p.op("vector", lambda e: e.tensor_tensor(c3, r3, m2b, ALU.is_ge),
             reads=(R("r_sm"), R("m2")), writes=(R("comb"),), tag="mask2")
        p.op("vector", lambda e: e.tensor_tensor(c3, c3, r3, ALU.mult),
             reads=(R("comb"), R("r_sm")), writes=(R("comb"),), tag="combu")
        p.op("vector", lambda e: e.tensor_add(sm8_[:, 40:48], sm8_[:, 24:32],
                                              sm8_[:, 32:40]),
             reads=(R("m1"), R("m2")), writes=(R("denom"),))
        p.op("vector", lambda e: e.reciprocal(sm8_[:, 48:56], sm8_[:, 40:48]),
             reads=(R("denom"),), writes=(R("drec"),))
        drb = _ap(sm8_, 48, [[64, 128], [1, 8], [0, 8]])
        p.op("vector", lambda e: e.tensor_tensor(c3, c3, drb, ALU.mult),
             reads=(R("comb"), R("drec")), writes=(R("comb"),), tag="combn")
        p.op("vector", lambda e: e.memset(
            _ap(comb9_, 0, [[72, 128], [9, 8], [1, 1]]), 1.0),
            writes=(R("comb9"),), tag="c9one")
        p.op("vector", lambda e: e.tensor_copy(
            _ap(comb9_, 1, [[72, 128], [9, 8], [1, 8]]), c3),
            reads=(R("comb"),), writes=(R("comb9"),), tag="c9cp")
        def fnc9T(t):
            inst = None
            for b_ in range(8):
                inst = t.transpose(_ap(PT, b_ * 128, [[1024, 9], [1, 128]]),
                                   comb9_[:, b_ * 9:(b_ + 1) * 9], ident[:, :])
            return inst
        p.op("tensor", fnc9T, reads=(R("comb9"), "ident"), writes=("PT",), tag="c9T")
        p.op("vector", lambda e: e.tensor_copy(cw_[0:9, :],
                                               _ap(PT, 0, [[1024, 9], [1, 1024]])),
             reads=("PT",), writes=(R("cw"),), tag="cwev")

    def b_experts(tt):
        par = tt % 2
        comb_ = combP[par]
        n_e = _LIMIT["ne"]
        for e_i in range(n_e):
            for pc in range(4):
                wsb, wres = wpiece(e_i, 1, pc)
                for mb in range(8):
                    mglob = pc * 8 + mb
                    pp = (PA, PB)[mglob % 2]
                    ppres = (("PA0", "PA1"), ("PB0", "PB1"))[mglob % 2]
                    groups = [(pp[:, n * 512:(n + 1) * 512],
                               [(wsb[:, kc * 1024 + mb * 128: kc * 1024 + (mb + 1) * 128],
                                 x1n_fm[:, kc * TT + n * 512: kc * TT + (n + 1) * 512])
                                for kc in range(8)]) for n in range(2)]
                    p.mm_multi(groups,
                               reads=(wres,) + tuple(f"xnfm.{b}" for b in range(8)),
                               writes=ppres, tag="h_mm")
                    p.op("scalar", lambda e, pp_=pp, m=mglob, ei=e_i: e.activation(
                        h_bf[:, m * TT:(m + 1) * TT], pp_[:, :], AF.Gelu_apprx_tanh,
                        bias=b1r[:, ei * 32 + m: ei * 32 + m + 1]),
                        reads=ppres + ("b1r",), writes=(f"h.{mglob}",), tag="gelu")
            if e_i == n_e - 1 and tt + 1 < _LIMIT["bt"]:
                b_prologue(tt + 1)
                b_softmax(tt + 1)
            cw_ = cwP[par]
            for pc in range(4):
                wsb, wres = wpiece(e_i, 2, pc)
                for b in range(BPT):
                    g = b + pc * BPT
                    pp = (PA, PB)[g % 2]
                    ppres = (("PA0", "PA1"), ("PB0", "PB1"))[g % 2]
                    hres = tuple(f"h.{pc * 8 + k}" for k in range(8))
                    rds = (wres,) + hres
                    groups = []
                    for n in range(2):
                        pairs = []
                        if e_i == 0 and pc == 0:
                            pairs.append((cw_[0:9, b * 128:(b + 1) * 128],
                                          b2bf[0:9, n * 512:(n + 1) * 512]))
                        pairs += [(h_bf[:, (pc * 8 + k) * TT + b * 128:
                                        (pc * 8 + k) * TT + (b + 1) * 128],
                                   wsb[:, k * 1024 + n * 512: k * 1024 + (n + 1) * 512])
                                  for k in range(8)]
                        groups.append((pp[:, n * 512:(n + 1) * 512], pairs))
                    if e_i == 0 and pc == 0:
                        rds = rds + (f"cw.{par}", "b2bf")
                    p.mm_multi(groups, reads=rds, writes=ppres, tag="y_mm")
                    if e_i == 0 and pc == 0:
                        p.op("vector", lambda e, pp_=pp, b_=b: e.tensor_copy(
                            acc[:, b_ * D:(b_ + 1) * D], pp_[:, :]),
                            reads=ppres, writes=(f"acc.{b}",), tag="acc0")
                    elif e_i == 0:
                        p.op("vector", lambda e, pp_=pp, b_=b: e.tensor_add(
                            acc[:, b_ * D:(b_ + 1) * D], acc[:, b_ * D:(b_ + 1) * D],
                            pp_[:, :]),
                            reads=ppres + (f"acc.{b}",), writes=(f"acc.{b}",), tag="accs")
                    else:
                        p.op("vector", lambda e, pp_=pp, b_=b, ei=e_i:
                             e.scalar_tensor_tensor(
                                 acc[:, b_ * D:(b_ + 1) * D], pp_[:, :],
                                 comb_[:, b_ * 8 + ei - 1: b_ * 8 + ei],
                                 acc[:, b_ * D:(b_ + 1) * D], ALU.mult, ALU.add),
                             reads=ppres + (f"acc.{b}", f"comb.{par}"),
                             writes=(f"acc.{b}",), tag="acce")

    def b_final(tt):
        for b in range(BPT):
            pg = b % 2
            blk = tt * BPT + b
            p.dma("sync", x1b[pg][:, :], _ap(x1_s, blk * C * D, [[D, 128], [1, D]]),
                  reads=("x1_hbm",), writes=(f"x1b{pg}",), tag="x1ld2")
            p.op("vector", lambda e, b_=b, pg_=pg: e.tensor_add(
                outb[pg_][:, :], acc[:, b_ * D:(b_ + 1) * D], x1b[pg_][:, :]),
                reads=(f"acc.{b}", f"x1b{pg}"), writes=(f"outb{pg}",), tag="out")
            p.dma("sync", _ap(out_h, blk * C * D, [[D, 128], [1, D]]), outb[pg][:, :],
                  reads=(f"outb{pg}",), writes=("out_hbm",), tag="outst")

    if _LIMIT["bt"] > 0:
        b_prologue(0)
        b_softmax(0)
    for tt in range(_LIMIT["bt"]):
        b_experts(tt)
        b_final(tt)

    p.barrier()
    n = p.emit()
    _cache[key] = nc
    _cache["n_ops"] = len(p.ops)
    return nc


def kernel(**inputs):
    cst = _consts()
    gn = np.asarray(inputs["L0_attn_g_norm_w"], np.float32)
    gn_bc = np.tile(gn, H)
    gnorm_r = np.ascontiguousarray(gn_bc.reshape(16, 128).T)
    rb = np.asarray(inputs["L0_ffn_router_b"], np.float32)
    rb_bc = np.ascontiguousarray(np.broadcast_to(np.tile(rb, 8)[None, :], (128, 64)))
    b1 = np.concatenate([np.asarray(inputs["L0_ffn_shared_b1"], np.float32)[None],
                         np.asarray(inputs["L0_ffn_spec_b1"], np.float32)], 0)
    b1_r = np.ascontiguousarray(
        b1.reshape(NEXP, 32, 128).transpose(2, 0, 1).reshape(128, NEXP * 32))
    b2 = np.concatenate([np.asarray(inputs["L0_ffn_shared_b2"], np.float32)[None],
                         np.asarray(inputs["L0_ffn_spec_b2"], np.float32)], 0)
    w1 = np.concatenate([np.asarray(inputs["L0_ffn_shared_w1"], np.float32)[None],
                         np.asarray(inputs["L0_ffn_spec_w1"], np.float32)], 0)
    w2 = np.concatenate([np.asarray(inputs["L0_ffn_shared_w2"], np.float32)[None],
                         np.asarray(inputs["L0_ffn_spec_w2"], np.float32)], 0)
    w1_all = np.stack([_wt_relayout(w1[i]) for i in range(NEXP)], 0)
    w2_all = np.stack([_wt_relayout(w2[i]) for i in range(NEXP)], 0)

    shared = dict(
        wq_t=_wt_relayout(np.asarray(inputs["L0_attn_Wq"], np.float32)),
        wk_t=_wt_relayout(np.asarray(inputs["L0_attn_Wk"], np.float32)),
        wv_t=_wt_relayout(np.asarray(inputs["L0_attn_Wv"], np.float32)),
        wg_t=_wt_relayout(np.asarray(inputs["L0_attn_Wg"], np.float32)),
        wo_t=_wt_relayout(np.asarray(inputs["L0_attn_Wo"], np.float32)),
        w1_all=w1_all, w2_all=w2_all, b1_r=b1_r, b2_all=np.ascontiguousarray(b2),
        rw_t=_wt_relayout(np.asarray(inputs["L0_ffn_router_w"], np.float32)),
        rb_bc=rb_bc, gnorm_r=gnorm_r, ident=cst["ident"],
        cos_tm=cst["cos_tm"], sin_tm=cst["sin_tm"], dq_ds=cst["dq_ds"],
        maskT=cst["maskT"],
    )
    x = np.asarray(inputs["L0_attn_x"], np.float32)
    nc = build_program()
    in_maps = []
    for c in range(NCORES):
        m = dict(shared)
        m["x"] = np.ascontiguousarray(x[NSEQ * c:NSEQ * (c + 1)].reshape(TOK, D))
        in_maps.append(m)
    res = run_bass_kernel_spmd(nc, in_maps, core_ids=list(range(NCORES)))
    out = np.empty((16, T, D), np.float32)
    for c in range(NCORES):
        out[NSEQ * c:NSEQ * (c + 1)] = res.results[c]["out"].reshape(NSEQ, T, D)
    return out


# version 10
# speedup vs baseline: 1.4015x; 1.3972x over previous
import numpy as np
import ml_dtypes
import concourse.bass as bass
import concourse.mybir as mybir
from concourse.bass import IndirectOffsetOnAxis
from concourse.bass_utils import run_bass_kernel_spmd

F32 = mybir.dt.float32
BF16 = mybir.dt.bfloat16
U32 = mybir.dt.uint32
I32 = mybir.dt.int32
AF = mybir.ActivationFunctionType
ALU = mybir.AluOpType
AX = mybir.AxisListType
BF = ml_dtypes.bfloat16

NCORES = 8
T = 4096
D = 1024
H, DK, DV = 4, 256, 512
C = 128
NSEQ = 2
TOK = NSEQ * T
NBLK = TOK // C
BLK_PER_SEQ = T // C
A_TILE = 2
D_FF = 4096
NEXP = 9
TT = 1024
NBT = TOK // TT
BPT = TT // C
EPS = 1e-5
NSLOT = 8
PAGES = 40
PSLOT = 512
NSPEC = 8

_cache = {}
_LIMIT = {"a1": 64, "a2": 64, "bt": 8, "ne": 1, "a2p": 15, "pg": PAGES,
          "b3": 64, "w": 1}


def _consts():
    if "c" in _cache:
        return _cache["c"]
    ld = np.log1p(-np.exp2(-5.0 - np.arange(H, dtype=np.float64)))
    i = np.arange(C, dtype=np.float64)
    decay_q = np.exp(ld[None, :] * (i[:, None] + 1)) / 16.0
    decay_state = np.exp(ld[None, :] * (C - 1 - i[:, None]))
    lamC = np.exp(ld * C).astype(np.float32)
    maskT = np.zeros((C, H * C), dtype=np.float64)
    for h in range(H):
        maskT[:, h * C:(h + 1) * C] = (np.exp(ld[h] * -(i[:, None] + 1.0))
                                       * (i[None, :] >= i[:, None]))
    inv = 1.0 / 10000.0 ** (np.arange(0, DK, 2, dtype=np.float64) / DK)
    t = np.arange(T, dtype=np.float64)
    f = t[:, None] * inv[None, :]
    q64 = np.arange(64)
    c = dict(
        dq_ds=np.concatenate([decay_q, decay_state], 1).astype(np.float32),
        lamC=lamC,
        maskT=maskT.astype(np.float32),
        cos_tm=np.cos(f).astype(np.float32),
        sin_tm=np.sin(f).astype(np.float32),
        ident=np.eye(128, dtype=np.float32).astype(BF),
        lstrict=(np.arange(128)[:, None] < np.arange(128)[None, :]).astype(BF),
        ones128c=np.ones((128, 1), np.float32).astype(BF),
        one1=np.ones((1, 1), np.float32).astype(BF),
        blockL64=(((q64[:, None] % 8) == (q64[None, :] % 8))
                  & ((q64[:, None] // 8) < (q64[None, :] // 8))).astype(BF),
        blockF64=((q64[:, None] % 8) == (q64[None, :] % 8)).astype(BF),
        ident64=np.eye(64, dtype=np.float32),
        ones1x128f=np.ones((1, 128), np.float32),
        ones1x128b=np.ones((1, 128), np.float32).astype(BF),
        iota128=np.arange(128, dtype=np.float32).reshape(128, 1),
        piota48=np.arange(48, dtype=np.float32).reshape(1, 48),
    )
    _cache["c"] = c
    return c


def _wt_relayout(w):
    dout, din = w.shape
    wt = np.asarray(w, np.float32).T
    return np.ascontiguousarray(
        wt.reshape(din // 128, 128, dout).transpose(1, 0, 2).reshape(128, -1))


class _Op:
    __slots__ = ("eng", "fn", "deps", "idx", "is_dma", "queue", "slot",
                 "event", "tag", "force_wait", "no_event")

    def __init__(self, eng, fn, tag=""):
        self.eng = eng
        self.fn = fn
        self.deps = set()
        self.idx = -1
        self.is_dma = False
        self.queue = None
        self.slot = None
        self.event = None
        self.tag = tag
        self.force_wait = False
        self.no_event = False


class P:
    ENGINES = ("tensor", "vector", "scalar", "gpsimd", "sync")

    def __init__(self, nc):
        self.nc = nc
        self.ops = []
        self.writer = {}
        self.readers = {}
        self.last_on = {}
        self.dmaq = {"sync": 0, "gpsimd": 0}

    @staticmethod
    def _flat(rs):
        out = []
        for r in rs:
            if isinstance(r, str):
                out.append(r)
            else:
                out.extend(r)
        return out

    def op(self, eng, fn, reads=(), writes=(), tag="", extra_deps=()):
        reads = self._flat(reads)
        writes = self._flat(writes)
        o = _Op(eng, fn, tag)
        deps = set(extra_deps)
        for r in reads:
            w = self.writer.get(r)
            if w is not None:
                deps.add(w)
        for r in writes:
            w = self.writer.get(r)
            if w is not None:
                deps.add(w)
            deps.update(self.readers.get(r, ()))
        deps.discard(o)
        o.deps = deps
        for r in reads:
            self.readers.setdefault(r, set()).add(o)
        for r in writes:
            self.writer[r] = o
            self.readers[r] = set()
        o.idx = len(self.ops)
        self.ops.append(o)
        self.last_on[eng] = o
        return o

    def dma(self, queue, out_ap, in_ap, reads=(), writes=(), tag=""):
        eng = "gpsimd" if queue == "gpsimd" else "sync"
        slot = self.dmaq[queue] % NSLOT
        self.dmaq[queue] += 1
        slotres = f"@dma.{queue}.{slot}"
        o = self.op(eng, lambda e, oa=out_ap, ia=in_ap: e.dma_start(oa, ia),
                    reads=tuple(reads), writes=tuple(writes) + (slotres,), tag=tag)
        o.is_dma = True
        o.queue = queue
        o.slot = slot
        return o

    def idma(self, out_ap, out_idx, in_ap, in_idx, *, elem_off=0, bc=None,
             cce=None, reads=(), writes=(), tag=""):
        slot = self.dmaq["gpsimd"] % NSLOT
        self.dmaq["gpsimd"] += 1
        slotres = f"@dma.gpsimd.{slot}"
        oo = (IndirectOffsetOnAxis(ap=out_idx[0], axis=out_idx[1])
              if out_idx is not None else None)
        io = (IndirectOffsetOnAxis(ap=in_idx[0], axis=in_idx[1])
              if in_idx is not None else None)
        kw = {}
        if bc is not None:
            kw = dict(bounds_check=bc, oob_is_err=False)
        if cce is not None:
            kw["compute_op"] = cce
        def fn(e, oa=out_ap, ia=in_ap, oo_=oo, io_=io, eo=elem_off, kw_=kw):
            return e.indirect_dma_start(oa, oo_, ia, io_, element_offset=eo, **kw_)
        o = self.op("gpsimd", fn, reads=tuple(reads),
                    writes=tuple(writes) + (slotres,), tag=tag)
        o.is_dma = True
        o.queue = "gpsimd"
        o.slot = slot
        return o

    def mm_group(self, psum_ap, pairs, reads=(), writes=(), tag=""):
        return self.mm_multi([(psum_ap, pairs)], reads=reads, writes=writes, tag=tag)

    def mm_multi(self, groups, reads=(), writes=(), tag=""):
        gg = tuple((ps, tuple(pairs)) for ps, pairs in groups)

        def fn(t, gg=gg):
            inst = None
            for ps, prs in gg:
                n = len(prs)
                for k, (l, r) in enumerate(prs):
                    inst = t.matmul(ps, l, r, start=(k == 0), stop=(k == n - 1))
            return inst
        return self.op("tensor", fn, reads=reads, writes=writes, tag=tag)

    def barrier(self):
        s1 = []
        for e in self.ENGINES:
            dep = self.last_on.get(e)
            o = self.op(e, lambda x: x.nop(),
                        extra_deps=(dep,) if dep is not None else ())
            o.force_wait = True
            s1.append(o)
        deps = set(s1)
        for w in self.writer.values():
            deps.add(w)
        for rl in self.readers.values():
            deps.update(rl)
        b = self.op("sync", lambda x: x.nop(), extra_deps=deps, tag="barrier")
        for r in list(self.writer):
            self.writer[r] = b
            self.readers[r] = set()
        return b

    def emit(self):
        nc = self.nc
        needed = set()
        for o in self.ops:
            for d in o.deps:
                if (d.eng == "tensor" and o.eng == "tensor" and not d.is_dma
                        and not o.force_wait):
                    continue
                if d.no_event:
                    continue
                needed.add(d)
        eng_sem = {e: nc.alloc_semaphore(name=f"es_{e}") for e in self.ENGINES}
        dma_sem = {q: [nc.alloc_semaphore(name=f"ds_{q}{s}") for s in range(NSLOT)]
                   for q in self.dmaq}
        eng_cnt = {e: 0 for e in self.ENGINES}
        slot_cnt = {q: [0] * NSLOT for q in self.dmaq}
        for o in self.ops:
            if o.is_dma:
                slot_cnt[o.queue][o.slot] += 1
                o.event = (dma_sem[o.queue][o.slot], 16 * slot_cnt[o.queue][o.slot])
            elif o in needed:
                eng_cnt[o.eng] += 1
                o.event = (eng_sem[o.eng], eng_cnt[o.eng])
        streams = {e: [o for o in self.ops if o.eng == e] for e in self.ENGINES}
        waited = {e: {} for e in self.ENGINES}

        def emit_stream(e, eng_handle):
            w = waited[e]
            for o in streams[e]:
                req = {}
                for d in o.deps:
                    if (d.eng == "tensor" and e == "tensor" and not d.is_dma
                            and not o.force_wait):
                        continue
                    if d.no_event:
                        continue
                    sem, val = d.event
                    if req.get(sem, 0) < val:
                        req[sem] = val
                for sem, val in req.items():
                    if w.get(sem, 0) < val:
                        eng_handle.wait_ge(sem, val)
                        w[sem] = val
                inst = o.fn(eng_handle)
                try:
                    _cache.setdefault("instmap", {})[inst.ins.name] = o
                except Exception:
                    pass
                if o.event is not None:
                    sem, _ = o.event
                    inst.then_inc(sem, 16 if o.is_dma else 1)

        with nc.Block() as block:
            block.tensor(lambda t: emit_stream("tensor", t))
            block.vector(lambda v: emit_stream("vector", v))
            block.scalar(lambda s: emit_stream("scalar", s))
            block.gpsimd(lambda g: emit_stream("gpsimd", g))
            block.sync(lambda sy: emit_stream("sync", sy))
        n_waits = sum(len(s) for s in streams.values())
        return n_waits


def _ap(h, off, dims):
    return bass.AP(h, off, [list(d) for d in dims])


def build_program():
    key = ("nc",) + tuple(sorted(_LIMIT.items()))
    if key in _cache:
        return _cache[key]
    nc = bass.Bass("TRN2", target_bir_lowering=False)
    dt_in = {}

    def din(name, shape, dt=F32):
        h = nc.dram_tensor(name, list(shape), dt, kind="ExternalInput")
        dt_in[name] = h
        return h

    x_h = din("x", [TOK, D])
    wq_h = din("wq_t", [128, 8 * D])
    wk_h = din("wk_t", [128, 8 * D])
    wv_h = din("wv_t", [128, 8 * 2048])
    wg_h = din("wg_t", [128, 8 * 2048])
    wo_h = din("wo_t", [128, 16 * D])
    w1_h = din("w1_all", [NEXP, 128, 8 * D_FF])
    w2_h = din("w2_all", [NEXP, 128, 32 * D])
    b1_h = din("b1_r", [128, NEXP * 32])
    b2_h = din("b2_all", [NEXP, D])
    rw_h = din("rw_t", [128, 64])
    rb_h = din("rb_bc", [128, 64])
    gn_h = din("gnorm_r", [128, 16])
    id_h = din("ident", [128, 128], BF16)
    cos_h = din("cos_tm", [T, 128])
    sin_h = din("sin_tm", [T, 128])
    dqds_h = din("dq_ds", [128, 8])
    mask_h = din("maskT", [128, H * C])

    b1s_h = din("b1s", [NSPEC * 128, 32])
    ls_h = din("lstrict", [128, 128], BF16)
    o128c_h = din("ones128c", [128, 1], BF16)
    one1_h = din("one1", [1, 1], BF16)
    bl64_h = din("blockL64", [64, 64], BF16)
    bf64_h = din("blockF64", [64, 64], BF16)
    id64_h = din("ident64", [64, 64])
    o1x128f_h = din("ones1x128f", [1, 128])
    o1x128b_h = din("ones1x128b", [1, 128], BF16)
    iota_h = din("iota128h", [128, 1])
    pio_h = din("piota48h", [1, 48])

    og_s = nc.dram_tensor("og_s", [TOK, H * DV], BF16, kind="Internal")
    x1_s = nc.dram_tensor("x1_s", [TOK, D], F32, kind="Internal")
    x1n_s = nc.dram_tensor("x1n_s", [TOK, D], BF16, kind="Internal")
    meta0_s = nc.dram_tensor("meta0_s", [TOK, 20], F32, kind="Internal")
    meta_s = nc.dram_tensor("meta_s", [PAGES * PSLOT, 2], F32, kind="Internal")
    w1s_bf = nc.dram_tensor("w1s_bf", [NSPEC * 128, 32768], BF16, kind="Internal")
    w2s_bf = nc.dram_tensor("w2s_bf", [NSPEC * 128, 32768], BF16, kind="Internal")
    ws1_bf = nc.dram_tensor("ws1_bf", [128, 32768], BF16, kind="Internal")
    ws2_bf = nc.dram_tensor("ws2_bf", [128, 32768], BF16, kind="Internal")
    cnt_s = nc.dram_tensor("cnt_s", [64], F32, kind="Internal")
    out_h = nc.dram_tensor("out", [TOK, D], F32, kind="ExternalOutput")

    lamC = _consts()["lamC"]

    p = P(nc)

    ident = nc.alloc_sbuf_tensor("identsb", [128, 128], BF16)
    dqds = nc.alloc_sbuf_tensor("dqds", [128, 8], F32)
    maskT = nc.alloc_sbuf_tensor("maskTsb", [128, H * C], F32)
    rb_bc = nc.alloc_sbuf_tensor("rbbc", [128, 64], F32)
    b1r = nc.alloc_sbuf_tensor("b1r", [128, NEXP * 32], F32)
    b2bf = nc.alloc_sbuf_tensor("b2bf", [NEXP, D], BF16)
    rwbf = nc.alloc_sbuf_tensor("rwbf", [128, 64], BF16)
    gnr = nc.alloc_sbuf_tensor("gnr", [128, 16], F32)
    cosb = [nc.alloc_sbuf_tensor(f"cosb{i}", [128, A_TILE * 128], F32) for i in range(2)]
    sinb = [nc.alloc_sbuf_tensor(f"sinb{i}", [128, A_TILE * 128], F32) for i in range(2)]
    stat = nc.alloc_sbuf_tensor("stat", [128, 32], F32)
    rms4 = nc.alloc_sbuf_tensor("rms4", [128, 16], F32)
    lstr = nc.alloc_sbuf_tensor("lstr", [128, 128], BF16)
    o128c = nc.alloc_sbuf_tensor("o128c", [128, 1], BF16)
    one1 = nc.alloc_sbuf_tensor("one1sb", [1, 1], BF16)
    bl64 = nc.alloc_sbuf_tensor("bl64", [64, 64], BF16)
    bf64 = nc.alloc_sbuf_tensor("bf64", [64, 64], BF16)
    id64 = nc.alloc_sbuf_tensor("id64", [64, 64], F32)
    o1x128f = nc.alloc_sbuf_tensor("o1x128f", [1, 128], F32)
    o1x128b = nc.alloc_sbuf_tensor("o1x128b", [1, 128], BF16)
    iota128 = nc.alloc_sbuf_tensor("iota128", [128, 1], F32)
    pio48 = nc.alloc_sbuf_tensor("pio48", [1, 48], F32)
    running64 = nc.alloc_sbuf_tensor("running64", [64, 1], F32)
    pbB = nc.alloc_sbuf_tensor("pbB", [128, 8], F32)
    wIdxu = nc.alloc_sbuf_tensor("wIdxu", [128, 48], U32)
    rowscr = nc.alloc_sbuf_tensor("rowscr", [1, 192], F32)
    rowbf = nc.alloc_sbuf_tensor("rowbf", [1, 64], BF16)
    ipg = nc.alloc_sbuf_tensor("ipg", [1, 8], I32)
    wcb = [nc.alloc_sbuf_tensor(f"wcb{i}", [128, 2048], BF16) for i in range(2)]
    fill9 = nc.alloc_sbuf_tensor("fill9", [128, 8], F32)
    rBC = nc.gpsimd.alloc_register("rBC")
    rBCW = nc.gpsimd.alloc_register("rBCW")

    PA = nc.alloc_psum_tensor("PA", [128, 1024], F32)
    PB = nc.alloc_psum_tensor("PB", [128, 1024], F32)
    P0 = nc.alloc_psum_tensor("P0", [128, 512], F32)
    P1 = nc.alloc_psum_tensor("P1", [128, 512], F32)
    PT = nc.alloc_psum_tensor("PTb", [128, 1024], BF16)
    PT2 = nc.alloc_psum_tensor("PTb2", [128, 1024], BF16)
    PTS = [(PT, "PT"), (PT2, "PT2")]

    region_base = nc.sbuf_base
    region_top = nc.sbuf_top

    class Region:
        def __init__(self):
            self.cur = region_base

        def t(self, name, shape, dt):
            per_part = int(np.prod(shape[1:])) * mybir.dt.size(dt)
            off = (self.cur + 31) // 32 * 32
            h = nc.alloc_sbuf_tensor_at(name, list(shape), dt, offset=off)
            self.cur = off + per_part
            assert self.cur <= region_top, (name, self.cur, region_top)
            return h

    p.dma("sync", ident[:, :], id_h[:, :], writes=("ident",))
    p.dma("sync", dqds[:, :], dqds_h[:, :], writes=("dqds",))
    p.dma("sync", maskT[:, :], mask_h[:, :], writes=("maskT",))
    p.dma("sync", rb_bc[:, :], rb_h[:, :], writes=("rbbc",))
    p.dma("sync", b1r[:, :], b1_h[:, :], writes=("b1r",))
    p.dma("gpsimd", b2bf[:, :], b2_h[:, :], writes=("b2bf",))
    p.dma("gpsimd", rwbf[:, :], rw_h[:, :], writes=("rwbf",))
    p.dma("sync", gnr[:, :], gn_h[:, :], writes=("gnr",))
    p.op("gpsimd", lambda e: e.memset(stat[:, 9:10], EPS), writes=("epsc",))
    p.dma("sync", lstr[:, :], ls_h[:, :], writes=("lstr",))
    p.dma("sync", o128c[:, :], o128c_h[:, :], writes=("o128c",))
    p.dma("sync", one1[:, :], one1_h[:, :], writes=("one1",))
    p.dma("sync", bl64[:, :], bl64_h[:, :], writes=("bl64",))
    p.dma("sync", bf64[:, :], bf64_h[:, :], writes=("bf64",))
    p.dma("sync", id64[:, :], id64_h[:, :], writes=("id64",))
    p.dma("sync", o1x128f[:, :], o1x128f_h[:, :], writes=("o1x128f",))
    p.dma("sync", o1x128b[:, :], o1x128b_h[:, :], writes=("o1x128b",))
    p.dma("sync", iota128[:, :], iota_h[:, :], writes=("iota128",))
    p.dma("sync", pio48[:, :], pio_h[:, :], writes=("pio48",))
    p.op("gpsimd", lambda e: e.reg_mov(rBC, TOK - 1),
         writes=("bcreg",)).no_event = True
    p.op("gpsimd", lambda e: e.reg_mov(rBCW, NSPEC * 128 - 1),
         writes=("bcreg2",)).no_event = True
    p.op("gpsimd", lambda e: e.memset(running64[:, :], 0.0), writes=("running64",))
    p.op("gpsimd", lambda e: e.memset(fill9[:, :], 1.0e9), writes=("fill9",))
    p.dma("sync", _ap(meta_s, 0, [[8, 128], [1024, PAGES], [1, 8]]),
          _ap(fill9, 0, [[8, 128], [0, PAGES], [1, 8]]),
          reads=("fill9",), writes=("meta_hbm",), tag="metafill")

    ra = Region()
    x_t = ra.t("x_t", [128, A_TILE * D], F32)
    xn_tm = ra.t("xn_tm", [128, D], BF16)
    xn_fm = ra.t("xn_fm", [128, D], BF16)
    qkraw = ra.t("qkraw", [128, D], F32)
    rt1 = ra.t("rt1", [128, 512], F32)
    rt2 = ra.t("rt2", [128, 512], F32)
    q_tm = ra.t("q_tm", [128, D], BF16)
    q_fm = ra.t("q_fm", [128, D], BF16)
    k_rope = ra.t("k_rope", [128, D], BF16)
    k_state = ra.t("k_state", [128, D], BF16)
    k_fm = ra.t("k_fm", [128, D], BF16)
    v_tm = ra.t("v_tm", [128, H * DV], BF16)
    atm = ra.t("atm", [128, H * C], BF16)
    gate = ra.t("gate", [128, H * DV], F32)
    o_sb = ra.t("o_sb", [128, H * DV], F32)
    o_sq = ra.t("o_sq", [128, DV], F32)
    og = ra.t("og", [128, H * DV], BF16)
    S = ra.t("S", [128, H * 2 * DV], F32)
    S_bf = ra.t("S_bf", [128, H * 2 * DV], BF16)
    lnsq_a = ra.t("lnsq_a", [128, D], F32)
    wq_bf = ra.t("wq_bf", [128, 8 * D], BF16)
    wk_bf = ra.t("wk_bf", [128, 8 * D], BF16)
    wv_bf = ra.t("wv_bf", [128, 8 * 2048], BF16)
    wg_bf = ra.t("wg_bf", [128, 8 * 2048], BF16)

    def emit_wcast(j):
        if not _LIMIT["w"] or j >= 288:
            return
        k, qr = j // 4, j % 4
        ei, rest = k // 8, k % 8
        which, pc = rest // 4, rest % 4
        src = (w1_h, w2_h)[which]
        if which == 0:
            ap_in = _ap(src, ei * 128 * 32768 + qr * 2 * 4096 + pc * 1024,
                        [[32768, 128], [4096, 2], [1, 1024]])
        else:
            ap_in = _ap(src, ei * 128 * 32768 + pc * 8192 + qr * 2048,
                        [[32768, 128], [1, 2048]])
        if ei == 0:
            dst = (ws1_bf, ws2_bf)[which]
            doff = pc * 8192 + qr * 2048
            res = f"wsbf.{which}.{pc}.{qr}"
        else:
            dst = (w1s_bf, w2s_bf)[which]
            doff = (ei - 1) * 128 * 32768 + pc * 8192 + qr * 2048
            res = f"wspec.{which}.{pc}.{ei}.{qr}"
        s = j % 2
        p.dma("gpsimd", _ap(wcb[s], 0, [[2048, 128], [1024, 2], [1, 1024]])
              if which == 0 else wcb[s][:, :], ap_in,
              writes=(f"wcb{s}",), tag="wcL")
        p.dma("sync", _ap(dst, doff, [[32768, 128], [1, 2048]]), wcb[s][:, :],
              reads=(f"wcb{s}",), writes=(res,), tag="wcS")

    p.dma("gpsimd", wq_bf[:, :], wq_h[:, :], writes=("wq",))
    p.dma("gpsimd", wk_bf[:, :], wk_h[:, :], writes=("wk",))
    for hlf in range(2):
        p.dma("gpsimd", wv_bf[:, hlf * 8192:(hlf + 1) * 8192],
              wv_h[:, hlf * 8192:(hlf + 1) * 8192], writes=(f"wv{hlf}",))
        p.dma("gpsimd", wg_bf[:, hlf * 8192:(hlf + 1) * 8192],
              wg_h[:, hlf * 8192:(hlf + 1) * 8192], writes=(f"wg{hlf}",))

    def emit_ln(x_ap, xres, out_ap, out_res, sq_scratch_ap, sq_scr_res):
        p.op("scalar", lambda e, i=x_ap, o=sq_scratch_ap: e.activation(
            o, i, AF.Square, accum_out=stat[:, 0:1]),
            reads=(xres,), writes=(sq_scr_res, "st0"), tag="ln_sq")
        p.op("vector", lambda e, i=x_ap: e.reduce_sum(
            stat[:, 1:2], i, AX.X, negate=True),
            reads=(xres,), writes=("st1",), tag="ln_sum")
        p.op("vector", lambda e: e.tensor_scalar(
            stat[:, 2:3], stat[:, 1:2], 1.0 / D, None, ALU.mult),
            reads=("st1",), writes=("st2",))
        p.op("vector", lambda e: e.tensor_mul(stat[:, 3:4], stat[:, 2:3], stat[:, 2:3]),
             reads=("st2",), writes=("st3",))
        p.op("vector", lambda e: e.tensor_scalar(
            stat[:, 4:5], stat[:, 0:1], 1.0 / D, None, ALU.mult),
            reads=("st0",), writes=("st4",))
        p.op("vector", lambda e: e.tensor_sub(stat[:, 5:6], stat[:, 4:5], stat[:, 3:4]),
             reads=("st3", "st4"), writes=("st5",))
        p.op("scalar", lambda e: e.activation(stat[:, 6:7], stat[:, 5:6], AF.Ln,
                                              bias=stat[:, 9:10]),
             reads=("st5", "epsc"), writes=("st6",))
        p.op("scalar", lambda e: e.activation(stat[:, 7:8], stat[:, 6:7], AF.Exp, scale=-0.5),
             reads=("st6",), writes=("st7",))
        p.op("vector", lambda e: e.tensor_mul(stat[:, 8:9], stat[:, 2:3], stat[:, 7:8]),
             reads=("st2", "st7"), writes=("st8",))
        p.op("scalar", lambda e, i=x_ap, o=out_ap: e.activation(
            o, i, AF.Identity, bias=stat[:, 8:9], scale=stat[:, 7:8]),
            reads=(xres, "st7", "st8"), writes=(out_res,), tag="ln_out")

    def emit_transpose8(src, src_res, dst, dst_res, nchunk=8, slot=0):
        rr = (src_res,) if isinstance(src_res, str) else tuple(src_res)
        pt, ptres = PTS[slot % 2]
        def fn(t, s=src, n=nchunk, pt_=pt):
            inst = None
            for cc in range(n):
                inst = t.transpose(pt_[:, cc * 128:(cc + 1) * 128],
                                   s[:, cc * 128:(cc + 1) * 128], ident[:, :])
            return inst
        p.op("tensor", fn, reads=rr + ("ident",), writes=(ptres,), tag="T8")
        p.op("vector", lambda e, d=dst, n=nchunk, pt_=pt: e.tensor_copy(
            d[:, :n * 128], pt_[:, :n * 128]),
            reads=(ptres,), writes=(dst_res,), tag="T8evac")

    def emit_rope(src_f32, src_res, dst_bf, dst_res, cosap, sinap, cs_res):
        x1 = _ap(src_f32, 0, [[1024, 128], [256, 4], [1, 128]])
        x2 = _ap(src_f32, 128, [[1024, 128], [256, 4], [1, 128]])
        o1 = _ap(dst_bf, 0, [[1024, 128], [256, 4], [1, 128]])
        o2 = _ap(dst_bf, 128, [[1024, 128], [256, 4], [1, 128]])
        t1 = _ap(rt1, 0, [[512, 128], [128, 4], [1, 128]])
        t2 = _ap(rt2, 0, [[512, 128], [128, 4], [1, 128]])
        p.op("vector", lambda e, a=x1, b=cosap, o=t1: e.tensor_mul(o, a, b),
             reads=(src_res,) + tuple(cs_res), writes=("rt1",))
        p.op("vector", lambda e, a=x2, b=sinap, o=t2: e.tensor_mul(o, a, b),
             reads=(src_res,) + tuple(cs_res), writes=("rt2",))
        p.op("vector", lambda e, o=o1: e.tensor_sub(o, t1, t2),
             reads=("rt1", "rt2"), writes=(dst_res + ".1",))
        p.op("vector", lambda e, a=x1, b=sinap, o=t1: e.tensor_mul(o, a, b),
             reads=(src_res,) + tuple(cs_res), writes=("rt1",))
        p.op("vector", lambda e, a=x2, b=cosap, o=t2: e.tensor_mul(o, a, b),
             reads=(src_res,) + tuple(cs_res), writes=("rt2",))
        p.op("vector", lambda e, o=o2: e.tensor_add(o, t1, t2),
             reads=("rt1", "rt2"), writes=(dst_res + ".2",))

    for blk in range(_LIMIT["a1"]):
        if blk < 64:
            emit_wcast(blk)
        bt = blk // A_TILE
        bi = blk % A_TILE
        seq_blk = blk % BLK_PER_SEQ
        pg = bt % 2
        if blk % BLK_PER_SEQ == 0:
            p.op("gpsimd", lambda e: e.memset(S[:, :], 0.0), writes=("S",))
            p.op("gpsimd", lambda e: e.memset(S_bf[:, :], 0.0),
                 writes=tuple(f"S_bf.{h}" for h in range(H)))
        if bi == 0:
            p.dma("sync", _ap(x_t, 0, [[A_TILE * D, 128], [D, A_TILE], [1, D]]),
                  _ap(x_h, blk * C * D, [[D, 128], [C * D, A_TILE], [1, D]]),
                  writes=("x_t",), tag=f"xload{bt}")
            pos0 = (seq_blk) * C
            p.dma("sync", _ap(cosb[pg], 0, [[A_TILE * 128, 128], [128, A_TILE], [1, 128]]),
                  _ap(cos_h, pos0 * 128, [[128, 128], [C * 128, A_TILE], [1, 128]]),
                  writes=(f"cos{pg}",))
            p.dma("sync", _ap(sinb[pg], 0, [[A_TILE * 128, 128], [128, A_TILE], [1, 128]]),
                  _ap(sin_h, pos0 * 128, [[128, 128], [C * 128, A_TILE], [1, 128]]),
                  writes=(f"sin{pg}",))
        x_b = x_t[:, bi * D:(bi + 1) * D]
        cosap = _ap(cosb[pg], bi * 128, [[A_TILE * 128, 128], [0, 4], [1, 128]])
        sinap = _ap(sinb[pg], bi * 128, [[A_TILE * 128, 128], [0, 4], [1, 128]])
        csres = (f"cos{pg}", f"sin{pg}")

        emit_ln(x_b, "x_t", xn_tm[:, :], "xn_tm", lnsq_a[:, :], "lnsq_a")
        emit_transpose8(xn_tm, "xn_tm", xn_fm, "xn_fm", slot=0)

        p.mm_multi([(PA[:, n * 512:(n + 1) * 512],
                     [(xn_fm[:, kc * 128:(kc + 1) * 128],
                       wq_bf[:, kc * D + n * 512: kc * D + (n + 1) * 512])
                      for kc in range(8)]) for n in range(2)],
                   reads=("xn_fm", "wq"), writes=("PA0", "PA1"), tag="Qmm")
        p.mm_multi([(PB[:, n * 512:(n + 1) * 512],
                     [(xn_fm[:, kc * 128:(kc + 1) * 128],
                       wk_bf[:, kc * D + n * 512: kc * D + (n + 1) * 512])
                      for kc in range(8)]) for n in range(2)],
                   reads=("xn_fm", "wk"), writes=("PB0", "PB1"), tag="Kmm")
        for h in range(H):
            sl = slice(h * 256, (h + 1) * 256)
            p.op("scalar", lambda e, s=sl, h_=h: e.activation(
                qkraw[:, s], _ap(PA, s.start, [[1024, 128], [1, 256]]), AF.Identity,
                scale=dqds[:, h_:h_ + 1]),
                reads=("PA0", "PA1", "dqds"), writes=("qkraw",), tag="Qevac")
        emit_rope(qkraw, "qkraw", q_tm, "q_tm", cosap, sinap, csres)

        for pair in range(2):
            p.mm_multi([((P0, P1)[i][:, :],
                         [(xn_fm[:, kc * 128:(kc + 1) * 128],
                           wv_bf[:, kc * 2048 + (pair * 2 + i) * 512:
                                 kc * 2048 + (pair * 2 + i + 1) * 512])
                          for kc in range(8)]) for i in range(2)],
                       reads=("xn_fm", "wv0", "wv1"),
                       writes=("P0b", "P1b"), tag="Vmm")
            for i in range(2):
                s4 = pair * 2 + i
                p.op("scalar", lambda e, b=(P0, P1)[i], s_=s4: e.activation(
                    v_tm[:, s_ * 512:(s_ + 1) * 512], b[:, :], AF.Identity),
                    reads=(("P0b", "P1b")[i],), writes=(f"v.{s4}",), tag="Vevac")
        emit_transpose8(q_tm, ("q_tm.1", "q_tm.2"), q_fm, "q_fm", slot=1)

        p.op("scalar", lambda e: e.activation(qkraw[:, :], PB[:, :], AF.Identity),
             reads=("PB0", "PB1"), writes=("qkraw",), tag="Kevac")
        for pair in range(2):
            p.mm_multi([((P0, P1)[i][:, :],
                         [(xn_fm[:, kc * 128:(kc + 1) * 128],
                           wg_bf[:, kc * 2048 + (pair * 2 + i) * 512:
                                 kc * 2048 + (pair * 2 + i + 1) * 512])
                          for kc in range(8)]) for i in range(2)],
                       reads=("xn_fm", "wg0", "wg1"),
                       writes=("P0b", "P1b"), tag="Gmm")
            for i in range(2):
                s4 = pair * 2 + i
                p.op("scalar", lambda e, b=(P0, P1)[i], s_=s4: e.activation(
                    gate[:, s_ * 512:(s_ + 1) * 512], b[:, :], AF.Silu),
                    reads=(("P0b", "P1b")[i],), writes=("gate",), tag="silu")
        emit_rope(qkraw, "qkraw", k_rope, "k_rope", cosap, sinap, csres)
        for h in range(H):
            sl = slice(h * 256, (h + 1) * 256)
            p.op("scalar", lambda e, s=sl, h_=h: e.activation(
                k_state[:, s], k_rope[:, s], AF.Identity, scale=dqds[:, 4 + h_:5 + h_]),
                reads=("k_rope.1", "k_rope.2", "dqds"), writes=("k_state",), tag="Kst")
        emit_transpose8(k_rope, ("k_rope.1", "k_rope.2"), k_fm, "k_fm", slot=0)

        p.mm_multi([(PA[:, h * 128:(h + 1) * 128],
                     [(k_fm[:, cc * 128:(cc + 1) * 128],
                       q_fm[:, cc * 128:(cc + 1) * 128]) for cc in (2 * h, 2 * h + 1)])
                    for h in range(H)],
                   reads=("k_fm", "q_fm"), writes=("PA0",), tag="ATmm")
        p.op("vector", lambda e: e.tensor_mul(atm[:, :], PA[:, 0:512], maskT[:, :]),
             reads=("PA0", "maskT"), writes=("atm",), tag="mask")

        for h in range(H):
            ob = (P0, P1)[h % 2]
            obres = ("P0b", "P1b")[h % 2]
            pairs = [(atm[:, h * 128:(h + 1) * 128], v_tm[:, h * 512:(h + 1) * 512])]
            pairs += [(q_fm[:, cc * 128:(cc + 1) * 128],
                       S_bf[:, h * 1024 + j * 512: h * 1024 + (j + 1) * 512])
                      for j, cc in enumerate((2 * h, 2 * h + 1))]
            p.mm_group(ob[:, :], pairs,
                       reads=("atm", f"v.{h}", "q_fm", f"S_bf.{h}"),
                       writes=(obres,), tag="Omm")
            p.op("scalar", lambda e, b=ob, h_=h: e.activation(
                o_sb[:, h_ * 512:(h_ + 1) * 512], b[:, :], AF.Identity),
                reads=(obres,), writes=(f"o.{h}",), tag="Oevac")
            p.op("scalar", lambda e, h_=h: e.activation(
                o_sq[:, :], o_sb[:, h_ * 512:(h_ + 1) * 512], AF.Square,
                accum_out=rms4[:, h_:h_ + 1]),
                reads=(f"o.{h}",), writes=("o_sq", f"rms.{h}"), tag="Osq")
            p.mm_multi([(PB[:, j * 512:(j + 1) * 512],
                         [(k_state[:, h * 256 + j * 128: h * 256 + (j + 1) * 128],
                           v_tm[:, h * 512:(h + 1) * 512])]) for j in range(2)],
                       reads=("k_state", f"v.{h}"), writes=("PB0", "PB1"), tag="Smm")
            for j in range(2):
                sl = slice(h * 1024 + j * 512, h * 1024 + (j + 1) * 512)
                ps = PB[:, j * 512:(j + 1) * 512]
                p.op("vector", lambda e, s=sl, ps_=ps, l=float(lamC[h]):
                     e.scalar_tensor_tensor(S[:, s], S[:, s], l, ps_, ALU.mult, ALU.add),
                     reads=(("PB0", "PB1")[j], "S"), writes=("S",), tag="Supd")

        p.op("scalar", lambda e: e.activation(rms4[:, 4:8], rms4[:, 0:4], AF.Ln,
                                              scale=1.0 / DV, bias=stat[:, 9:10]),
             reads=tuple(f"rms.{h}" for h in range(H)) + ("epsc",),
             writes=("rmsl",), tag="rmsln")
        p.op("scalar", lambda e: e.activation(rms4[:, 8:12], rms4[:, 4:8], AF.Exp,
                                              scale=-0.5),
             reads=("rmsl",), writes=("rstd4",), tag="rmsexp")
        for h in range(H):
            sl = slice(h * 512, (h + 1) * 512)
            p.op("vector", lambda e, s=sl, h_=h: e.scalar_tensor_tensor(
                og[:, s], o_sb[:, s], rms4[:, 8 + h_:9 + h_], gate[:, s],
                ALU.mult, ALU.mult),
                reads=(f"o.{h}", "rstd4", "gate"), writes=("og",), tag="og")
        for h in range(H):
            sl = slice(h * 1024, (h + 1) * 1024)
            p.op("scalar", lambda e, s=sl: e.activation(S_bf[:, s], S[:, s], AF.Identity),
                 reads=("S",), writes=(f"S_bf.{h}",), tag="Sbf")
        p.dma("sync", _ap(og_s, blk * C * 2048, [[2048, 128], [1, 2048]]),
              og[:, :], reads=("og",), writes=("og_hbm",), tag="ogst")

    p.barrier()

    ra2 = Region()
    wo_bf = ra2.t("wo_bf", [128, 16 * D], BF16)
    x_t2 = ra2.t("x_t2", [128, A_TILE * D], F32)
    og_in = [ra2.t(f"og_in{i}", [128, H * DV], BF16) for i in range(2)]
    og_fm = ra2.t("og_fm", [128, 2048], BF16)
    x1o = [ra2.t(f"x1o{i}", [128, D], F32) for i in range(2)]

    p.dma("gpsimd", wo_bf[:, :], wo_h[:, :], writes=("wo_raw",))
    def wo_fold(e):
        inst = None
        for cc in range(16):
            inst = e.tensor_scalar(wo_bf[:, cc * D:(cc + 1) * D],
                                   wo_bf[:, cc * D:(cc + 1) * D],
                                   gnr[:, cc:cc + 1], None, ALU.mult)
        return inst
    p.op("vector", wo_fold, reads=("wo_raw", "gnr"), writes=("wo",), tag="wofold")

    for blk in range(_LIMIT["a2"]):
        if blk < 64:
            emit_wcast(64 + blk)
        bt, bi = blk // A_TILE, blk % A_TILE
        pg = blk % 2
        if bi == 0:
            p.dma("sync", _ap(x_t2, 0, [[A_TILE * D, 128], [D, A_TILE], [1, D]]),
                  _ap(x_h, blk * C * D, [[D, 128], [C * D, A_TILE], [1, D]]),
                  writes=("x_t2",), tag="x2load")
        if _LIMIT["a2p"] & 1:
            p.dma("sync", og_in[pg][:, :],
                  _ap(og_s, blk * C * 2048, [[2048, 128], [1, 2048]]),
                  reads=("og_hbm",), writes=(f"ogin{pg}",), tag="ogld")
        for half in range(2 if _LIMIT["a2p"] & 2 else 0):
            def fnT(t, src=og_in[pg], hf=half):
                inst = None
                for cc in range(8):
                    inst = t.transpose(PT[:, cc * 128:(cc + 1) * 128],
                                       src[:, (hf * 8 + cc) * 128:(hf * 8 + cc + 1) * 128],
                                       ident[:, :])
                return inst
            p.op("tensor", fnT, reads=(f"ogin{pg}", "ident"), writes=("PT",), tag="ogT")
            p.op("vector", lambda e, hf=half: e.tensor_copy(
                og_fm[:, hf * 1024:(hf + 1) * 1024], PT[:, :]),
                reads=("PT",), writes=(f"ogfm{half}",), tag="ogTev")
        pp = (PA, PB)[pg]
        ppres = (("PA0", "PA1"), ("PB0", "PB1"))[pg]
        for n in range(2 if _LIMIT["a2p"] & 4 else 0):
            pairs = [(og_fm[:, cc * 128:(cc + 1) * 128],
                      wo_bf[:, cc * D + n * 512: cc * D + (n + 1) * 512])
                     for cc in range(16)]
            p.mm_group(pp[:, n * 512:(n + 1) * 512], pairs,
                       reads=("ogfm0", "ogfm1", "wo"), writes=(ppres[n],), tag="WOmm")
        if _LIMIT["a2p"] & 8:
            p.op("vector", lambda e, pp_=pp, xb=x_t2[:, bi * D:(bi + 1) * D],
                 o=x1o[pg]: e.tensor_add(o[:, :], pp_[:, :], xb),
                 reads=ppres + ("x_t2",), writes=(f"x1o{pg}",), tag="resid")
            p.dma("sync", _ap(x1_s, blk * C * D, [[D, 128], [1, D]]), x1o[pg][:, :],
                  reads=(f"x1o{pg}",), writes=("x1_hbm",), tag="x1st")

    p.barrier()

    rb = Region()
    x1n_fm = rb.t("x1n_fm", [128, 8 * TT], BF16)
    acc = rb.t("acc", [128, TT * 8], F32)
    h_bf = rb.t("h_bf", [128, 32 * TT], BF16)
    wpc = [rb.t(f"wpc{i}", [128, 8192], BF16) for i in range(2)]
    x1b = [rb.t(f"x1b{i}", [128, D], F32) for i in range(2)]
    outb = [rb.t(f"outb{i}", [128, D], F32) for i in range(2)]
    x1n_tm = rb.t("x1n_tm", [128, D], BF16)
    lnsq = rb.t("lnsq", [128, D], F32)
    lg = rb.t("lg", [128, 64], F32)
    r_sm = rb.t("r_sm", [128, 64], F32)
    sm8 = rb.t("sm8", [128, 64], F32)
    comb = rb.t("comb", [128, 64], F32)
    comb9 = rb.t("comb9", [128, 72], BF16)
    cw_sb = rb.t("cw_sb", [9, 1024], BF16)

    wslot = [0]

    def wpiece(e_i, which, pc):
        assert e_i == 0
        s = wslot[0] % 2
        wslot[0] += 1
        src = ws1_bf if which == 1 else ws2_bf
        ap_in = _ap(src, pc * 8192, [[32768, 128], [1, 8192]])
        res = f"wpc{s}"
        rds = tuple(f"wsbf.{which - 1}.{pc}.{qr}" for qr in range(4))
        p.dma("gpsimd", _ap(wpc[s], 0, [[8192, 128], [1, 8192]]), ap_in,
              reads=rds, writes=(res,), tag=f"w{which}ld")
        return wpc[s], res

    lgP = [lg, rb.t("lgB", [128, 64], F32)]
    rsmP = [r_sm, rb.t("r_smB", [128, 64], F32)]
    sm8P = [sm8, rb.t("sm8B", [128, 64], F32)]
    combP = [comb, rb.t("combB", [128, 64], F32)]
    comb9P = [comb9, rb.t("comb9B", [128, 72], BF16)]
    cwP = [cw_sb, rb.t("cw_sbB", [9, 1024], BF16)]
    msk0F = rb.t("msk0F", [128, 64], F32)
    msk1F = rb.t("msk1F", [128, 64], F32)
    mselF = rb.t("mselF", [128, 64], F32)
    msel_bf = rb.t("msel_bf", [128, 64], BF16)
    preF = rb.t("preF", [128, 64], F32)
    offBF = rb.t("offBF", [128, 64], F32)
    slotw = rb.t("slotw", [128, 64], F32)
    tmp64 = rb.t("tmp64", [128, 64], F32)
    trow_bf = rb.t("trow_bf", [1, 64], BF16)
    ttT_bf = rb.t("ttT_bf", [64, 1], BF16)
    cumoff = rb.t("cumoff", [64, 4], F32)
    orow = rb.t("orow", [1, 64], F32)
    sw8 = rb.t("sw8", [128, 16], F32)
    sc8 = rb.t("sc8", [128, 16], F32)
    m0asm = rb.t("m0asm", [128, 160], F32)

    def b_prologue(tt):
        for b in range(BPT):
            pg = b % 2
            blk = tt * BPT + b
            p.dma("sync", x1b[pg][:, :], _ap(x1_s, blk * C * D, [[D, 128], [1, D]]),
                  reads=("x1_hbm",), writes=(f"x1b{pg}",), tag="x1ld")
            emit_ln(x1b[pg][:, :], f"x1b{pg}", x1n_tm[:, :], "x1n_tm",
                    lnsq[:, :], "lnsq")
            p.dma("sync", _ap(x1n_s, blk * C * D, [[D, 128], [1, D]]), x1n_tm[:, :],
                  reads=("x1n_tm",), writes=("x1n_hbm",), tag="x1nst")
            pt_, ptres = PTS[b % 2]
            def fnT(t, src=x1n_tm, pt2=pt_):
                inst = None
                for cc in range(8):
                    inst = t.transpose(pt2[:, cc * 128:(cc + 1) * 128],
                                       src[:, cc * 128:(cc + 1) * 128], ident[:, :])
                return inst
            p.op("tensor", fnT, reads=("x1n_tm", "ident"), writes=(ptres,), tag="xnT")
            def fnTe(e, b_=b, pt2=pt_):
                inst = None
                for cc in range(8):
                    inst = e.tensor_copy(
                        x1n_fm[:, cc * TT + b_ * 128: cc * TT + (b_ + 1) * 128],
                        pt2[:, cc * 128:(cc + 1) * 128])
                return inst
            p.op("vector", fnTe, reads=(ptres,), writes=(f"xnfm.{b}",), tag="xnTev")
            pairs = [(x1n_fm[:, kc * TT + b * 128: kc * TT + (b + 1) * 128],
                      rwbf[:, kc * 8:(kc + 1) * 8]) for kc in range(8)]
            p.mm_group(P0[:, b * 8:(b + 1) * 8], pairs,
                       reads=(f"xnfm.{b}", "rwbf"), writes=("P0b",), tag="routmm")

    def b_softmax(tt):
        par = tt % 2
        lg_, r_sm_, sm8_, comb_ = lgP[par], rsmP[par], sm8P[par], combP[par]
        comb9_, cw_ = comb9P[par], cwP[par]
        R = lambda n: f"{n}.{par}"
        p.op("vector", lambda e: e.tensor_add(lg_[:, :], P0[:, 0:64], rb_bc[:, :]),
             reads=("P0b", "rbbc"), writes=(R("lg"),), tag="lgevac")
        lg3 = _ap(lg_, 0, [[64, 128], [8, 8], [1, 8]])
        p.op("vector", lambda e: e.tensor_reduce(
            _ap(sm8_, 0, [[64, 128], [1, 8], [1, 1]]), lg3, AX.X, ALU.max, negate=True),
            reads=(R("lg"),), writes=(R("negmax"),), tag="rmax")
        nmb = _ap(sm8_, 0, [[64, 128], [1, 8], [0, 8]])
        r3 = _ap(r_sm_, 0, [[64, 128], [8, 8], [1, 8]])
        p.op("vector", lambda e: e.tensor_tensor(r3, lg3, nmb, ALU.add),
             reads=(R("lg"), R("negmax")), writes=(R("r_sm"),), tag="subm")
        p.op("scalar", lambda e: e.activation(r_sm_[:, :], r_sm_[:, :], AF.Exp),
             reads=(R("r_sm"),), writes=(R("r_sm"),), tag="exp")
        p.op("vector", lambda e: e.tensor_reduce(
            _ap(sm8_, 8, [[64, 128], [1, 8], [1, 1]]), r3, AX.X, ALU.add),
            reads=(R("r_sm"),), writes=(R("ssum"),), tag="rsum")
        p.op("vector", lambda e: e.reciprocal(sm8_[:, 16:24], sm8_[:, 8:16]),
             reads=(R("ssum"),), writes=(R("srec"),))
        srb = _ap(sm8_, 16, [[64, 128], [1, 8], [0, 8]])
        p.op("vector", lambda e: e.tensor_tensor(r3, r3, srb, ALU.mult),
             reads=(R("r_sm"), R("srec")), writes=(R("r_sm"),), tag="rnorm")
        p.op("vector", lambda e: e.tensor_reduce(
            _ap(sm8_, 24, [[64, 128], [1, 8], [1, 1]]), r3, AX.X, ALU.max),
            reads=(R("r_sm"),), writes=(R("m1"),), tag="m1")
        m1b = _ap(sm8_, 24, [[64, 128], [1, 8], [0, 8]])
        c3 = _ap(comb_, 0, [[64, 128], [8, 8], [1, 8]])
        p.op("vector", lambda e: e.tensor_tensor(c3, r3, m1b, ALU.is_ge),
             reads=(R("r_sm"), R("m1")), writes=(R("comb"),), tag="eq1")
        p.op("scalar", lambda e: e.activation(msk0F[:, :], comb_[:, :], AF.Identity),
             reads=(R("comb"),), writes=("msk0F",), tag="msk0cp")
        p.op("vector", lambda e: e.scalar_tensor_tensor(c3, c3, -1e9, r3,
                                                        ALU.mult, ALU.add),
             reads=(R("comb"), R("r_sm")), writes=(R("comb"),), tag="r2")
        p.op("vector", lambda e: e.tensor_reduce(
            _ap(sm8_, 32, [[64, 128], [1, 8], [1, 1]]), c3, AX.X, ALU.max),
            reads=(R("comb"),), writes=(R("m2"),), tag="m2")
        m2b = _ap(sm8_, 32, [[64, 128], [1, 8], [0, 8]])
        p.op("vector", lambda e: e.tensor_tensor(c3, r3, m2b, ALU.is_ge),
             reads=(R("r_sm"), R("m2")), writes=(R("comb"),), tag="mask2")
        p.op("scalar", lambda e: e.activation(mselF[:, :], comb_[:, :], AF.Identity),
             reads=(R("comb"),), writes=("mselF",), tag="mselcp")
        p.op("scalar", lambda e: e.activation(msel_bf[:, :], comb_[:, :], AF.Identity),
             reads=(R("comb"),), writes=("msel_bf",), tag="mselbf")
        p.op("vector", lambda e: e.tensor_sub(msk1F[:, :], mselF[:, :], msk0F[:, :]),
             reads=("mselF", "msk0F"), writes=("msk1F",), tag="msk1")
        p.op("vector", lambda e: e.tensor_tensor(c3, c3, r3, ALU.mult),
             reads=(R("comb"), R("r_sm")), writes=(R("comb"),), tag="combu")
        p.op("vector", lambda e: e.tensor_add(sm8_[:, 40:48], sm8_[:, 24:32],
                                              sm8_[:, 32:40]),
             reads=(R("m1"), R("m2")), writes=(R("denom"),))
        p.op("vector", lambda e: e.reciprocal(sm8_[:, 48:56], sm8_[:, 40:48]),
             reads=(R("denom"),), writes=(R("drec"),))
        drb = _ap(sm8_, 48, [[64, 128], [1, 8], [0, 8]])
        p.op("vector", lambda e: e.tensor_tensor(c3, c3, drb, ALU.mult),
             reads=(R("comb"), R("drec")), writes=(R("comb"),), tag="combn")
        p.op("vector", lambda e: e.memset(
            _ap(comb9_, 0, [[72, 128], [9, 8], [1, 1]]), 1.0),
            writes=(R("comb9"),), tag="c9one")
        p.op("vector", lambda e: e.tensor_copy(
            _ap(comb9_, 1, [[72, 128], [9, 8], [1, 8]]), c3),
            reads=(R("comb"),), writes=(R("comb9"),), tag="c9cp")
        def fnc9T(t):
            inst = None
            for b_ in range(8):
                inst = t.transpose(_ap(PT, b_ * 128, [[1024, 9], [1, 128]]),
                                   comb9_[:, b_ * 9:(b_ + 1) * 9], ident[:, :])
            return inst
        p.op("tensor", fnc9T, reads=(R("comb9"), "ident"), writes=("PT",), tag="c9T")
        p.op("vector", lambda e: e.tensor_copy(cw_[0:9, :],
                                               _ap(PT, 0, [[1024, 9], [1, 1024]])),
             reads=("PT",), writes=(R("cw"),), tag="cwev")
        p.mm_multi([(P1[:, 0:64], [(lstr[:, :], msel_bf[:, :])]),
                    (P1[0:1, 64:128], [(o128c[:, :], msel_bf[:, :])])],
                   reads=("msel_bf", "lstr", "o128c"),
                   writes=("P1pre", "P1tot"), tag="pfxmm")
        p.op("scalar", lambda e: e.activation(preF[:, :], P1[:, 0:64], AF.Identity),
             reads=("P1pre",), writes=("preF",), tag="preev")
        p.op("vector", lambda e: e.tensor_copy(trow_bf[0:1, :], P1[0:1, 64:128]),
             reads=("P1tot",), writes=("trow_bf",), tag="totev")
        p.mm_group(P1[0:64, 128:129], [(trow_bf[0:1, :], one1[0:1, :])],
                   reads=("trow_bf", "one1"), writes=("P1tt",), tag="totTmm")
        p.op("vector", lambda e: e.tensor_copy(ttT_bf[0:64, :], P1[0:64, 128:129]),
             reads=("P1tt",), writes=("ttT_bf",), tag="ttTev")
        p.mm_multi([(P1[0:64, 192:193], [(bl64[0:64, :], ttT_bf[0:64, :])]),
                    (P1[0:64, 200:201], [(bf64[0:64, :], ttT_bf[0:64, :])])],
                   reads=("ttT_bf", "bl64", "bf64"),
                   writes=("P1cum", "P1grp"), tag="cummm")
        p.op("vector", lambda e: e.tensor_add(cumoff[0:64, 2:3], P1[0:64, 192:193],
                                              running64[0:64, :]),
             reads=("P1cum", "running64"), writes=("off64",), tag="off64")
        p.op("vector", lambda e: e.tensor_add(running64[0:64, :], running64[0:64, :],
                                              P1[0:64, 200:201]),
             reads=("P1grp", "running64", "off64"), writes=("running64",), tag="runupd")
        p.mm_group(P1[0:1, 256:320], [(cumoff[0:64, 2:3], id64[0:64, :])],
                   reads=("off64", "id64"), writes=("P1or",), tag="orowmm")
        p.op("vector", lambda e: e.tensor_copy(orow[0:1, :], P1[0:1, 256:320]),
             reads=("P1or",), writes=("orow",), tag="orowev")
        p.mm_group(P1[:, 320:384], [(o1x128f[0:1, :], orow[0:1, :])],
                   reads=("orow", "o1x128f"), writes=("P1ob",), tag="offBmm")
        p.op("scalar", lambda e: e.activation(offBF[:, :], P1[:, 320:384], AF.Identity),
             reads=("P1ob",), writes=("offBF",), tag="offBev")
        p.op("vector", lambda e: e.tensor_add(slotw[:, :], preF[:, :], offBF[:, :]),
             reads=("preF", "offBF"), writes=("slotw",), tag="slotw")
        sw3 = lambda off: _ap(sw8, off, [[16, 128], [1, 8], [1, 1]])
        t3 = _ap(tmp64, 0, [[64, 128], [8, 8], [1, 8]])
        for r_, mk in ((0, "msk0F"), (1, "msk1F")):
            p.op("vector", lambda e, mk_=mk: e.tensor_mul(
                tmp64[:, :], (msk0F if mk_ == "msk0F" else msk1F)[:, :], slotw[:, :]),
                reads=(mk, "slotw"), writes=("tmp64",), tag="swmul")
            p.op("vector", lambda e, r2_=r_: e.tensor_reduce(
                sw3(r2_ * 8), t3, AX.X, ALU.add),
                reads=("tmp64",), writes=(f"sw{r_}",), tag="swred")
        p.op("vector", lambda e: e.tensor_mul(sc8[:, 0:8], sm8_[:, 24:32],
                                              sm8_[:, 48:56]),
             reads=(R("m1"), R("drec")), writes=("sc0",), tag="sc0")
        p.op("vector", lambda e: e.tensor_mul(sc8[:, 8:16], sm8_[:, 32:40],
                                              sm8_[:, 48:56]),
             reads=(R("m2"), R("drec")), writes=("sc1",), tag="sc1")
        m3 = lambda off, n: _ap(m0asm, off, [[160, 128], [20, 8], [1, n]])
        p.op("scalar", lambda e: e.activation(
            m3(0, 8), _ap(msk0F, 0, [[64, 128], [8, 8], [1, 8]]), AF.Identity),
            reads=("msk0F",), writes=("m0asm",), tag="asm0")
        p.op("scalar", lambda e: e.activation(
            m3(8, 8), _ap(msk1F, 0, [[64, 128], [8, 8], [1, 8]]), AF.Identity),
            reads=("msk1F", "m0asm"), writes=("m0asm",), tag="asm1")
        p.op("scalar", lambda e: e.activation(
            m3(16, 1), _ap(sw8, 0, [[16, 128], [1, 8], [1, 1]]), AF.Identity),
            reads=("sw0", "m0asm"), writes=("m0asm",), tag="asm2")
        p.op("scalar", lambda e: e.activation(
            m3(17, 1), _ap(sw8, 8, [[16, 128], [1, 8], [1, 1]]), AF.Identity),
            reads=("sw1", "m0asm"), writes=("m0asm",), tag="asm3")
        p.op("scalar", lambda e: e.activation(
            m3(18, 1), _ap(sc8, 0, [[16, 128], [1, 8], [1, 1]]), AF.Identity),
            reads=("sc0", "m0asm"), writes=("m0asm",), tag="asm4")
        p.op("scalar", lambda e: e.activation(
            m3(19, 1), _ap(sc8, 8, [[16, 128], [1, 8], [1, 1]]), AF.Identity),
            reads=("sc1", "m0asm"), writes=("m0asm",), tag="asm5")
        p.dma("sync", _ap(meta0_s, tt * 1024 * 20, [[20, 128], [2560, 8], [1, 20]]),
              _ap(m0asm, 0, [[160, 128], [20, 8], [1, 20]]),
              reads=("m0asm",), writes=("meta0_hbm",), tag="m0st")

    def b_experts(tt):
        par = tt % 2
        comb_ = combP[par]
        n_e = _LIMIT["ne"]
        for e_i in range(n_e):
            for pc in range(4):
                wsb, wres = wpiece(e_i, 1, pc)
                for mb in range(8):
                    mglob = pc * 8 + mb
                    pp = (PA, PB)[mglob % 2]
                    ppres = (("PA0", "PA1"), ("PB0", "PB1"))[mglob % 2]
                    groups = [(pp[:, n * 512:(n + 1) * 512],
                               [(wsb[:, kc * 1024 + mb * 128: kc * 1024 + (mb + 1) * 128],
                                 x1n_fm[:, kc * TT + n * 512: kc * TT + (n + 1) * 512])
                                for kc in range(8)]) for n in range(2)]
                    p.mm_multi(groups,
                               reads=(wres,) + tuple(f"xnfm.{b}" for b in range(8)),
                               writes=ppres, tag="h_mm")
                    p.op("scalar", lambda e, pp_=pp, m=mglob, ei=e_i: e.activation(
                        h_bf[:, m * TT:(m + 1) * TT], pp_[:, :], AF.Gelu_apprx_tanh,
                        bias=b1r[:, ei * 32 + m: ei * 32 + m + 1]),
                        reads=ppres + ("b1r",), writes=(f"h.{mglob}",), tag="gelu")
            if e_i == n_e - 1 and tt + 1 < _LIMIT["bt"]:
                b_prologue(tt + 1)
                b_softmax(tt + 1)
            cw_ = cwP[par]
            for pc in range(4):
                wsb, wres = wpiece(e_i, 2, pc)
                for b in range(BPT):
                    g = b + pc * BPT
                    pp = (PA, PB)[g % 2]
                    ppres = (("PA0", "PA1"), ("PB0", "PB1"))[g % 2]
                    hres = tuple(f"h.{pc * 8 + k}" for k in range(8))
                    rds = (wres,) + hres
                    groups = []
                    for n in range(2):
                        pairs = []
                        if e_i == 0 and pc == 0:
                            pairs.append((cw_[0:9, b * 128:(b + 1) * 128],
                                          b2bf[0:9, n * 512:(n + 1) * 512]))
                        pairs += [(h_bf[:, (pc * 8 + k) * TT + b * 128:
                                        (pc * 8 + k) * TT + (b + 1) * 128],
                                   wsb[:, k * 1024 + n * 512: k * 1024 + (n + 1) * 512])
                                  for k in range(8)]
                        groups.append((pp[:, n * 512:(n + 1) * 512], pairs))
                    if e_i == 0 and pc == 0:
                        rds = rds + (f"cw.{par}", "b2bf")
                    p.mm_multi(groups, reads=rds, writes=ppres, tag="y_mm")
                    if e_i == 0 and pc == 0:
                        p.op("vector", lambda e, pp_=pp, b_=b: e.tensor_copy(
                            acc[:, b_ * D:(b_ + 1) * D], pp_[:, :]),
                            reads=ppres, writes=(f"acc.{b}",), tag="acc0")
                    elif e_i == 0:
                        p.op("vector", lambda e, pp_=pp, b_=b: e.tensor_add(
                            acc[:, b_ * D:(b_ + 1) * D], acc[:, b_ * D:(b_ + 1) * D],
                            pp_[:, :]),
                            reads=ppres + (f"acc.{b}",), writes=(f"acc.{b}",), tag="accs")
                    else:
                        p.op("vector", lambda e, pp_=pp, b_=b, ei=e_i:
                             e.scalar_tensor_tensor(
                                 acc[:, b_ * D:(b_ + 1) * D], pp_[:, :],
                                 comb_[:, b_ * 8 + ei - 1: b_ * 8 + ei],
                                 acc[:, b_ * D:(b_ + 1) * D], ALU.mult, ALU.add),
                             reads=ppres + (f"acc.{b}", f"comb.{par}"),
                             writes=(f"acc.{b}",), tag="acce")

    def b_final(tt):
        for b in range(BPT):
            pg = b % 2
            blk = tt * BPT + b
            p.dma("sync", x1b[pg][:, :], _ap(x1_s, blk * C * D, [[D, 128], [1, D]]),
                  reads=("x1_hbm",), writes=(f"x1b{pg}",), tag="x1ld2")
            p.op("vector", lambda e, b_=b, pg_=pg: e.tensor_add(
                outb[pg_][:, :], acc[:, b_ * D:(b_ + 1) * D], x1b[pg_][:, :]),
                reads=(f"acc.{b}", f"x1b{pg}"), writes=(f"outb{pg}",), tag="out")
            p.dma("sync", _ap(out_h, blk * C * D, [[D, 128], [1, D]]), outb[pg][:, :],
                  reads=(f"outb{pg}",), writes=("out_hbm",), tag="outst")

    if _LIMIT["bt"] > 0:
        b_prologue(0)
        b_softmax(0)
    for tt in range(_LIMIT["bt"]):
        for i in range(20):
            emit_wcast(128 + tt * 20 + i)
        b_experts(tt)
        b_final(tt)

    def b2_pages():
        p.dma("sync", _ap(cnt_s, 0, [[1, 64], [1, 1]]), running64[0:64, 0:1],
              reads=("running64",), writes=("cnt_hbm",), tag="cntst")
        cr = rowscr
        p.dma("sync", cr[0:1, 0:8], _ap(cnt_s, 0, [[8, 1], [1, 8]]),
              reads=("cnt_hbm",), writes=("cntrow",), tag="cntld")
        A, B_, Cb = (16, 40, 64)
        p.op("vector", lambda e: e.memset(cr[0:1, A:A + 24], 0.0), writes=("bufA",))
        for j in range(16):
            p.op("vector", lambda e, j_=j: e.tensor_scalar(
                cr[0:1, 8:16], cr[0:1, 0:8], float(j_ * 512 + 1), None,
                ALU.is_ge),
                reads=("cntrow",), writes=("pgm",), tag="pgge")
            p.op("vector", lambda e: e.tensor_add(
                cr[0:1, A + 8:A + 16], cr[0:1, A + 8:A + 16], cr[0:1, 8:16]),
                reads=("bufA", "pgm"), writes=("bufA",), tag="pgacc")
        p.op("vector", lambda e: e.memset(cr[0:1, B_:B_ + 24], 0.0), writes=("bufB",))
        p.op("vector", lambda e: e.tensor_add(
            cr[0:1, B_ + 8:B_ + 16], cr[0:1, A + 8:A + 16], cr[0:1, A + 7:A + 15]),
            reads=("bufA", "bufB"), writes=("bufB",), tag="pfx1")
        p.op("vector", lambda e: e.memset(cr[0:1, Cb:Cb + 24], 0.0), writes=("bufC",))
        p.op("vector", lambda e: e.tensor_add(
            cr[0:1, Cb + 8:Cb + 16], cr[0:1, B_ + 8:B_ + 16], cr[0:1, B_ + 6:B_ + 14]),
            reads=("bufB", "bufC"), writes=("bufC",), tag="pfx2")
        p.op("vector", lambda e: e.memset(cr[0:1, A:A + 8], 0.0),
             reads=("bufA",), writes=("bufA",))
        p.op("vector", lambda e: e.tensor_add(
            cr[0:1, A + 8:A + 16], cr[0:1, Cb + 8:Cb + 16], cr[0:1, Cb + 4:Cb + 12]),
            reads=("bufC", "bufA"), writes=("incl",), tag="pfx3")
        p.op("vector", lambda e: e.memset(cr[0:1, 88:136], 0.0), writes=("eop",))
        for e_ in range(8):
            p.op("vector", lambda ee, e2=e_: ee.tensor_scalar(
                cr[0:1, 136:184], pio48[0:1, :], cr[0:1, A + 8 + e2:A + 9 + e2],
                None, ALU.is_ge),
                reads=("incl", "pio48"), writes=("tmp48",), tag="eopge")
            p.op("vector", lambda ee: ee.tensor_add(
                cr[0:1, 88:136], cr[0:1, 88:136], cr[0:1, 136:184]),
                reads=("tmp48", "eop"), writes=("eop",), tag="eopacc")
        p.op("vector", lambda e: e.tensor_copy(rowbf[0:1, 0:48], cr[0:1, 88:136]),
             reads=("eop",), writes=("eopbf",), tag="eopbf")
        p.op("vector", lambda e: e.tensor_copy(rowbf[0:1, 48:56],
                                               cr[0:1, A + 7:A + 15]),
             reads=("incl",), writes=("pbbf",), tag="pbbf")
        p.mm_multi([(P0[:, 64:112], [(o1x128b[0:1, :], rowbf[0:1, 0:48])]),
                    (P0[:, 112:120], [(o1x128b[0:1, :], rowbf[0:1, 48:56])])],
                   reads=("eopbf", "pbbf", "o1x128b"),
                   writes=("P0e", "P0p"), tag="bcmm")
        p.op("scalar", lambda e: e.activation(pbB[:, :], P0[:, 112:120], AF.Identity),
             reads=("P0p",), writes=("pbB",), tag="pbBev")
        p.op("scalar", lambda e: e.activation(tmp64[:, 0:48], P0[:, 64:112],
                                              AF.Identity),
             reads=("P0e",), writes=("tmp64",), tag="eAllev")
        iotaB = _ap(iota128, 0, [[1, 128], [0, 48]])
        p.op("vector", lambda e: e.scalar_tensor_tensor(
            tmp64[:, 0:48], tmp64[:, 0:48], 128.0, iotaB, ALU.mult, ALU.add),
            reads=("tmp64", "iota128"), writes=("tmp64",), tag="widxf")
        p.op("vector", lambda e: e.tensor_copy(wIdxu[:, :], tmp64[:, 0:48]),
             reads=("tmp64",), writes=("wIdxu",), tag="widxu")

    m0b = [rb.t(f"m0b{i}", [128, 20], F32) for i in range(2)]
    b3t = rb.t("b3t", [128, 8], F32)
    b3g = rb.t("b3g", [128, 2], F32)
    b3i = rb.t("b3i", [128, 2], U32)
    b3v = [rb.t(f"b3v{i}", [128, 4], F32) for i in range(2)]
    meta_res = []

    def b3_block(blk):
        pg = blk % 2
        p.dma("sync", m0b[pg][:, :],
              _ap(meta0_s, blk * 128 * 20, [[20, 128], [1, 20]]),
              reads=("meta0_hbm",), writes=(f"m0b{pg}",), tag="m0ld")
        for r_ in range(2):
            p.op("vector", lambda e, r2=r_, pg_=pg: e.tensor_mul(
                b3t[:, :], m0b[pg_][:, r2 * 8:(r2 + 1) * 8], pbB[:, :]),
                reads=(f"m0b{pg}", "pbB"), writes=("b3t",), tag="b3mul")
            p.op("vector", lambda e, r2=r_: e.tensor_reduce(
                b3g[:, r2:r2 + 1], b3t[:, :], AX.X, ALU.add),
                reads=("b3t",), writes=("b3g",), tag="b3red")
            p.op("vector", lambda e, r2=r_, pg_=pg: e.scalar_tensor_tensor(
                b3g[:, r2:r2 + 1], b3g[:, r2:r2 + 1], 512.0,
                m0b[pg_][:, 16 + r2:17 + r2], ALU.mult, ALU.add),
                reads=(f"m0b{pg}", "b3g"), writes=("b3g",), tag="b3glob")
            p.op("vector", lambda e, r2=r_, pg_=pg, bb=blk: e.tensor_scalar(
                b3v[pg_][:, r2 * 2:r2 * 2 + 1], iota128[:, :], float(bb * 128),
                None, ALU.add),
                reads=("iota128",), writes=(f"b3v{pg}",), tag="b3tok")
            p.op("vector", lambda e, r2=r_, pg_=pg: e.tensor_copy(
                b3v[pg_][:, r2 * 2 + 1:r2 * 2 + 2], m0b[pg_][:, 18 + r2:19 + r2]),
                reads=(f"m0b{pg}", f"b3v{pg}"), writes=(f"b3v{pg}",), tag="b3sc")
            p.op("vector", lambda e, r2=r_: e.tensor_copy(
                b3i[:, r2:r2 + 1], b3g[:, r2:r2 + 1]),
                reads=("b3g", "b3i"), writes=("b3i",), tag="b3idx")
            res = f"meta_w.{blk}.{r_}"
            meta_res.append(res)
            p.idma(meta_s[:, :], (b3i[:, r_:r_ + 1], 0),
                   b3v[pg][:, r_ * 2:r_ * 2 + 2], None,
                   reads=(f"b3v{pg}", "b3i", "meta_hbm"), writes=(res,),
                   tag="b3scat")

    if _LIMIT["bt"] >= 8:
        b2_pages()
        for blk in range(_LIMIT["b3"]):
            b3_block(blk)

    p.barrier()
    rc = Region()
    metapg = [rc.t(f"metapg{i}", [128, 8], F32) for i in range(2)]
    toku = [rc.t(f"toku{i}", [128, 4], U32) for i in range(2)]
    xgt = [rc.t(f"xgt{i}", [128, 4096], BF16) for i in range(2)]
    xgfm = [rc.t(f"xgfm{i}", [128, 4096], BF16) for i in range(2)]
    hC = rc.t("hC", [128, 32 * 512], BF16)
    yacc = rc.t("yacc", [128, 4 * 1024], F32)
    ysc = [rc.t(f"ysc{i}", [128, 1024], F32) for i in range(2)]
    wpcC = [rc.t(f"wpcC{i}", [128, 8192], BF16) for i in range(3)]
    b1pg = [rc.t(f"b1pg{i}", [128, 32], F32) for i in range(2)]
    wslotC = [0]

    def c_wpiece(pgi, which, pc):
        s = wslotC[0] % 3
        wslotC[0] += 1
        src = w1s_bf if which == 1 else w2s_bf
        res = f"wpcC{s}"
        rds = tuple(f"wspec.{which - 1}.{pc}.{ei}.{qr}"
                    for ei in range(1, 9) for qr in range(4))
        p.idma(_ap(wpcC[s], 0, [[8192, 128], [1, 8192]]), None,
               src[:, :], (wIdxu[:, pgi:pgi + 1], 0),
               elem_off=pc * 8192, bc=rBCW,
               reads=rds + ("wIdxu", "bcreg2"), writes=(res,), tag=f"cw{which}g")
        return wpcC[s], res

    if _LIMIT["bt"] >= 8:
        p.op("gpsimd", lambda e: e.memset(xgt[0][:, :], 0.0), writes=("xgt0",))
        p.op("gpsimd", lambda e: e.memset(xgt[1][:, :], 0.0), writes=("xgt1",))
        for pgi in range(_LIMIT["pg"]):
            par = pgi % 2
            p.dma("sync", _ap(metapg[par], 0, [[8, 128], [2, 4], [1, 2]]),
                  _ap(meta_s, pgi * 1024, [[2, 128], [256, 4], [1, 2]]),
                  reads=tuple(meta_res), writes=(f"metapg{par}",), tag="cmld")
            p.op("vector", lambda e, par_=par: e.tensor_copy(
                toku[par_][:, :], _ap(metapg[par_], 0, [[8, 128], [2, 4], [1, 1]])),
                reads=(f"metapg{par}", f"toku{par}"), writes=(f"toku{par}",),
                tag="ctok")
            for sb in range(4):
                p.idma(xgt[par][:, sb * 1024:(sb + 1) * 1024], None,
                       x1n_s[:, :], (toku[par][:, sb:sb + 1], 0), bc=rBC,
                       reads=("x1n_hbm", f"toku{par}", "bcreg", f"xgt{par}"),
                       writes=(f"xgt{par}",), tag="cxg")
            for sb in range(4):
                pt_, ptres = PTS[sb % 2]
                def fnT(t, src=xgt[par], sb_=sb, pt2=pt_):
                    inst = None
                    for cc in range(8):
                        inst = t.transpose(
                            pt2[:, cc * 128:(cc + 1) * 128],
                            src[:, sb_ * 1024 + cc * 128:sb_ * 1024 + (cc + 1) * 128],
                            ident[:, :])
                    return inst
                p.op("tensor", fnT, reads=(f"xgt{par}", "ident"), writes=(ptres,),
                     tag="cxT")
                p.op("vector", lambda e, sb_=sb, pt2=pt_, par_=par: e.tensor_copy(
                    _ap(xgfm[par_], sb_ * 128, [[4096, 128], [512, 8], [1, 128]]),
                    _ap(pt2, 0, [[1024, 128], [128, 8], [1, 128]])),
                    reads=(ptres, f"xgfm{par}"), writes=(f"xgfm{par}",), tag="cxTev")
            p.idma(b1pg[par][:, :], None, b1s_h[:, :], (wIdxu[:, pgi:pgi + 1], 0),
                   bc=rBCW, reads=("wIdxu", "bcreg2", f"b1pg{par}"),
                   writes=(f"b1pg{par}",), tag="cb1g")
            for pc in range(4):
                wsb, wres = c_wpiece(pgi, 1, pc)
                for mb in range(8):
                    mglob = pc * 8 + mb
                    pp = (P0, P1)[mglob % 2]
                    ppres = ("P0b", "P1b")[mglob % 2]
                    pairs = [(wsb[:, kc * 1024 + mb * 128: kc * 1024 + (mb + 1) * 128],
                              xgfm[par][:, kc * 512:(kc + 1) * 512])
                             for kc in range(8)]
                    p.mm_group(pp[:, :], pairs,
                               reads=(wres, f"xgfm{par}"), writes=(ppres,),
                               tag="ch_mm")
                    p.op("scalar", lambda e, pp_=pp, m=mglob, par_=par: e.activation(
                        hC[:, m * 512:(m + 1) * 512], pp_[:, :], AF.Gelu_apprx_tanh,
                        bias=b1pg[par_][:, m:m + 1]),
                        reads=(ppres, f"b1pg{par}"), writes=(f"hC.{mglob}",),
                        tag="cgelu")
            for pc2 in range(4):
                wsb2, wres2 = c_wpiece(pgi, 2, pc2)
                for b in range(4):
                    pp = (PA, PB)[b % 2]
                    ppres = (("PA0", "PA1"), ("PB0", "PB1"))[b % 2]
                    hres = tuple(f"hC.{pc2 * 8 + k}" for k in range(8))
                    groups = [(pp[:, n * 512:(n + 1) * 512],
                               [(hC[:, (pc2 * 8 + k) * 512 + b * 128:
                                     (pc2 * 8 + k) * 512 + (b + 1) * 128],
                                 wsb2[:, k * 1024 + n * 512: k * 1024 + (n + 1) * 512])
                                for k in range(8)]) for n in range(2)]
                    p.mm_multi(groups, reads=(wres2,) + hres, writes=ppres,
                               tag="cy_mm")
                    if pc2 == 0:
                        p.op("vector", lambda e, pp_=pp, b_=b: e.tensor_copy(
                            yacc[:, b_ * 1024:(b_ + 1) * 1024], pp_[:, :]),
                            reads=ppres, writes=(f"yacc.{b}",), tag="cy0")
                    else:
                        p.op("vector", lambda e, pp_=pp, b_=b: e.tensor_add(
                            yacc[:, b_ * 1024:(b_ + 1) * 1024],
                            yacc[:, b_ * 1024:(b_ + 1) * 1024], pp_[:, :]),
                            reads=ppres + (f"yacc.{b}",), writes=(f"yacc.{b}",),
                            tag="cyacc")
            for b in range(4):
                yp = b % 2
                p.op("vector", lambda e, b_=b, yp_=yp, par_=par: e.tensor_scalar(
                    ysc[yp_][:, :], yacc[:, b_ * 1024:(b_ + 1) * 1024],
                    metapg[par_][:, 2 * b_ + 1:2 * b_ + 2], None, ALU.mult),
                    reads=(f"yacc.{b}", f"metapg{par}", f"ysc{yp}"),
                    writes=(f"ysc{yp}",), tag="cyscale")
                p.idma(out_h[:, :], (toku[par][:, b:b + 1], 0),
                       ysc[yp][:, :], None, bc=rBC, cce=ALU.add,
                       reads=(f"ysc{yp}", f"toku{par}", "bcreg", "out_hbm"),
                       writes=("out_hbm",), tag="cyadd")

    p.barrier()
    n = p.emit()
    _cache[key] = nc
    _cache["n_ops"] = len(p.ops)
    return nc


def kernel(**inputs):
    cst = _consts()
    gn = np.asarray(inputs["L0_attn_g_norm_w"], np.float32)
    gn_bc = np.tile(gn, H)
    gnorm_r = np.ascontiguousarray(gn_bc.reshape(16, 128).T)
    rb = np.asarray(inputs["L0_ffn_router_b"], np.float32)
    rb_bc = np.ascontiguousarray(np.broadcast_to(np.tile(rb, 8)[None, :], (128, 64)))
    b1 = np.concatenate([np.asarray(inputs["L0_ffn_shared_b1"], np.float32)[None],
                         np.asarray(inputs["L0_ffn_spec_b1"], np.float32)], 0)
    b1_r = np.ascontiguousarray(
        b1.reshape(NEXP, 32, 128).transpose(2, 0, 1).reshape(128, NEXP * 32))
    b2 = np.concatenate([np.asarray(inputs["L0_ffn_shared_b2"], np.float32)[None],
                         np.asarray(inputs["L0_ffn_spec_b2"], np.float32)], 0)
    w1 = np.concatenate([np.asarray(inputs["L0_ffn_shared_w1"], np.float32)[None],
                         np.asarray(inputs["L0_ffn_spec_w1"], np.float32)], 0)
    w2 = np.concatenate([np.asarray(inputs["L0_ffn_shared_w2"], np.float32)[None],
                         np.asarray(inputs["L0_ffn_spec_w2"], np.float32)], 0)
    w1_all = np.stack([_wt_relayout(w1[i]) for i in range(NEXP)], 0)
    w2_all = np.stack([_wt_relayout(w2[i]) for i in range(NEXP)], 0)

    b1s = np.ascontiguousarray(
        b1[1:].reshape(NSPEC, 32, 128).transpose(0, 2, 1).reshape(NSPEC * 128, 32))
    shared = dict(
        wq_t=_wt_relayout(np.asarray(inputs["L0_attn_Wq"], np.float32)),
        wk_t=_wt_relayout(np.asarray(inputs["L0_attn_Wk"], np.float32)),
        wv_t=_wt_relayout(np.asarray(inputs["L0_attn_Wv"], np.float32)),
        wg_t=_wt_relayout(np.asarray(inputs["L0_attn_Wg"], np.float32)),
        wo_t=_wt_relayout(np.asarray(inputs["L0_attn_Wo"], np.float32)),
        w1_all=w1_all, w2_all=w2_all, b1_r=b1_r, b2_all=np.ascontiguousarray(b2),
        rw_t=_wt_relayout(np.asarray(inputs["L0_ffn_router_w"], np.float32)),
        rb_bc=rb_bc, gnorm_r=gnorm_r, ident=cst["ident"],
        cos_tm=cst["cos_tm"], sin_tm=cst["sin_tm"], dq_ds=cst["dq_ds"],
        maskT=cst["maskT"],
        b1s=b1s, lstrict=cst["lstrict"], ones128c=cst["ones128c"],
        one1=cst["one1"], blockL64=cst["blockL64"], blockF64=cst["blockF64"],
        ident64=cst["ident64"], ones1x128f=cst["ones1x128f"],
        ones1x128b=cst["ones1x128b"], iota128h=cst["iota128"],
        piota48h=cst["piota48"],
    )
    x = np.asarray(inputs["L0_attn_x"], np.float32)
    nc = build_program()
    in_maps = []
    for c in range(NCORES):
        m = dict(shared)
        m["x"] = np.ascontiguousarray(x[NSEQ * c:NSEQ * (c + 1)].reshape(TOK, D))
        in_maps.append(m)
    res = run_bass_kernel_spmd(nc, in_maps, core_ids=list(range(NCORES)))
    out = np.empty((16, T, D), np.float32)
    for c in range(NCORES):
        out[NSEQ * c:NSEQ * (c + 1)] = res.results[c]["out"].reshape(NSEQ, T, D)
    return out


# version 11
# speedup vs baseline: 1.4594x; 1.0413x over previous
import numpy as np
import ml_dtypes
import concourse.bass as bass
import concourse.mybir as mybir
from concourse.bass import IndirectOffsetOnAxis
from concourse.bass_utils import run_bass_kernel_spmd

F32 = mybir.dt.float32
BF16 = mybir.dt.bfloat16
U32 = mybir.dt.uint32
I32 = mybir.dt.int32
AF = mybir.ActivationFunctionType
ALU = mybir.AluOpType
AX = mybir.AxisListType
BF = ml_dtypes.bfloat16

NCORES = 8
T = 4096
D = 1024
H, DK, DV = 4, 256, 512
C = 128
NSEQ = 2
TOK = NSEQ * T
NBLK = TOK // C
BLK_PER_SEQ = T // C
A_TILE = 2
D_FF = 4096
NEXP = 9
TT = 1024
NBT = TOK // TT
BPT = TT // C
EPS = 1e-5
NSLOT = 8
PAGES = 40
PSLOT = 512
NSPEC = 8

_cache = {}
_LIMIT = {"a1": 64, "a2": 64, "bt": 8, "ne": 1, "a2p": 15, "pg": PAGES,
          "b3": 64, "w": 1}


def _consts():
    if "c" in _cache:
        return _cache["c"]
    ld = np.log1p(-np.exp2(-5.0 - np.arange(H, dtype=np.float64)))
    i = np.arange(C, dtype=np.float64)
    decay_q = np.exp(ld[None, :] * (i[:, None] + 1)) / 16.0
    decay_state = np.exp(ld[None, :] * (C - 1 - i[:, None]))
    lamC = np.exp(ld * C).astype(np.float32)
    maskT = np.zeros((C, H * C), dtype=np.float64)
    for h in range(H):
        maskT[:, h * C:(h + 1) * C] = (np.exp(ld[h] * -(i[:, None] + 1.0))
                                       * (i[None, :] >= i[:, None]))
    inv = 1.0 / 10000.0 ** (np.arange(0, DK, 2, dtype=np.float64) / DK)
    t = np.arange(T, dtype=np.float64)
    f = t[:, None] * inv[None, :]
    q64 = np.arange(64)
    c = dict(
        dq_ds=np.concatenate([decay_q, decay_state], 1).astype(np.float32),
        lamC=lamC,
        maskT=maskT.astype(np.float32),
        cos_tm=np.cos(f).astype(np.float32),
        sin_tm=np.sin(f).astype(np.float32),
        ident=np.eye(128, dtype=np.float32).astype(BF),
        lstrict=(np.arange(128)[:, None] < np.arange(128)[None, :]).astype(BF),
        ones128c=np.ones((128, 1), np.float32).astype(BF),
        one1=np.ones((1, 1), np.float32).astype(BF),
        blockL64=(((q64[:, None] % 8) == (q64[None, :] % 8))
                  & ((q64[:, None] // 8) < (q64[None, :] // 8))).astype(BF),
        blockF64=((q64[:, None] % 8) == (q64[None, :] % 8)).astype(BF),
        ident64=np.eye(64, dtype=np.float32),
        ones1x128f=np.ones((1, 128), np.float32),
        ones1x128b=np.ones((1, 128), np.float32).astype(BF),
        iota128=np.arange(128, dtype=np.float32).reshape(128, 1),
        piota48=np.arange(48, dtype=np.float32).reshape(1, 48),
    )
    _cache["c"] = c
    return c


def _wt_relayout(w):
    dout, din = w.shape
    wt = np.asarray(w, np.float32).T
    return np.ascontiguousarray(
        wt.reshape(din // 128, 128, dout).transpose(1, 0, 2).reshape(128, -1))


class _Op:
    __slots__ = ("eng", "fn", "deps", "idx", "is_dma", "queue", "slot",
                 "event", "tag", "force_wait", "no_event")

    def __init__(self, eng, fn, tag=""):
        self.eng = eng
        self.fn = fn
        self.deps = set()
        self.idx = -1
        self.is_dma = False
        self.queue = None
        self.slot = None
        self.event = None
        self.tag = tag
        self.force_wait = False
        self.no_event = False


class P:
    ENGINES = ("tensor", "vector", "scalar", "gpsimd", "sync")

    def __init__(self, nc):
        self.nc = nc
        self.ops = []
        self.writer = {}
        self.readers = {}
        self.last_on = {}
        self.dmaq = {"sync": 0, "gpsimd": 0}

    @staticmethod
    def _flat(rs):
        out = []
        for r in rs:
            if isinstance(r, str):
                out.append(r)
            else:
                out.extend(r)
        return out

    def op(self, eng, fn, reads=(), writes=(), tag="", extra_deps=()):
        reads = self._flat(reads)
        writes = self._flat(writes)
        o = _Op(eng, fn, tag)
        deps = set(extra_deps)
        for r in reads:
            w = self.writer.get(r)
            if w is not None:
                deps.add(w)
        for r in writes:
            w = self.writer.get(r)
            if w is not None:
                deps.add(w)
            deps.update(self.readers.get(r, ()))
        deps.discard(o)
        o.deps = deps
        for r in reads:
            self.readers.setdefault(r, set()).add(o)
        for r in writes:
            self.writer[r] = o
            self.readers[r] = set()
        o.idx = len(self.ops)
        self.ops.append(o)
        self.last_on[eng] = o
        return o

    def dma(self, queue, out_ap, in_ap, reads=(), writes=(), tag=""):
        eng = "gpsimd" if queue == "gpsimd" else "sync"
        slot = self.dmaq[queue] % NSLOT
        self.dmaq[queue] += 1
        slotres = f"@dma.{queue}.{slot}"
        o = self.op(eng, lambda e, oa=out_ap, ia=in_ap: e.dma_start(oa, ia),
                    reads=tuple(reads), writes=tuple(writes) + (slotres,), tag=tag)
        o.is_dma = True
        o.queue = queue
        o.slot = slot
        return o

    def idma(self, out_ap, out_idx, in_ap, in_idx, *, elem_off=0, bc=None,
             cce=None, reads=(), writes=(), tag=""):
        slot = self.dmaq["gpsimd"] % NSLOT
        self.dmaq["gpsimd"] += 1
        slotres = f"@dma.gpsimd.{slot}"
        oo = (IndirectOffsetOnAxis(ap=out_idx[0], axis=out_idx[1])
              if out_idx is not None else None)
        io = (IndirectOffsetOnAxis(ap=in_idx[0], axis=in_idx[1])
              if in_idx is not None else None)
        kw = {}
        if bc is not None:
            kw = dict(bounds_check=bc, oob_is_err=False)
        if cce is not None:
            kw["compute_op"] = cce
        def fn(e, oa=out_ap, ia=in_ap, oo_=oo, io_=io, eo=elem_off, kw_=kw):
            return e.indirect_dma_start(oa, oo_, ia, io_, element_offset=eo, **kw_)
        o = self.op("gpsimd", fn, reads=tuple(reads),
                    writes=tuple(writes) + (slotres,), tag=tag)
        o.is_dma = True
        o.queue = "gpsimd"
        o.slot = slot
        return o

    def mm_group(self, psum_ap, pairs, reads=(), writes=(), tag=""):
        return self.mm_multi([(psum_ap, pairs)], reads=reads, writes=writes, tag=tag)

    def mm_multi(self, groups, reads=(), writes=(), tag=""):
        gg = tuple((ps, tuple(pairs)) for ps, pairs in groups)

        def fn(t, gg=gg):
            inst = None
            for ps, prs in gg:
                n = len(prs)
                for k, (l, r) in enumerate(prs):
                    inst = t.matmul(ps, l, r, start=(k == 0), stop=(k == n - 1))
            return inst
        return self.op("tensor", fn, reads=reads, writes=writes, tag=tag)

    def barrier(self):
        s1 = []
        for e in self.ENGINES:
            dep = self.last_on.get(e)
            o = self.op(e, lambda x: x.nop(),
                        extra_deps=(dep,) if dep is not None else ())
            o.force_wait = True
            s1.append(o)
        deps = set(s1)
        for w in self.writer.values():
            deps.add(w)
        for rl in self.readers.values():
            deps.update(rl)
        b = self.op("sync", lambda x: x.nop(), extra_deps=deps, tag="barrier")
        for r in list(self.writer):
            self.writer[r] = b
            self.readers[r] = set()
        return b

    def emit(self):
        nc = self.nc
        needed = set()
        for o in self.ops:
            for d in o.deps:
                if (d.eng == "tensor" and o.eng == "tensor" and not d.is_dma
                        and not o.force_wait):
                    continue
                if d.no_event:
                    continue
                needed.add(d)
        eng_sem = {e: nc.alloc_semaphore(name=f"es_{e}") for e in self.ENGINES}
        dma_sem = {q: [nc.alloc_semaphore(name=f"ds_{q}{s}") for s in range(NSLOT)]
                   for q in self.dmaq}
        eng_cnt = {e: 0 for e in self.ENGINES}
        slot_cnt = {q: [0] * NSLOT for q in self.dmaq}
        for o in self.ops:
            if o.is_dma:
                slot_cnt[o.queue][o.slot] += 1
                o.event = (dma_sem[o.queue][o.slot], 16 * slot_cnt[o.queue][o.slot])
            elif o in needed:
                eng_cnt[o.eng] += 1
                o.event = (eng_sem[o.eng], eng_cnt[o.eng])
        streams = {e: [o for o in self.ops if o.eng == e] for e in self.ENGINES}
        waited = {e: {} for e in self.ENGINES}

        def emit_stream(e, eng_handle):
            w = waited[e]
            for o in streams[e]:
                req = {}
                for d in o.deps:
                    if (d.eng == "tensor" and e == "tensor" and not d.is_dma
                            and not o.force_wait):
                        continue
                    if d.no_event:
                        continue
                    sem, val = d.event
                    if req.get(sem, 0) < val:
                        req[sem] = val
                for sem, val in req.items():
                    if w.get(sem, 0) < val:
                        eng_handle.wait_ge(sem, val)
                        w[sem] = val
                inst = o.fn(eng_handle)
                try:
                    _cache.setdefault("instmap", {})[inst.ins.name] = o
                except Exception:
                    pass
                if o.event is not None:
                    sem, _ = o.event
                    inst.then_inc(sem, 16 if o.is_dma else 1)

        with nc.Block() as block:
            block.tensor(lambda t: emit_stream("tensor", t))
            block.vector(lambda v: emit_stream("vector", v))
            block.scalar(lambda s: emit_stream("scalar", s))
            block.gpsimd(lambda g: emit_stream("gpsimd", g))
            block.sync(lambda sy: emit_stream("sync", sy))
        n_waits = sum(len(s) for s in streams.values())
        return n_waits


def _ap(h, off, dims):
    return bass.AP(h, off, [list(d) for d in dims])


def build_program():
    key = ("nc",) + tuple(sorted(_LIMIT.items()))
    if key in _cache:
        return _cache[key]
    nc = bass.Bass("TRN2", target_bir_lowering=False)
    dt_in = {}

    def din(name, shape, dt=F32):
        h = nc.dram_tensor(name, list(shape), dt, kind="ExternalInput")
        dt_in[name] = h
        return h

    x_h = din("x", [TOK, D])
    wq_h = din("wq_t", [128, 8 * D])
    wk_h = din("wk_t", [128, 8 * D])
    wv_h = din("wv_t", [128, 8 * 2048])
    wg_h = din("wg_t", [128, 8 * 2048])
    wo_h = din("wo_t", [128, 16 * D])
    w1_h = din("w1_all", [NEXP, 128, 8 * D_FF])
    w2_h = din("w2_all", [NEXP, 128, 32 * D])
    b1_h = din("b1_r", [128, NEXP * 32])
    b2_h = din("b2_all", [NEXP, D])
    rw_h = din("rw_t", [128, 64])
    rb_h = din("rb_bc", [128, 64])
    gn_h = din("gnorm_r", [128, 16])
    id_h = din("ident", [128, 128], BF16)
    cos_h = din("cos_tm", [T, 128])
    sin_h = din("sin_tm", [T, 128])
    dqds_h = din("dq_ds", [128, 8])
    mask_h = din("maskT", [128, H * C])

    b1s_h = din("b1s", [NSPEC * 128, 32])
    ls_h = din("lstrict", [128, 128], BF16)
    o128c_h = din("ones128c", [128, 1], BF16)
    one1_h = din("one1", [1, 1], BF16)
    bl64_h = din("blockL64", [64, 64], BF16)
    bf64_h = din("blockF64", [64, 64], BF16)
    id64_h = din("ident64", [64, 64])
    o1x128f_h = din("ones1x128f", [1, 128])
    o1x128b_h = din("ones1x128b", [1, 128], BF16)
    iota_h = din("iota128h", [128, 1])
    pio_h = din("piota48h", [1, 48])

    og_s = nc.dram_tensor("og_s", [TOK, H * DV], BF16, kind="Internal")
    x1_s = nc.dram_tensor("x1_s", [TOK, D], F32, kind="Internal")
    x1n_s = nc.dram_tensor("x1n_s", [TOK, D], BF16, kind="Internal")
    meta0_s = nc.dram_tensor("meta0_s", [TOK, 20], F32, kind="Internal")
    meta_s = nc.dram_tensor("meta_s", [PAGES * PSLOT, 2], F32, kind="Internal")
    w1s_bf = nc.dram_tensor("w1s_bf", [NSPEC * 128, 32768], BF16, kind="Internal")
    w2s_bf = nc.dram_tensor("w2s_bf", [NSPEC * 128, 32768], BF16, kind="Internal")
    ws1_bf = nc.dram_tensor("ws1_bf", [128, 32768], BF16, kind="Internal")
    ws2_bf = nc.dram_tensor("ws2_bf", [128, 32768], BF16, kind="Internal")
    cnt_s = nc.dram_tensor("cnt_s", [64], F32, kind="Internal")
    out_h = nc.dram_tensor("out", [TOK, D], F32, kind="ExternalOutput")

    lamC = _consts()["lamC"]

    p = P(nc)

    ident = nc.alloc_sbuf_tensor("identsb", [128, 128], BF16)
    dqds = nc.alloc_sbuf_tensor("dqds", [128, 8], F32)
    maskT = nc.alloc_sbuf_tensor("maskTsb", [128, H * C], F32)
    rb_bc = nc.alloc_sbuf_tensor("rbbc", [128, 64], F32)
    b1r = nc.alloc_sbuf_tensor("b1r", [128, NEXP * 32], F32)
    b2bf = nc.alloc_sbuf_tensor("b2bf", [NEXP, D], BF16)
    rwbf = nc.alloc_sbuf_tensor("rwbf", [128, 64], BF16)
    gnr = nc.alloc_sbuf_tensor("gnr", [128, 16], F32)
    cosb = [nc.alloc_sbuf_tensor(f"cosb{i}", [128, A_TILE * 128], F32) for i in range(2)]
    sinb = [nc.alloc_sbuf_tensor(f"sinb{i}", [128, A_TILE * 128], F32) for i in range(2)]
    stat = nc.alloc_sbuf_tensor("stat", [128, 32], F32)
    rms4 = nc.alloc_sbuf_tensor("rms4", [128, 16], F32)
    lstr = nc.alloc_sbuf_tensor("lstr", [128, 128], BF16)
    o128c = nc.alloc_sbuf_tensor("o128c", [128, 1], BF16)
    one1 = nc.alloc_sbuf_tensor("one1sb", [1, 1], BF16)
    bl64 = nc.alloc_sbuf_tensor("bl64", [64, 64], BF16)
    bf64 = nc.alloc_sbuf_tensor("bf64", [64, 64], BF16)
    id64 = nc.alloc_sbuf_tensor("id64", [64, 64], F32)
    o1x128f = nc.alloc_sbuf_tensor("o1x128f", [1, 128], F32)
    o1x128b = nc.alloc_sbuf_tensor("o1x128b", [1, 128], BF16)
    iota128 = nc.alloc_sbuf_tensor("iota128", [128, 1], F32)
    pio48 = nc.alloc_sbuf_tensor("pio48", [1, 48], F32)
    running64 = nc.alloc_sbuf_tensor("running64", [64, 1], F32)
    pbB = nc.alloc_sbuf_tensor("pbB", [128, 8], F32)
    wIdxu = nc.alloc_sbuf_tensor("wIdxu", [128, 48], U32)
    rowscr = nc.alloc_sbuf_tensor("rowscr", [1, 192], F32)
    rowbf = nc.alloc_sbuf_tensor("rowbf", [1, 64], BF16)
    ipg = nc.alloc_sbuf_tensor("ipg", [1, 8], I32)
    wcb = [nc.alloc_sbuf_tensor(f"wcb{i}", [128, 2048], BF16) for i in range(2)]
    fill9 = nc.alloc_sbuf_tensor("fill9", [128, 8], F32)
    rBC = nc.gpsimd.alloc_register("rBC")
    rBCW = nc.gpsimd.alloc_register("rBCW")

    PA = nc.alloc_psum_tensor("PA", [128, 1024], F32)
    PB = nc.alloc_psum_tensor("PB", [128, 1024], F32)
    P0 = nc.alloc_psum_tensor("P0", [128, 512], F32)
    P1 = nc.alloc_psum_tensor("P1", [128, 512], F32)
    PT = nc.alloc_psum_tensor("PTb", [128, 1024], BF16)
    PT2 = nc.alloc_psum_tensor("PTb2", [128, 1024], BF16)
    PTS = [(PT, "PT"), (PT2, "PT2")]

    region_base = nc.sbuf_base
    region_top = nc.sbuf_top

    class Region:
        def __init__(self):
            self.cur = region_base

        def t(self, name, shape, dt):
            per_part = int(np.prod(shape[1:])) * mybir.dt.size(dt)
            off = (self.cur + 31) // 32 * 32
            h = nc.alloc_sbuf_tensor_at(name, list(shape), dt, offset=off)
            self.cur = off + per_part
            assert self.cur <= region_top, (name, self.cur, region_top)
            return h

    p.dma("sync", ident[:, :], id_h[:, :], writes=("ident",))
    p.dma("sync", dqds[:, :], dqds_h[:, :], writes=("dqds",))
    p.dma("sync", maskT[:, :], mask_h[:, :], writes=("maskT",))
    p.dma("sync", rb_bc[:, :], rb_h[:, :], writes=("rbbc",))
    p.dma("sync", b1r[:, :], b1_h[:, :], writes=("b1r",))
    p.dma("gpsimd", b2bf[:, :], b2_h[:, :], writes=("b2bf",))
    p.dma("gpsimd", rwbf[:, :], rw_h[:, :], writes=("rwbf",))
    p.dma("sync", gnr[:, :], gn_h[:, :], writes=("gnr",))
    p.op("gpsimd", lambda e: e.memset(stat[:, 9:10], EPS), writes=("epsc",))
    p.dma("sync", lstr[:, :], ls_h[:, :], writes=("lstr",))
    p.dma("sync", o128c[:, :], o128c_h[:, :], writes=("o128c",))
    p.dma("sync", one1[:, :], one1_h[:, :], writes=("one1",))
    p.dma("sync", bl64[:, :], bl64_h[:, :], writes=("bl64",))
    p.dma("sync", bf64[:, :], bf64_h[:, :], writes=("bf64",))
    p.dma("sync", id64[:, :], id64_h[:, :], writes=("id64",))
    p.dma("sync", o1x128f[:, :], o1x128f_h[:, :], writes=("o1x128f",))
    p.dma("sync", o1x128b[:, :], o1x128b_h[:, :], writes=("o1x128b",))
    p.dma("sync", iota128[:, :], iota_h[:, :], writes=("iota128",))
    p.dma("sync", pio48[:, :], pio_h[:, :], writes=("pio48",))
    p.op("gpsimd", lambda e: e.reg_mov(rBC, TOK - 1),
         writes=("bcreg",)).no_event = True
    p.op("gpsimd", lambda e: e.reg_mov(rBCW, NSPEC * 128 - 1),
         writes=("bcreg2",)).no_event = True
    p.op("gpsimd", lambda e: e.memset(running64[:, :], 0.0), writes=("running64",))
    p.op("gpsimd", lambda e: e.memset(fill9[:, :], 1.0e9), writes=("fill9",))
    p.dma("sync", _ap(meta_s, 0, [[8, 128], [1024, PAGES], [1, 8]]),
          _ap(fill9, 0, [[8, 128], [0, PAGES], [1, 8]]),
          reads=("fill9",), writes=("meta_hbm",), tag="metafill")

    ra = Region()
    x_t = ra.t("x_t", [128, A_TILE * D], F32)
    xn_tm = ra.t("xn_tm", [128, D], BF16)
    xn_fm = ra.t("xn_fm", [128, D], BF16)
    qkraw = ra.t("qkraw", [128, D], F32)
    rt1 = ra.t("rt1", [128, 512], F32)
    rt2 = ra.t("rt2", [128, 512], F32)
    q_tm = ra.t("q_tm", [128, D], BF16)
    q_fm = ra.t("q_fm", [128, D], BF16)
    k_rope = ra.t("k_rope", [128, D], BF16)
    k_state = ra.t("k_state", [128, D], BF16)
    k_fm = ra.t("k_fm", [128, D], BF16)
    v_tm = ra.t("v_tm", [128, H * DV], BF16)
    atm = ra.t("atm", [128, H * C], BF16)
    gate = ra.t("gate", [128, H * DV], F32)
    o_sb = ra.t("o_sb", [128, H * DV], F32)
    o_sq = ra.t("o_sq", [128, DV], F32)
    og = ra.t("og", [128, H * DV], BF16)
    S = ra.t("S", [128, H * 2 * DV], F32)
    S_bf = ra.t("S_bf", [128, H * 2 * DV], BF16)
    lnsq_a = ra.t("lnsq_a", [128, D], F32)
    wq_bf = ra.t("wq_bf", [128, 8 * D], BF16)
    wk_bf = ra.t("wk_bf", [128, 8 * D], BF16)
    wv_bf = ra.t("wv_bf", [128, 8 * 2048], BF16)
    wg_bf = ra.t("wg_bf", [128, 8 * 2048], BF16)

    def emit_wcast(j):
        if not _LIMIT["w"] or j >= 288:
            return
        k, qr = j // 4, j % 4
        ei, rest = k // 8, k % 8
        which, pc = rest // 4, rest % 4
        src = (w1_h, w2_h)[which]
        if which == 0:
            ap_in = _ap(src, ei * 128 * 32768 + qr * 2 * 4096 + pc * 1024,
                        [[32768, 128], [4096, 2], [1, 1024]])
        else:
            ap_in = _ap(src, ei * 128 * 32768 + pc * 8192 + qr * 2048,
                        [[32768, 128], [1, 2048]])
        if ei == 0:
            dst = (ws1_bf, ws2_bf)[which]
            doff = pc * 8192 + qr * 2048
            res = f"wsbf.{which}.{pc}.{qr}"
        else:
            dst = (w1s_bf, w2s_bf)[which]
            doff = (ei - 1) * 128 * 32768 + pc * 8192 + qr * 2048
            res = f"wspec.{which}.{pc}.{ei}.{qr}"
        s = j % 2
        p.dma("gpsimd", _ap(wcb[s], 0, [[2048, 128], [1024, 2], [1, 1024]])
              if which == 0 else wcb[s][:, :], ap_in,
              writes=(f"wcb{s}",), tag="wcL")
        p.dma("sync", _ap(dst, doff, [[32768, 128], [1, 2048]]), wcb[s][:, :],
              reads=(f"wcb{s}",), writes=(res,), tag="wcS")

    p.dma("gpsimd", wq_bf[:, :], wq_h[:, :], writes=("wq",))
    p.dma("gpsimd", wk_bf[:, :], wk_h[:, :], writes=("wk",))
    for hlf in range(2):
        p.dma("gpsimd", wv_bf[:, hlf * 8192:(hlf + 1) * 8192],
              wv_h[:, hlf * 8192:(hlf + 1) * 8192], writes=(f"wv{hlf}",))
        p.dma("gpsimd", wg_bf[:, hlf * 8192:(hlf + 1) * 8192],
              wg_h[:, hlf * 8192:(hlf + 1) * 8192], writes=(f"wg{hlf}",))

    def emit_ln(x_ap, xres, out_ap, out_res, sq_scratch_ap, sq_scr_res):
        p.op("scalar", lambda e, i=x_ap, o=sq_scratch_ap: e.activation(
            o, i, AF.Square, accum_out=stat[:, 0:1]),
            reads=(xres,), writes=(sq_scr_res, "st0"), tag="ln_sq")
        p.op("vector", lambda e, i=x_ap: e.reduce_sum(
            stat[:, 1:2], i, AX.X, negate=True),
            reads=(xres,), writes=("st1",), tag="ln_sum")
        p.op("vector", lambda e: e.tensor_scalar(
            stat[:, 2:3], stat[:, 1:2], 1.0 / D, None, ALU.mult),
            reads=("st1",), writes=("st2",))
        p.op("vector", lambda e: e.tensor_mul(stat[:, 3:4], stat[:, 2:3], stat[:, 2:3]),
             reads=("st2",), writes=("st3",))
        p.op("vector", lambda e: e.tensor_scalar(
            stat[:, 4:5], stat[:, 0:1], 1.0 / D, None, ALU.mult),
            reads=("st0",), writes=("st4",))
        p.op("vector", lambda e: e.tensor_sub(stat[:, 5:6], stat[:, 4:5], stat[:, 3:4]),
             reads=("st3", "st4"), writes=("st5",))
        p.op("scalar", lambda e: e.activation(stat[:, 6:7], stat[:, 5:6], AF.Ln,
                                              bias=stat[:, 9:10]),
             reads=("st5", "epsc"), writes=("st6",))
        p.op("scalar", lambda e: e.activation(stat[:, 7:8], stat[:, 6:7], AF.Exp, scale=-0.5),
             reads=("st6",), writes=("st7",))
        p.op("vector", lambda e: e.tensor_mul(stat[:, 8:9], stat[:, 2:3], stat[:, 7:8]),
             reads=("st2", "st7"), writes=("st8",))
        p.op("scalar", lambda e, i=x_ap, o=out_ap: e.activation(
            o, i, AF.Identity, bias=stat[:, 8:9], scale=stat[:, 7:8]),
            reads=(xres, "st7", "st8"), writes=(out_res,), tag="ln_out")

    def emit_transpose8(src, src_res, dst, dst_res, nchunk=8, slot=0):
        rr = (src_res,) if isinstance(src_res, str) else tuple(src_res)
        pt, ptres = PTS[slot % 2]
        def fn(t, s=src, n=nchunk, pt_=pt):
            inst = None
            for cc in range(n):
                inst = t.transpose(pt_[:, cc * 128:(cc + 1) * 128],
                                   s[:, cc * 128:(cc + 1) * 128], ident[:, :])
            return inst
        p.op("tensor", fn, reads=rr + ("ident",), writes=(ptres,), tag="T8")
        p.op("vector", lambda e, d=dst, n=nchunk, pt_=pt: e.tensor_copy(
            d[:, :n * 128], pt_[:, :n * 128]),
            reads=(ptres,), writes=(dst_res,), tag="T8evac")

    def emit_rope(src_f32, src_res, dst_bf, dst_res, cosap, sinap, cs_res):
        x1 = _ap(src_f32, 0, [[1024, 128], [256, 4], [1, 128]])
        x2 = _ap(src_f32, 128, [[1024, 128], [256, 4], [1, 128]])
        o1 = _ap(dst_bf, 0, [[1024, 128], [256, 4], [1, 128]])
        o2 = _ap(dst_bf, 128, [[1024, 128], [256, 4], [1, 128]])
        t1 = _ap(rt1, 0, [[512, 128], [128, 4], [1, 128]])
        t2 = _ap(rt2, 0, [[512, 128], [128, 4], [1, 128]])
        p.op("vector", lambda e, a=x1, b=cosap, o=t1: e.tensor_mul(o, a, b),
             reads=(src_res,) + tuple(cs_res), writes=("rt1",))
        p.op("vector", lambda e, a=x2, b=sinap, o=t2: e.tensor_mul(o, a, b),
             reads=(src_res,) + tuple(cs_res), writes=("rt2",))
        p.op("vector", lambda e, o=o1: e.tensor_sub(o, t1, t2),
             reads=("rt1", "rt2"), writes=(dst_res + ".1",))
        p.op("vector", lambda e, a=x1, b=sinap, o=t1: e.tensor_mul(o, a, b),
             reads=(src_res,) + tuple(cs_res), writes=("rt1",))
        p.op("vector", lambda e, a=x2, b=cosap, o=t2: e.tensor_mul(o, a, b),
             reads=(src_res,) + tuple(cs_res), writes=("rt2",))
        p.op("vector", lambda e, o=o2: e.tensor_add(o, t1, t2),
             reads=("rt1", "rt2"), writes=(dst_res + ".2",))

    for blk in range(_LIMIT["a1"]):
        if blk < 64:
            emit_wcast(blk)
        bt = blk // A_TILE
        bi = blk % A_TILE
        seq_blk = blk % BLK_PER_SEQ
        pg = bt % 2
        if blk % BLK_PER_SEQ == 0:
            p.op("gpsimd", lambda e: e.memset(S[:, :], 0.0), writes=("S",))
            p.op("gpsimd", lambda e: e.memset(S_bf[:, :], 0.0),
                 writes=tuple(f"S_bf.{h}" for h in range(H)))
        if bi == 0:
            p.dma("sync", _ap(x_t, 0, [[A_TILE * D, 128], [D, A_TILE], [1, D]]),
                  _ap(x_h, blk * C * D, [[D, 128], [C * D, A_TILE], [1, D]]),
                  writes=("x_t",), tag=f"xload{bt}")
            pos0 = (seq_blk) * C
            p.dma("sync", _ap(cosb[pg], 0, [[A_TILE * 128, 128], [128, A_TILE], [1, 128]]),
                  _ap(cos_h, pos0 * 128, [[128, 128], [C * 128, A_TILE], [1, 128]]),
                  writes=(f"cos{pg}",))
            p.dma("sync", _ap(sinb[pg], 0, [[A_TILE * 128, 128], [128, A_TILE], [1, 128]]),
                  _ap(sin_h, pos0 * 128, [[128, 128], [C * 128, A_TILE], [1, 128]]),
                  writes=(f"sin{pg}",))
        x_b = x_t[:, bi * D:(bi + 1) * D]
        cosap = _ap(cosb[pg], bi * 128, [[A_TILE * 128, 128], [0, 4], [1, 128]])
        sinap = _ap(sinb[pg], bi * 128, [[A_TILE * 128, 128], [0, 4], [1, 128]])
        csres = (f"cos{pg}", f"sin{pg}")

        emit_ln(x_b, "x_t", xn_tm[:, :], "xn_tm", lnsq_a[:, :], "lnsq_a")
        emit_transpose8(xn_tm, "xn_tm", xn_fm, "xn_fm", slot=0)

        p.mm_multi([(PA[:, n * 512:(n + 1) * 512],
                     [(xn_fm[:, kc * 128:(kc + 1) * 128],
                       wq_bf[:, kc * D + n * 512: kc * D + (n + 1) * 512])
                      for kc in range(8)]) for n in range(2)],
                   reads=("xn_fm", "wq"), writes=("PA0", "PA1"), tag="Qmm")
        p.mm_multi([(PB[:, n * 512:(n + 1) * 512],
                     [(xn_fm[:, kc * 128:(kc + 1) * 128],
                       wk_bf[:, kc * D + n * 512: kc * D + (n + 1) * 512])
                      for kc in range(8)]) for n in range(2)],
                   reads=("xn_fm", "wk"), writes=("PB0", "PB1"), tag="Kmm")
        for h in range(H):
            sl = slice(h * 256, (h + 1) * 256)
            p.op("scalar", lambda e, s=sl, h_=h: e.activation(
                qkraw[:, s], _ap(PA, s.start, [[1024, 128], [1, 256]]), AF.Identity,
                scale=dqds[:, h_:h_ + 1]),
                reads=("PA0", "PA1", "dqds"), writes=("qkraw",), tag="Qevac")
        emit_rope(qkraw, "qkraw", q_tm, "q_tm", cosap, sinap, csres)

        for pair in range(2):
            p.mm_multi([((P0, P1)[i][:, :],
                         [(xn_fm[:, kc * 128:(kc + 1) * 128],
                           wv_bf[:, kc * 2048 + (pair * 2 + i) * 512:
                                 kc * 2048 + (pair * 2 + i + 1) * 512])
                          for kc in range(8)]) for i in range(2)],
                       reads=("xn_fm", "wv0", "wv1"),
                       writes=("P0b", "P1b"), tag="Vmm")
            for i in range(2):
                s4 = pair * 2 + i
                p.op("scalar", lambda e, b=(P0, P1)[i], s_=s4: e.activation(
                    v_tm[:, s_ * 512:(s_ + 1) * 512], b[:, :], AF.Identity),
                    reads=(("P0b", "P1b")[i],), writes=(f"v.{s4}",), tag="Vevac")
        emit_transpose8(q_tm, ("q_tm.1", "q_tm.2"), q_fm, "q_fm", slot=1)

        p.op("scalar", lambda e: e.activation(qkraw[:, :], PB[:, :], AF.Identity),
             reads=("PB0", "PB1"), writes=("qkraw",), tag="Kevac")
        for pair in range(2):
            p.mm_multi([((P0, P1)[i][:, :],
                         [(xn_fm[:, kc * 128:(kc + 1) * 128],
                           wg_bf[:, kc * 2048 + (pair * 2 + i) * 512:
                                 kc * 2048 + (pair * 2 + i + 1) * 512])
                          for kc in range(8)]) for i in range(2)],
                       reads=("xn_fm", "wg0", "wg1"),
                       writes=("P0b", "P1b"), tag="Gmm")
            for i in range(2):
                s4 = pair * 2 + i
                p.op("scalar", lambda e, b=(P0, P1)[i], s_=s4: e.activation(
                    gate[:, s_ * 512:(s_ + 1) * 512], b[:, :], AF.Silu),
                    reads=(("P0b", "P1b")[i],), writes=("gate",), tag="silu")
        emit_rope(qkraw, "qkraw", k_rope, "k_rope", cosap, sinap, csres)
        for h in range(H):
            sl = slice(h * 256, (h + 1) * 256)
            p.op("scalar", lambda e, s=sl, h_=h: e.activation(
                k_state[:, s], k_rope[:, s], AF.Identity, scale=dqds[:, 4 + h_:5 + h_]),
                reads=("k_rope.1", "k_rope.2", "dqds"), writes=("k_state",), tag="Kst")
        emit_transpose8(k_rope, ("k_rope.1", "k_rope.2"), k_fm, "k_fm", slot=0)

        p.mm_multi([(PA[:, h * 128:(h + 1) * 128],
                     [(k_fm[:, cc * 128:(cc + 1) * 128],
                       q_fm[:, cc * 128:(cc + 1) * 128]) for cc in (2 * h, 2 * h + 1)])
                    for h in range(H)],
                   reads=("k_fm", "q_fm"), writes=("PA0",), tag="ATmm")
        p.op("vector", lambda e: e.tensor_mul(atm[:, :], PA[:, 0:512], maskT[:, :]),
             reads=("PA0", "maskT"), writes=("atm",), tag="mask")

        for h in range(H):
            ob = (P0, P1)[h % 2]
            obres = ("P0b", "P1b")[h % 2]
            pairs = [(atm[:, h * 128:(h + 1) * 128], v_tm[:, h * 512:(h + 1) * 512])]
            pairs += [(q_fm[:, cc * 128:(cc + 1) * 128],
                       S_bf[:, h * 1024 + j * 512: h * 1024 + (j + 1) * 512])
                      for j, cc in enumerate((2 * h, 2 * h + 1))]
            p.mm_group(ob[:, :], pairs,
                       reads=("atm", f"v.{h}", "q_fm", f"S_bf.{h}"),
                       writes=(obres,), tag="Omm")
            p.op("scalar", lambda e, b=ob, h_=h: e.activation(
                o_sb[:, h_ * 512:(h_ + 1) * 512], b[:, :], AF.Identity),
                reads=(obres,), writes=(f"o.{h}",), tag="Oevac")
            p.op("scalar", lambda e, h_=h: e.activation(
                o_sq[:, :], o_sb[:, h_ * 512:(h_ + 1) * 512], AF.Square,
                accum_out=rms4[:, h_:h_ + 1]),
                reads=(f"o.{h}",), writes=("o_sq", f"rms.{h}"), tag="Osq")
            p.mm_multi([(PB[:, j * 512:(j + 1) * 512],
                         [(k_state[:, h * 256 + j * 128: h * 256 + (j + 1) * 128],
                           v_tm[:, h * 512:(h + 1) * 512])]) for j in range(2)],
                       reads=("k_state", f"v.{h}"), writes=("PB0", "PB1"), tag="Smm")
            for j in range(2):
                sl = slice(h * 1024 + j * 512, h * 1024 + (j + 1) * 512)
                ps = PB[:, j * 512:(j + 1) * 512]
                p.op("vector", lambda e, s=sl, ps_=ps, l=float(lamC[h]):
                     e.scalar_tensor_tensor(S[:, s], S[:, s], l, ps_, ALU.mult, ALU.add),
                     reads=(("PB0", "PB1")[j], "S"), writes=("S",), tag="Supd")

        p.op("scalar", lambda e: e.activation(rms4[:, 4:8], rms4[:, 0:4], AF.Ln,
                                              scale=1.0 / DV, bias=stat[:, 9:10]),
             reads=tuple(f"rms.{h}" for h in range(H)) + ("epsc",),
             writes=("rmsl",), tag="rmsln")
        p.op("scalar", lambda e: e.activation(rms4[:, 8:12], rms4[:, 4:8], AF.Exp,
                                              scale=-0.5),
             reads=("rmsl",), writes=("rstd4",), tag="rmsexp")
        for h in range(H):
            sl = slice(h * 512, (h + 1) * 512)
            p.op("vector", lambda e, s=sl, h_=h: e.scalar_tensor_tensor(
                og[:, s], o_sb[:, s], rms4[:, 8 + h_:9 + h_], gate[:, s],
                ALU.mult, ALU.mult),
                reads=(f"o.{h}", "rstd4", "gate"), writes=("og",), tag="og")
        for h in range(H):
            sl = slice(h * 1024, (h + 1) * 1024)
            p.op("scalar", lambda e, s=sl: e.activation(S_bf[:, s], S[:, s], AF.Identity),
                 reads=("S",), writes=(f"S_bf.{h}",), tag="Sbf")
        p.dma("sync", _ap(og_s, blk * C * 2048, [[2048, 128], [1, 2048]]),
              og[:, :], reads=("og",), writes=("og_hbm",), tag="ogst")

    p.barrier()

    ra2 = Region()
    wo_bf = ra2.t("wo_bf", [128, 16 * D], BF16)
    x_t2 = ra2.t("x_t2", [128, A_TILE * D], F32)
    og_in = [ra2.t(f"og_in{i}", [128, H * DV], BF16) for i in range(2)]
    og_fm = ra2.t("og_fm", [128, 2048], BF16)
    x1o = [ra2.t(f"x1o{i}", [128, D], F32) for i in range(2)]

    p.dma("gpsimd", wo_bf[:, :], wo_h[:, :], writes=("wo_raw",))
    def wo_fold(e):
        inst = None
        for cc in range(16):
            inst = e.tensor_scalar(wo_bf[:, cc * D:(cc + 1) * D],
                                   wo_bf[:, cc * D:(cc + 1) * D],
                                   gnr[:, cc:cc + 1], None, ALU.mult)
        return inst
    p.op("vector", wo_fold, reads=("wo_raw", "gnr"), writes=("wo",), tag="wofold")

    for blk in range(_LIMIT["a2"]):
        if blk < 64:
            emit_wcast(64 + blk)
        bt, bi = blk // A_TILE, blk % A_TILE
        pg = blk % 2
        if bi == 0:
            p.dma("sync", _ap(x_t2, 0, [[A_TILE * D, 128], [D, A_TILE], [1, D]]),
                  _ap(x_h, blk * C * D, [[D, 128], [C * D, A_TILE], [1, D]]),
                  writes=("x_t2",), tag="x2load")
        if _LIMIT["a2p"] & 1:
            p.dma("sync", og_in[pg][:, :],
                  _ap(og_s, blk * C * 2048, [[2048, 128], [1, 2048]]),
                  reads=("og_hbm",), writes=(f"ogin{pg}",), tag="ogld")
        for half in range(2 if _LIMIT["a2p"] & 2 else 0):
            def fnT(t, src=og_in[pg], hf=half):
                inst = None
                for cc in range(8):
                    inst = t.transpose(PT[:, cc * 128:(cc + 1) * 128],
                                       src[:, (hf * 8 + cc) * 128:(hf * 8 + cc + 1) * 128],
                                       ident[:, :])
                return inst
            p.op("tensor", fnT, reads=(f"ogin{pg}", "ident"), writes=("PT",), tag="ogT")
            p.op("vector", lambda e, hf=half: e.tensor_copy(
                og_fm[:, hf * 1024:(hf + 1) * 1024], PT[:, :]),
                reads=("PT",), writes=(f"ogfm{half}",), tag="ogTev")
        pp = (PA, PB)[pg]
        ppres = (("PA0", "PA1"), ("PB0", "PB1"))[pg]
        for n in range(2 if _LIMIT["a2p"] & 4 else 0):
            pairs = [(og_fm[:, cc * 128:(cc + 1) * 128],
                      wo_bf[:, cc * D + n * 512: cc * D + (n + 1) * 512])
                     for cc in range(16)]
            p.mm_group(pp[:, n * 512:(n + 1) * 512], pairs,
                       reads=("ogfm0", "ogfm1", "wo"), writes=(ppres[n],), tag="WOmm")
        if _LIMIT["a2p"] & 8:
            p.op("vector", lambda e, pp_=pp, xb=x_t2[:, bi * D:(bi + 1) * D],
                 o=x1o[pg]: e.tensor_add(o[:, :], pp_[:, :], xb),
                 reads=ppres + ("x_t2",), writes=(f"x1o{pg}",), tag="resid")
            p.dma("sync", _ap(x1_s, blk * C * D, [[D, 128], [1, D]]), x1o[pg][:, :],
                  reads=(f"x1o{pg}",), writes=("x1_hbm",), tag="x1st")

    p.barrier()

    rb = Region()
    x1n_fm = rb.t("x1n_fm", [128, 8 * TT], BF16)
    acc = rb.t("acc", [128, TT * 8], F32)
    h_bf = rb.t("h_bf", [128, 32 * TT], BF16)
    wpc = [rb.t(f"wpc{i}", [128, 8192], BF16) for i in range(2)]
    x1b = [rb.t(f"x1b{i}", [128, D], F32) for i in range(2)]
    outb = [rb.t(f"outb{i}", [128, D], F32) for i in range(2)]
    x1n_tm = rb.t("x1n_tm", [128, D], BF16)
    lnsq = rb.t("lnsq", [128, D], F32)
    lg = rb.t("lg", [128, 64], F32)
    r_sm = rb.t("r_sm", [128, 64], F32)
    sm8 = rb.t("sm8", [128, 64], F32)
    comb = rb.t("comb", [128, 64], F32)
    comb9 = rb.t("comb9", [128, 72], BF16)
    cw_sb = rb.t("cw_sb", [9, 1024], BF16)

    wslot = [0]

    def wpiece(e_i, which, pc):
        assert e_i == 0
        s = wslot[0] % 2
        wslot[0] += 1
        src = ws1_bf if which == 1 else ws2_bf
        rds = tuple(f"wsbf.{which - 1}.{pc}.{qr}" for qr in range(4))
        res = tuple(f"wpc{s}.{q4}" for q4 in range(4))
        for q4 in range(4):
            p.dma(("gpsimd", "sync")[q4 % 2],
                  _ap(wpc[s], q4 * 2048, [[8192, 128], [1, 2048]]),
                  _ap(src, pc * 8192 + q4 * 2048, [[32768, 128], [1, 2048]]),
                  reads=rds, writes=(res[q4],), tag=f"w{which}ld")
        return wpc[s], res

    lgP = [lg, rb.t("lgB", [128, 64], F32)]
    rsmP = [r_sm, rb.t("r_smB", [128, 64], F32)]
    sm8P = [sm8, rb.t("sm8B", [128, 64], F32)]
    combP = [comb, rb.t("combB", [128, 64], F32)]
    comb9P = [comb9, rb.t("comb9B", [128, 72], BF16)]
    cwP = [cw_sb, rb.t("cw_sbB", [9, 1024], BF16)]
    msk0F = rb.t("msk0F", [128, 64], F32)
    msk1F = rb.t("msk1F", [128, 64], F32)
    mselF = rb.t("mselF", [128, 64], F32)
    msel_bf = rb.t("msel_bf", [128, 64], BF16)
    preF = rb.t("preF", [128, 64], F32)
    offBF = rb.t("offBF", [128, 64], F32)
    slotw = rb.t("slotw", [128, 64], F32)
    tmp64 = rb.t("tmp64", [128, 64], F32)
    trow_bf = rb.t("trow_bf", [1, 64], BF16)
    ttT_bf = rb.t("ttT_bf", [64, 1], BF16)
    cumoff = rb.t("cumoff", [64, 4], F32)
    orow = rb.t("orow", [1, 64], F32)
    sw8 = rb.t("sw8", [128, 16], F32)
    sc8 = rb.t("sc8", [128, 16], F32)
    m0asm = rb.t("m0asm", [128, 160], F32)

    def b_prologue(tt):
        for b in range(BPT):
            pg = b % 2
            blk = tt * BPT + b
            p.dma("sync", x1b[pg][:, :], _ap(x1_s, blk * C * D, [[D, 128], [1, D]]),
                  reads=("x1_hbm",), writes=(f"x1b{pg}",), tag="x1ld")
            emit_ln(x1b[pg][:, :], f"x1b{pg}", x1n_tm[:, :], "x1n_tm",
                    lnsq[:, :], "lnsq")
            p.dma("sync", _ap(x1n_s, blk * C * D, [[D, 128], [1, D]]), x1n_tm[:, :],
                  reads=("x1n_tm",), writes=("x1n_hbm",), tag="x1nst")
            pt_, ptres = PTS[b % 2]
            def fnT(t, src=x1n_tm, pt2=pt_):
                inst = None
                for cc in range(8):
                    inst = t.transpose(pt2[:, cc * 128:(cc + 1) * 128],
                                       src[:, cc * 128:(cc + 1) * 128], ident[:, :])
                return inst
            p.op("tensor", fnT, reads=("x1n_tm", "ident"), writes=(ptres,), tag="xnT")
            def fnTe(e, b_=b, pt2=pt_):
                inst = None
                for cc in range(8):
                    inst = e.tensor_copy(
                        x1n_fm[:, cc * TT + b_ * 128: cc * TT + (b_ + 1) * 128],
                        pt2[:, cc * 128:(cc + 1) * 128])
                return inst
            p.op("vector", fnTe, reads=(ptres,), writes=(f"xnfm.{b}",), tag="xnTev")
            pairs = [(x1n_fm[:, kc * TT + b * 128: kc * TT + (b + 1) * 128],
                      rwbf[:, kc * 8:(kc + 1) * 8]) for kc in range(8)]
            p.mm_group(P0[:, b * 8:(b + 1) * 8], pairs,
                       reads=(f"xnfm.{b}", "rwbf"), writes=("P0b",), tag="routmm")

    def b_softmax(tt):
        par = tt % 2
        lg_, r_sm_, sm8_, comb_ = lgP[par], rsmP[par], sm8P[par], combP[par]
        comb9_, cw_ = comb9P[par], cwP[par]
        R = lambda n: f"{n}.{par}"
        p.op("vector", lambda e: e.tensor_add(lg_[:, :], P0[:, 0:64], rb_bc[:, :]),
             reads=("P0b", "rbbc"), writes=(R("lg"),), tag="lgevac")
        lg3 = _ap(lg_, 0, [[64, 128], [8, 8], [1, 8]])
        p.op("vector", lambda e: e.tensor_reduce(
            _ap(sm8_, 0, [[64, 128], [1, 8], [1, 1]]), lg3, AX.X, ALU.max, negate=True),
            reads=(R("lg"),), writes=(R("negmax"),), tag="rmax")
        nmb = _ap(sm8_, 0, [[64, 128], [1, 8], [0, 8]])
        r3 = _ap(r_sm_, 0, [[64, 128], [8, 8], [1, 8]])
        p.op("vector", lambda e: e.tensor_tensor(r3, lg3, nmb, ALU.add),
             reads=(R("lg"), R("negmax")), writes=(R("r_sm"),), tag="subm")
        p.op("scalar", lambda e: e.activation(r_sm_[:, :], r_sm_[:, :], AF.Exp),
             reads=(R("r_sm"),), writes=(R("r_sm"),), tag="exp")
        p.op("vector", lambda e: e.tensor_reduce(
            _ap(sm8_, 8, [[64, 128], [1, 8], [1, 1]]), r3, AX.X, ALU.add),
            reads=(R("r_sm"),), writes=(R("ssum"),), tag="rsum")
        p.op("vector", lambda e: e.reciprocal(sm8_[:, 16:24], sm8_[:, 8:16]),
             reads=(R("ssum"),), writes=(R("srec"),))
        srb = _ap(sm8_, 16, [[64, 128], [1, 8], [0, 8]])
        p.op("vector", lambda e: e.tensor_tensor(r3, r3, srb, ALU.mult),
             reads=(R("r_sm"), R("srec")), writes=(R("r_sm"),), tag="rnorm")
        p.op("vector", lambda e: e.tensor_reduce(
            _ap(sm8_, 24, [[64, 128], [1, 8], [1, 1]]), r3, AX.X, ALU.max),
            reads=(R("r_sm"),), writes=(R("m1"),), tag="m1")
        m1b = _ap(sm8_, 24, [[64, 128], [1, 8], [0, 8]])
        c3 = _ap(comb_, 0, [[64, 128], [8, 8], [1, 8]])
        p.op("vector", lambda e: e.tensor_tensor(c3, r3, m1b, ALU.is_ge),
             reads=(R("r_sm"), R("m1")), writes=(R("comb"),), tag="eq1")
        p.op("scalar", lambda e: e.activation(msk0F[:, :], comb_[:, :], AF.Identity),
             reads=(R("comb"),), writes=("msk0F",), tag="msk0cp")
        p.op("vector", lambda e: e.scalar_tensor_tensor(c3, c3, -1e9, r3,
                                                        ALU.mult, ALU.add),
             reads=(R("comb"), R("r_sm")), writes=(R("comb"),), tag="r2")
        p.op("vector", lambda e: e.tensor_reduce(
            _ap(sm8_, 32, [[64, 128], [1, 8], [1, 1]]), c3, AX.X, ALU.max),
            reads=(R("comb"),), writes=(R("m2"),), tag="m2")
        m2b = _ap(sm8_, 32, [[64, 128], [1, 8], [0, 8]])
        p.op("vector", lambda e: e.tensor_tensor(c3, r3, m2b, ALU.is_ge),
             reads=(R("r_sm"), R("m2")), writes=(R("comb"),), tag="mask2")
        p.op("scalar", lambda e: e.activation(mselF[:, :], comb_[:, :], AF.Identity),
             reads=(R("comb"),), writes=("mselF",), tag="mselcp")
        p.op("scalar", lambda e: e.activation(msel_bf[:, :], comb_[:, :], AF.Identity),
             reads=(R("comb"),), writes=("msel_bf",), tag="mselbf")
        p.op("vector", lambda e: e.tensor_sub(msk1F[:, :], mselF[:, :], msk0F[:, :]),
             reads=("mselF", "msk0F"), writes=("msk1F",), tag="msk1")
        p.op("vector", lambda e: e.tensor_tensor(c3, c3, r3, ALU.mult),
             reads=(R("comb"), R("r_sm")), writes=(R("comb"),), tag="combu")
        p.op("vector", lambda e: e.tensor_add(sm8_[:, 40:48], sm8_[:, 24:32],
                                              sm8_[:, 32:40]),
             reads=(R("m1"), R("m2")), writes=(R("denom"),))
        p.op("vector", lambda e: e.reciprocal(sm8_[:, 48:56], sm8_[:, 40:48]),
             reads=(R("denom"),), writes=(R("drec"),))
        drb = _ap(sm8_, 48, [[64, 128], [1, 8], [0, 8]])
        p.op("vector", lambda e: e.tensor_tensor(c3, c3, drb, ALU.mult),
             reads=(R("comb"), R("drec")), writes=(R("comb"),), tag="combn")
        p.op("vector", lambda e: e.memset(
            _ap(comb9_, 0, [[72, 128], [9, 8], [1, 1]]), 1.0),
            writes=(R("comb9"),), tag="c9one")
        p.op("vector", lambda e: e.tensor_copy(
            _ap(comb9_, 1, [[72, 128], [9, 8], [1, 8]]), c3),
            reads=(R("comb"),), writes=(R("comb9"),), tag="c9cp")
        def fnc9T(t):
            inst = None
            for b_ in range(8):
                inst = t.transpose(_ap(PT, b_ * 128, [[1024, 9], [1, 128]]),
                                   comb9_[:, b_ * 9:(b_ + 1) * 9], ident[:, :])
            return inst
        p.op("tensor", fnc9T, reads=(R("comb9"), "ident"), writes=("PT",), tag="c9T")
        p.op("vector", lambda e: e.tensor_copy(cw_[0:9, :],
                                               _ap(PT, 0, [[1024, 9], [1, 1024]])),
             reads=("PT",), writes=(R("cw"),), tag="cwev")
        p.mm_multi([(P1[:, 0:64], [(lstr[:, :], msel_bf[:, :])]),
                    (P1[0:1, 64:128], [(o128c[:, :], msel_bf[:, :])])],
                   reads=("msel_bf", "lstr", "o128c"),
                   writes=("P1pre", "P1tot"), tag="pfxmm")
        p.op("scalar", lambda e: e.activation(preF[:, :], P1[:, 0:64], AF.Identity),
             reads=("P1pre",), writes=("preF",), tag="preev")
        p.op("vector", lambda e: e.tensor_copy(trow_bf[0:1, :], P1[0:1, 64:128]),
             reads=("P1tot",), writes=("trow_bf",), tag="totev")
        p.mm_group(P1[0:64, 128:129], [(trow_bf[0:1, :], one1[0:1, :])],
                   reads=("trow_bf", "one1"), writes=("P1tt",), tag="totTmm")
        p.op("vector", lambda e: e.tensor_copy(ttT_bf[0:64, :], P1[0:64, 128:129]),
             reads=("P1tt",), writes=("ttT_bf",), tag="ttTev")
        p.mm_multi([(P1[0:64, 192:193], [(bl64[0:64, :], ttT_bf[0:64, :])]),
                    (P1[0:64, 200:201], [(bf64[0:64, :], ttT_bf[0:64, :])])],
                   reads=("ttT_bf", "bl64", "bf64"),
                   writes=("P1cum", "P1grp"), tag="cummm")
        p.op("vector", lambda e: e.tensor_add(cumoff[0:64, 2:3], P1[0:64, 192:193],
                                              running64[0:64, :]),
             reads=("P1cum", "running64"), writes=("off64",), tag="off64")
        p.op("vector", lambda e: e.tensor_add(running64[0:64, :], running64[0:64, :],
                                              P1[0:64, 200:201]),
             reads=("P1grp", "running64", "off64"), writes=("running64",), tag="runupd")
        p.mm_group(P1[0:1, 256:320], [(cumoff[0:64, 2:3], id64[0:64, :])],
                   reads=("off64", "id64"), writes=("P1or",), tag="orowmm")
        p.op("vector", lambda e: e.tensor_copy(orow[0:1, :], P1[0:1, 256:320]),
             reads=("P1or",), writes=("orow",), tag="orowev")
        p.mm_group(P1[:, 320:384], [(o1x128f[0:1, :], orow[0:1, :])],
                   reads=("orow", "o1x128f"), writes=("P1ob",), tag="offBmm")
        p.op("scalar", lambda e: e.activation(offBF[:, :], P1[:, 320:384], AF.Identity),
             reads=("P1ob",), writes=("offBF",), tag="offBev")
        p.op("vector", lambda e: e.tensor_add(slotw[:, :], preF[:, :], offBF[:, :]),
             reads=("preF", "offBF"), writes=("slotw",), tag="slotw")
        sw3 = lambda off: _ap(sw8, off, [[16, 128], [1, 8], [1, 1]])
        t3 = _ap(tmp64, 0, [[64, 128], [8, 8], [1, 8]])
        for r_, mk in ((0, "msk0F"), (1, "msk1F")):
            p.op("vector", lambda e, mk_=mk: e.tensor_mul(
                tmp64[:, :], (msk0F if mk_ == "msk0F" else msk1F)[:, :], slotw[:, :]),
                reads=(mk, "slotw"), writes=("tmp64",), tag="swmul")
            p.op("vector", lambda e, r2_=r_: e.tensor_reduce(
                sw3(r2_ * 8), t3, AX.X, ALU.add),
                reads=("tmp64",), writes=(f"sw{r_}",), tag="swred")
        p.op("vector", lambda e: e.tensor_mul(sc8[:, 0:8], sm8_[:, 24:32],
                                              sm8_[:, 48:56]),
             reads=(R("m1"), R("drec")), writes=("sc0",), tag="sc0")
        p.op("vector", lambda e: e.tensor_mul(sc8[:, 8:16], sm8_[:, 32:40],
                                              sm8_[:, 48:56]),
             reads=(R("m2"), R("drec")), writes=("sc1",), tag="sc1")
        m3 = lambda off, n: _ap(m0asm, off, [[160, 128], [20, 8], [1, n]])
        p.op("scalar", lambda e: e.activation(
            m3(0, 8), _ap(msk0F, 0, [[64, 128], [8, 8], [1, 8]]), AF.Identity),
            reads=("msk0F",), writes=("m0asm",), tag="asm0")
        p.op("scalar", lambda e: e.activation(
            m3(8, 8), _ap(msk1F, 0, [[64, 128], [8, 8], [1, 8]]), AF.Identity),
            reads=("msk1F", "m0asm"), writes=("m0asm",), tag="asm1")
        p.op("scalar", lambda e: e.activation(
            m3(16, 1), _ap(sw8, 0, [[16, 128], [1, 8], [1, 1]]), AF.Identity),
            reads=("sw0", "m0asm"), writes=("m0asm",), tag="asm2")
        p.op("scalar", lambda e: e.activation(
            m3(17, 1), _ap(sw8, 8, [[16, 128], [1, 8], [1, 1]]), AF.Identity),
            reads=("sw1", "m0asm"), writes=("m0asm",), tag="asm3")
        p.op("scalar", lambda e: e.activation(
            m3(18, 1), _ap(sc8, 0, [[16, 128], [1, 8], [1, 1]]), AF.Identity),
            reads=("sc0", "m0asm"), writes=("m0asm",), tag="asm4")
        p.op("scalar", lambda e: e.activation(
            m3(19, 1), _ap(sc8, 8, [[16, 128], [1, 8], [1, 1]]), AF.Identity),
            reads=("sc1", "m0asm"), writes=("m0asm",), tag="asm5")
        p.dma("sync", _ap(meta0_s, tt * 1024 * 20, [[20, 128], [2560, 8], [1, 20]]),
              _ap(m0asm, 0, [[160, 128], [20, 8], [1, 20]]),
              reads=("m0asm",), writes=("meta0_hbm",), tag="m0st")

    def b_experts(tt):
        par = tt % 2
        comb_ = combP[par]
        n_e = _LIMIT["ne"]
        for e_i in range(n_e):
            for pc in range(4):
                wsb, wres = wpiece(e_i, 1, pc)
                for mb in range(8):
                    mglob = pc * 8 + mb
                    pp = (PA, PB)[mglob % 2]
                    ppres = (("PA0", "PA1"), ("PB0", "PB1"))[mglob % 2]
                    groups = [(pp[:, n * 512:(n + 1) * 512],
                               [(wsb[:, kc * 1024 + mb * 128: kc * 1024 + (mb + 1) * 128],
                                 x1n_fm[:, kc * TT + n * 512: kc * TT + (n + 1) * 512])
                                for kc in range(8)]) for n in range(2)]
                    p.mm_multi(groups,
                               reads=(wres,) + tuple(f"xnfm.{b}" for b in range(8)),
                               writes=ppres, tag="h_mm")
                    p.op("scalar", lambda e, pp_=pp, m=mglob, ei=e_i: e.activation(
                        h_bf[:, m * TT:(m + 1) * TT], pp_[:, :], AF.Gelu_apprx_tanh,
                        bias=b1r[:, ei * 32 + m: ei * 32 + m + 1]),
                        reads=ppres + ("b1r",), writes=(f"h.{mglob}",), tag="gelu")
            if e_i == n_e - 1 and tt + 1 < _LIMIT["bt"]:
                b_prologue(tt + 1)
                b_softmax(tt + 1)
            cw_ = cwP[par]
            for pc in range(4):
                wsb, wres = wpiece(e_i, 2, pc)
                for b in range(BPT):
                    g = b + pc * BPT
                    pp = (PA, PB)[g % 2]
                    ppres = (("PA0", "PA1"), ("PB0", "PB1"))[g % 2]
                    hres = tuple(f"h.{pc * 8 + k}" for k in range(8))
                    rds = (wres,) + hres
                    groups = []
                    for n in range(2):
                        pairs = []
                        if e_i == 0 and pc == 0:
                            pairs.append((cw_[0:9, b * 128:(b + 1) * 128],
                                          b2bf[0:9, n * 512:(n + 1) * 512]))
                        pairs += [(h_bf[:, (pc * 8 + k) * TT + b * 128:
                                        (pc * 8 + k) * TT + (b + 1) * 128],
                                   wsb[:, k * 1024 + n * 512: k * 1024 + (n + 1) * 512])
                                  for k in range(8)]
                        groups.append((pp[:, n * 512:(n + 1) * 512], pairs))
                    if e_i == 0 and pc == 0:
                        rds = rds + (f"cw.{par}", "b2bf")
                    p.mm_multi(groups, reads=rds, writes=ppres, tag="y_mm")
                    if e_i == 0 and pc == 0:
                        p.op("vector", lambda e, pp_=pp, b_=b: e.tensor_copy(
                            acc[:, b_ * D:(b_ + 1) * D], pp_[:, :]),
                            reads=ppres, writes=(f"acc.{b}",), tag="acc0")
                    elif e_i == 0:
                        p.op("vector", lambda e, pp_=pp, b_=b: e.tensor_add(
                            acc[:, b_ * D:(b_ + 1) * D], acc[:, b_ * D:(b_ + 1) * D],
                            pp_[:, :]),
                            reads=ppres + (f"acc.{b}",), writes=(f"acc.{b}",), tag="accs")
                    else:
                        p.op("vector", lambda e, pp_=pp, b_=b, ei=e_i:
                             e.scalar_tensor_tensor(
                                 acc[:, b_ * D:(b_ + 1) * D], pp_[:, :],
                                 comb_[:, b_ * 8 + ei - 1: b_ * 8 + ei],
                                 acc[:, b_ * D:(b_ + 1) * D], ALU.mult, ALU.add),
                             reads=ppres + (f"acc.{b}", f"comb.{par}"),
                             writes=(f"acc.{b}",), tag="acce")

    def b_final(tt):
        for b in range(BPT):
            pg = b % 2
            blk = tt * BPT + b
            p.dma("sync", x1b[pg][:, :], _ap(x1_s, blk * C * D, [[D, 128], [1, D]]),
                  reads=("x1_hbm",), writes=(f"x1b{pg}",), tag="x1ld2")
            p.op("vector", lambda e, b_=b, pg_=pg: e.tensor_add(
                outb[pg_][:, :], acc[:, b_ * D:(b_ + 1) * D], x1b[pg_][:, :]),
                reads=(f"acc.{b}", f"x1b{pg}"), writes=(f"outb{pg}",), tag="out")
            p.dma("sync", _ap(out_h, blk * C * D, [[D, 128], [1, D]]), outb[pg][:, :],
                  reads=(f"outb{pg}",), writes=("out_hbm",), tag="outst")

    if _LIMIT["bt"] > 0:
        b_prologue(0)
        b_softmax(0)
    for tt in range(_LIMIT["bt"]):
        for i in range(20):
            emit_wcast(128 + tt * 20 + i)
        b_experts(tt)
        b_final(tt)

    def b2_pages():
        p.dma("sync", _ap(cnt_s, 0, [[1, 64], [1, 1]]), running64[0:64, 0:1],
              reads=("running64",), writes=("cnt_hbm",), tag="cntst")
        cr = rowscr
        p.dma("sync", cr[0:1, 0:8], _ap(cnt_s, 0, [[8, 1], [1, 8]]),
              reads=("cnt_hbm",), writes=("cntrow",), tag="cntld")
        A, B_, Cb = (16, 40, 64)
        p.op("vector", lambda e: e.memset(cr[0:1, A:A + 24], 0.0), writes=("bufA",))
        for j in range(16):
            p.op("vector", lambda e, j_=j: e.tensor_scalar(
                cr[0:1, 8:16], cr[0:1, 0:8], float(j_ * 512 + 1), None,
                ALU.is_ge),
                reads=("cntrow",), writes=("pgm",), tag="pgge")
            p.op("vector", lambda e: e.tensor_add(
                cr[0:1, A + 8:A + 16], cr[0:1, A + 8:A + 16], cr[0:1, 8:16]),
                reads=("bufA", "pgm"), writes=("bufA",), tag="pgacc")
        p.op("vector", lambda e: e.memset(cr[0:1, B_:B_ + 24], 0.0), writes=("bufB",))
        p.op("vector", lambda e: e.tensor_add(
            cr[0:1, B_ + 8:B_ + 16], cr[0:1, A + 8:A + 16], cr[0:1, A + 7:A + 15]),
            reads=("bufA", "bufB"), writes=("bufB",), tag="pfx1")
        p.op("vector", lambda e: e.memset(cr[0:1, Cb:Cb + 24], 0.0), writes=("bufC",))
        p.op("vector", lambda e: e.tensor_add(
            cr[0:1, Cb + 8:Cb + 16], cr[0:1, B_ + 8:B_ + 16], cr[0:1, B_ + 6:B_ + 14]),
            reads=("bufB", "bufC"), writes=("bufC",), tag="pfx2")
        p.op("vector", lambda e: e.memset(cr[0:1, A:A + 8], 0.0),
             reads=("bufA",), writes=("bufA",))
        p.op("vector", lambda e: e.tensor_add(
            cr[0:1, A + 8:A + 16], cr[0:1, Cb + 8:Cb + 16], cr[0:1, Cb + 4:Cb + 12]),
            reads=("bufC", "bufA"), writes=("incl",), tag="pfx3")
        p.op("vector", lambda e: e.memset(cr[0:1, 88:136], 0.0), writes=("eop",))
        for e_ in range(8):
            p.op("vector", lambda ee, e2=e_: ee.tensor_scalar(
                cr[0:1, 136:184], pio48[0:1, :], cr[0:1, A + 8 + e2:A + 9 + e2],
                None, ALU.is_ge),
                reads=("incl", "pio48"), writes=("tmp48",), tag="eopge")
            p.op("vector", lambda ee: ee.tensor_add(
                cr[0:1, 88:136], cr[0:1, 88:136], cr[0:1, 136:184]),
                reads=("tmp48", "eop"), writes=("eop",), tag="eopacc")
        p.op("vector", lambda e: e.tensor_copy(rowbf[0:1, 0:48], cr[0:1, 88:136]),
             reads=("eop",), writes=("eopbf",), tag="eopbf")
        p.op("vector", lambda e: e.tensor_copy(rowbf[0:1, 48:56],
                                               cr[0:1, A + 7:A + 15]),
             reads=("incl",), writes=("pbbf",), tag="pbbf")
        p.mm_multi([(P0[:, 64:112], [(o1x128b[0:1, :], rowbf[0:1, 0:48])]),
                    (P0[:, 112:120], [(o1x128b[0:1, :], rowbf[0:1, 48:56])])],
                   reads=("eopbf", "pbbf", "o1x128b"),
                   writes=("P0e", "P0p"), tag="bcmm")
        p.op("scalar", lambda e: e.activation(pbB[:, :], P0[:, 112:120], AF.Identity),
             reads=("P0p",), writes=("pbB",), tag="pbBev")
        p.op("scalar", lambda e: e.activation(tmp64[:, 0:48], P0[:, 64:112],
                                              AF.Identity),
             reads=("P0e",), writes=("tmp64",), tag="eAllev")
        iotaB = _ap(iota128, 0, [[1, 128], [0, 48]])
        p.op("vector", lambda e: e.scalar_tensor_tensor(
            tmp64[:, 0:48], tmp64[:, 0:48], 128.0, iotaB, ALU.mult, ALU.add),
            reads=("tmp64", "iota128"), writes=("tmp64",), tag="widxf")
        p.op("vector", lambda e: e.tensor_copy(wIdxu[:, :], tmp64[:, 0:48]),
             reads=("tmp64",), writes=("wIdxu",), tag="widxu")

    m0b = [rb.t(f"m0b{i}", [128, 20], F32) for i in range(2)]
    b3t = rb.t("b3t", [128, 8], F32)
    b3g = rb.t("b3g", [128, 2], F32)
    b3i = rb.t("b3i", [128, 2], U32)
    b3v = [rb.t(f"b3v{i}", [128, 4], F32) for i in range(2)]
    meta_res = []

    def b3_block(blk):
        pg = blk % 2
        p.dma("sync", m0b[pg][:, :],
              _ap(meta0_s, blk * 128 * 20, [[20, 128], [1, 20]]),
              reads=("meta0_hbm",), writes=(f"m0b{pg}",), tag="m0ld")
        for r_ in range(2):
            p.op("vector", lambda e, r2=r_, pg_=pg: e.tensor_mul(
                b3t[:, :], m0b[pg_][:, r2 * 8:(r2 + 1) * 8], pbB[:, :]),
                reads=(f"m0b{pg}", "pbB"), writes=("b3t",), tag="b3mul")
            p.op("vector", lambda e, r2=r_: e.tensor_reduce(
                b3g[:, r2:r2 + 1], b3t[:, :], AX.X, ALU.add),
                reads=("b3t",), writes=("b3g",), tag="b3red")
            p.op("vector", lambda e, r2=r_, pg_=pg: e.scalar_tensor_tensor(
                b3g[:, r2:r2 + 1], b3g[:, r2:r2 + 1], 512.0,
                m0b[pg_][:, 16 + r2:17 + r2], ALU.mult, ALU.add),
                reads=(f"m0b{pg}", "b3g"), writes=("b3g",), tag="b3glob")
            p.op("vector", lambda e, r2=r_, pg_=pg, bb=blk: e.tensor_scalar(
                b3v[pg_][:, r2 * 2:r2 * 2 + 1], iota128[:, :], float(bb * 128),
                None, ALU.add),
                reads=("iota128",), writes=(f"b3v{pg}",), tag="b3tok")
            p.op("vector", lambda e, r2=r_, pg_=pg: e.tensor_copy(
                b3v[pg_][:, r2 * 2 + 1:r2 * 2 + 2], m0b[pg_][:, 18 + r2:19 + r2]),
                reads=(f"m0b{pg}", f"b3v{pg}"), writes=(f"b3v{pg}",), tag="b3sc")
            p.op("vector", lambda e, r2=r_: e.tensor_copy(
                b3i[:, r2:r2 + 1], b3g[:, r2:r2 + 1]),
                reads=("b3g", "b3i"), writes=("b3i",), tag="b3idx")
            res = f"meta_w.{blk}.{r_}"
            meta_res.append(res)
            p.idma(meta_s[:, :], (b3i[:, r_:r_ + 1], 0),
                   b3v[pg][:, r_ * 2:r_ * 2 + 2], None,
                   reads=(f"b3v{pg}", "b3i", "meta_hbm"), writes=(res,),
                   tag="b3scat")

    if _LIMIT["bt"] >= 8:
        b2_pages()
        for blk in range(_LIMIT["b3"]):
            b3_block(blk)

    p.barrier()
    rc = Region()
    metapg = [rc.t(f"metapg{i}", [128, 8], F32) for i in range(2)]
    toku = [rc.t(f"toku{i}", [128, 4], U32) for i in range(2)]
    xgt = [rc.t(f"xgt{i}", [128, 4096], BF16) for i in range(2)]
    xgfm = [rc.t(f"xgfm{i}", [128, 4096], BF16) for i in range(2)]
    hC = rc.t("hC", [128, 32 * 512], BF16)
    yacc = rc.t("yacc", [128, 4 * 1024], F32)
    ysc = [rc.t(f"ysc{i}", [128, 1024], F32) for i in range(2)]
    wpcC = [rc.t(f"wpcC{i}", [128, 8192], BF16) for i in range(3)]
    b1pg = [rc.t(f"b1pg{i}", [128, 32], F32) for i in range(2)]
    wslotC = [0]

    def c_wpiece(pgi, which, pc):
        s = wslotC[0] % 3
        wslotC[0] += 1
        src = w1s_bf if which == 1 else w2s_bf
        rds = tuple(f"wspec.{which - 1}.{pc}.{ei}.{qr}"
                    for ei in range(1, 9) for qr in range(4))
        res = tuple(f"wpcC{s}.{q4}" for q4 in range(4))
        for q4 in range(4):
            p.idma(_ap(wpcC[s], q4 * 2048, [[8192, 128], [1, 2048]]), None,
                   src[:, :], (wIdxu[:, pgi:pgi + 1], 0),
                   elem_off=pc * 8192 + q4 * 2048, bc=rBCW,
                   reads=rds + ("wIdxu", "bcreg2"), writes=(res[q4],),
                   tag=f"cw{which}g")
        return wpcC[s], res

    if _LIMIT["bt"] >= 8:
        p.op("gpsimd", lambda e: e.memset(xgt[0][:, :], 0.0), writes=("xgt0",))
        p.op("gpsimd", lambda e: e.memset(xgt[1][:, :], 0.0), writes=("xgt1",))
        for pgi in range(_LIMIT["pg"]):
            par = pgi % 2
            p.dma("sync", _ap(metapg[par], 0, [[8, 128], [2, 4], [1, 2]]),
                  _ap(meta_s, pgi * 1024, [[2, 128], [256, 4], [1, 2]]),
                  reads=tuple(meta_res), writes=(f"metapg{par}",), tag="cmld")
            p.op("vector", lambda e, par_=par: e.tensor_copy(
                toku[par_][:, :], _ap(metapg[par_], 0, [[8, 128], [2, 4], [1, 1]])),
                reads=(f"metapg{par}", f"toku{par}"), writes=(f"toku{par}",),
                tag="ctok")
            for sb in range(4):
                p.idma(xgt[par][:, sb * 1024:(sb + 1) * 1024], None,
                       x1n_s[:, :], (toku[par][:, sb:sb + 1], 0), bc=rBC,
                       reads=("x1n_hbm", f"toku{par}", "bcreg", f"xgt{par}"),
                       writes=(f"xgt{par}",), tag="cxg")
            for sb in range(4):
                pt_, ptres = PTS[sb % 2]
                def fnT(t, src=xgt[par], sb_=sb, pt2=pt_):
                    inst = None
                    for cc in range(8):
                        inst = t.transpose(
                            pt2[:, cc * 128:(cc + 1) * 128],
                            src[:, sb_ * 1024 + cc * 128:sb_ * 1024 + (cc + 1) * 128],
                            ident[:, :])
                    return inst
                p.op("tensor", fnT, reads=(f"xgt{par}", "ident"), writes=(ptres,),
                     tag="cxT")
                p.op("vector", lambda e, sb_=sb, pt2=pt_, par_=par: e.tensor_copy(
                    _ap(xgfm[par_], sb_ * 128, [[4096, 128], [512, 8], [1, 128]]),
                    _ap(pt2, 0, [[1024, 128], [128, 8], [1, 128]])),
                    reads=(ptres, f"xgfm{par}"), writes=(f"xgfm{par}",), tag="cxTev")
            p.idma(b1pg[par][:, :], None, b1s_h[:, :], (wIdxu[:, pgi:pgi + 1], 0),
                   bc=rBCW, reads=("wIdxu", "bcreg2", f"b1pg{par}"),
                   writes=(f"b1pg{par}",), tag="cb1g")
            for pc in range(4):
                wsb, wres = c_wpiece(pgi, 1, pc)
                for mb in range(8):
                    mglob = pc * 8 + mb
                    pp = (P0, P1)[mglob % 2]
                    ppres = ("P0b", "P1b")[mglob % 2]
                    pairs = [(wsb[:, kc * 1024 + mb * 128: kc * 1024 + (mb + 1) * 128],
                              xgfm[par][:, kc * 512:(kc + 1) * 512])
                             for kc in range(8)]
                    p.mm_group(pp[:, :], pairs,
                               reads=(wres, f"xgfm{par}"), writes=(ppres,),
                               tag="ch_mm")
                    p.op("scalar", lambda e, pp_=pp, m=mglob, par_=par: e.activation(
                        hC[:, m * 512:(m + 1) * 512], pp_[:, :], AF.Gelu_apprx_tanh,
                        bias=b1pg[par_][:, m:m + 1]),
                        reads=(ppres, f"b1pg{par}"), writes=(f"hC.{mglob}",),
                        tag="cgelu")
            for pc2 in range(4):
                wsb2, wres2 = c_wpiece(pgi, 2, pc2)
                for b in range(4):
                    pp = (PA, PB)[b % 2]
                    ppres = (("PA0", "PA1"), ("PB0", "PB1"))[b % 2]
                    hres = tuple(f"hC.{pc2 * 8 + k}" for k in range(8))
                    groups = [(pp[:, n * 512:(n + 1) * 512],
                               [(hC[:, (pc2 * 8 + k) * 512 + b * 128:
                                     (pc2 * 8 + k) * 512 + (b + 1) * 128],
                                 wsb2[:, k * 1024 + n * 512: k * 1024 + (n + 1) * 512])
                                for k in range(8)]) for n in range(2)]
                    p.mm_multi(groups, reads=(wres2,) + hres, writes=ppres,
                               tag="cy_mm")
                    if pc2 == 0:
                        p.op("vector", lambda e, pp_=pp, b_=b: e.tensor_copy(
                            yacc[:, b_ * 1024:(b_ + 1) * 1024], pp_[:, :]),
                            reads=ppres, writes=(f"yacc.{b}",), tag="cy0")
                    else:
                        p.op("vector", lambda e, pp_=pp, b_=b: e.tensor_add(
                            yacc[:, b_ * 1024:(b_ + 1) * 1024],
                            yacc[:, b_ * 1024:(b_ + 1) * 1024], pp_[:, :]),
                            reads=ppres + (f"yacc.{b}",), writes=(f"yacc.{b}",),
                            tag="cyacc")
            for b in range(4):
                yp = b % 2
                p.op("vector", lambda e, b_=b, yp_=yp, par_=par: e.tensor_scalar(
                    ysc[yp_][:, :], yacc[:, b_ * 1024:(b_ + 1) * 1024],
                    metapg[par_][:, 2 * b_ + 1:2 * b_ + 2], None, ALU.mult),
                    reads=(f"yacc.{b}", f"metapg{par}", f"ysc{yp}"),
                    writes=(f"ysc{yp}",), tag="cyscale")
                p.idma(out_h[:, :], (toku[par][:, b:b + 1], 0),
                       ysc[yp][:, :], None, bc=rBC, cce=ALU.add,
                       reads=(f"ysc{yp}", f"toku{par}", "bcreg", "out_hbm"),
                       writes=("out_hbm",), tag="cyadd")

    p.barrier()
    n = p.emit()
    _cache[key] = nc
    _cache["n_ops"] = len(p.ops)
    return nc


def kernel(**inputs):
    cst = _consts()
    gn = np.asarray(inputs["L0_attn_g_norm_w"], np.float32)
    gn_bc = np.tile(gn, H)
    gnorm_r = np.ascontiguousarray(gn_bc.reshape(16, 128).T)
    rb = np.asarray(inputs["L0_ffn_router_b"], np.float32)
    rb_bc = np.ascontiguousarray(np.broadcast_to(np.tile(rb, 8)[None, :], (128, 64)))
    b1 = np.concatenate([np.asarray(inputs["L0_ffn_shared_b1"], np.float32)[None],
                         np.asarray(inputs["L0_ffn_spec_b1"], np.float32)], 0)
    b1_r = np.ascontiguousarray(
        b1.reshape(NEXP, 32, 128).transpose(2, 0, 1).reshape(128, NEXP * 32))
    b2 = np.concatenate([np.asarray(inputs["L0_ffn_shared_b2"], np.float32)[None],
                         np.asarray(inputs["L0_ffn_spec_b2"], np.float32)], 0)
    w1 = np.concatenate([np.asarray(inputs["L0_ffn_shared_w1"], np.float32)[None],
                         np.asarray(inputs["L0_ffn_spec_w1"], np.float32)], 0)
    w2 = np.concatenate([np.asarray(inputs["L0_ffn_shared_w2"], np.float32)[None],
                         np.asarray(inputs["L0_ffn_spec_w2"], np.float32)], 0)
    w1_all = np.stack([_wt_relayout(w1[i]) for i in range(NEXP)], 0)
    w2_all = np.stack([_wt_relayout(w2[i]) for i in range(NEXP)], 0)

    b1s = np.ascontiguousarray(
        b1[1:].reshape(NSPEC, 32, 128).transpose(0, 2, 1).reshape(NSPEC * 128, 32))
    shared = dict(
        wq_t=_wt_relayout(np.asarray(inputs["L0_attn_Wq"], np.float32)),
        wk_t=_wt_relayout(np.asarray(inputs["L0_attn_Wk"], np.float32)),
        wv_t=_wt_relayout(np.asarray(inputs["L0_attn_Wv"], np.float32)),
        wg_t=_wt_relayout(np.asarray(inputs["L0_attn_Wg"], np.float32)),
        wo_t=_wt_relayout(np.asarray(inputs["L0_attn_Wo"], np.float32)),
        w1_all=w1_all, w2_all=w2_all, b1_r=b1_r, b2_all=np.ascontiguousarray(b2),
        rw_t=_wt_relayout(np.asarray(inputs["L0_ffn_router_w"], np.float32)),
        rb_bc=rb_bc, gnorm_r=gnorm_r, ident=cst["ident"],
        cos_tm=cst["cos_tm"], sin_tm=cst["sin_tm"], dq_ds=cst["dq_ds"],
        maskT=cst["maskT"],
        b1s=b1s, lstrict=cst["lstrict"], ones128c=cst["ones128c"],
        one1=cst["one1"], blockL64=cst["blockL64"], blockF64=cst["blockF64"],
        ident64=cst["ident64"], ones1x128f=cst["ones1x128f"],
        ones1x128b=cst["ones1x128b"], iota128h=cst["iota128"],
        piota48h=cst["piota48"],
    )
    x = np.asarray(inputs["L0_attn_x"], np.float32)
    nc = build_program()
    in_maps = []
    for c in range(NCORES):
        m = dict(shared)
        m["x"] = np.ascontiguousarray(x[NSEQ * c:NSEQ * (c + 1)].reshape(TOK, D))
        in_maps.append(m)
    res = run_bass_kernel_spmd(nc, in_maps, core_ids=list(range(NCORES)))
    out = np.empty((16, T, D), np.float32)
    for c in range(NCORES):
        out[NSEQ * c:NSEQ * (c + 1)] = res.results[c]["out"].reshape(NSEQ, T, D)
    return out
